# version 20
# baseline (speedup 1.0000x reference)
import numpy as np
import ml_dtypes
import contextlib
import concourse.bass as bass
import concourse.mybir as mybir
from concourse.bass_utils import run_bass_kernel_spmd

F32 = mybir.dt.float32
BF16 = mybir.dt.bfloat16
I32 = mybir.dt.int32
AF = mybir.ActivationFunctionType
ALU = mybir.AluOpType
AX = mybir.AxisListType.X
bfdt = ml_dtypes.bfloat16

D = 1024
SEQ = 4096
BS = 2
NCORE = 8
HID = 4096
F = 512
NCH = SEQ // F
CT = 8
DT = 8
JT = HID // 128
EPS = 1e-8
LN_EPS = 1e-5
EPS_NORM = 1e-8
EPS_Q = 1e-5
GN_N = float(SEQ * 64)


class Prog:
    ENGS = ("vector", "scalar", "gpsimd", "tensor", "sync")

    def __init__(self, nc, dma_pool=8):
        self.nc = nc
        self.scope = "prep"
        self.q = {e: [] for e in self.ENGS}
        self.cnt = {e: 0 for e in self.ENGS}
        self.dma_pool = dma_pool
        self.dma_idx = {e: 0 for e in self.ENGS}
        self.dma_uses = {}
        self.buf = {}
        self.last_wait = {}
        self.sem_keys = set(self.ENGS)

    def _bufstate(self, h):
        return self.buf.setdefault(id(h), {"w": None, "r": []})

    def _collect(self, eng, reads, writes):
        waits = {}

        def need(dep):
            if dep is None:
                return
            key, val = dep
            if key == eng == "tensor":
                return
            if val <= self.last_wait.get((eng, key), 0):
                return
            if waits.get(key, 0) < val:
                waits[key] = val
        for h in reads:
            need(self._bufstate(h)["w"])
        for h in writes:
            st = self._bufstate(h)
            need(st["w"])
            for r in st["r"]:
                need(r)
        for key, val in waits.items():
            self.last_wait[(eng, key)] = val
        return waits

    def emit(self, eng, fn, reads=(), writes=()):
        waits = self._collect(eng, reads, writes)
        self.cnt[eng] += 1
        me = (eng, self.cnt[eng])
        for h in reads:
            self._bufstate(h)["r"].append(me)
        for h in writes:
            st = self._bufstate(h)
            st["w"] = me
            st["r"] = []
        self.q[eng].append((waits, fn, eng, 1, self.scope))
        return me

    def dma(self, eng, fn, reads=(), writes=()):
        waits = self._collect(eng, reads, writes)
        j = self.dma_idx[eng]
        self.dma_idx[eng] = (j + 1) % self.dma_pool
        key = (eng, "dma", j)
        self.sem_keys.add(key)
        uses = self.dma_uses.get(key, 0)
        if uses > 0:
            v = 16 * uses
            if v > self.last_wait.get((eng, key), 0):
                if waits.get(key, 0) < v:
                    waits[key] = v
                self.last_wait[(eng, key)] = v
        self.dma_uses[key] = uses + 1
        me = (key, 16 * (uses + 1))
        for h in reads:
            self._bufstate(h)["r"].append(me)
        for h in writes:
            st = self._bufstate(h)
            st["w"] = me
            st["r"] = []
        self.q[eng].append((waits, fn, key, 16, self.scope))
        return me

    def barrier_all(self):
        snap = [(e, self.cnt[e]) for e in self.ENGS if self.cnt[e] > 0]
        snap += [(k, 16 * u) for k, u in self.dma_uses.items()]
        for eng in self.ENGS:
            waits = {}
            for key, val in snap:
                if key == eng:
                    continue
                if val > self.last_wait.get((eng, key), 0):
                    waits[key] = val
                    self.last_wait[(eng, key)] = val
            if waits:
                self.q[eng].append((waits, None, None, 0, self.scope))

    def final_wait(self, eng):
        waits = {}
        for key, val in [(e, self.cnt[e]) for e in self.ENGS] + \
                        [(k, 16 * u) for k, u in self.dma_uses.items()]:
            if key == eng or val == 0:
                continue
            if val > self.last_wait.get((eng, key), 0):
                waits[key] = val
                self.last_wait[(eng, key)] = val
        self.q[eng].append((waits, None, None, 0, self.scope))

    def build(self, cx):
        sems = {}
        for key in sorted(self.sem_keys, key=str):
            name = key if isinstance(key, str) else f"{key[0]}_d{key[2]}"
            sems[key] = cx.enter_context(self.nc.semaphore(f"s_{name}"))

        def mk(eng):
            def body(e):
                cur = None
                import contextlib as _cl
                stack = _cl.ExitStack()
                for waits, fn, sem_key, inc, scope in self.q[eng]:
                    if scope != cur:
                        stack.close()
                        stack = _cl.ExitStack()
                        stack.enter_context(self.nc.named_scope(f"{scope}_{eng}"))
                        cur = scope
                    for key, val in sorted(waits.items(), key=str):
                        e.wait_ge(sems[key], val)
                    if fn is not None:
                        fn(e).then_inc(sems[sem_key], inc)
                stack.close()
            return body
        with self.nc.Block() as blk:
            for eng in self.ENGS:
                if self.q[eng]:
                    getattr(blk, eng)(mk(eng))


class Arena:
    TOP = 229344

    def __init__(self, nc, base=0):
        self.nc = nc
        self.off = base
        self.n = 0

    def alloc(self, name, shape, dtype):
        per_part = int(np.prod(shape[1:])) * mybir.dt.size(dtype)
        size32 = (per_part + 31) & ~31
        self.off = ((self.off + 31) & ~31) + size32
        self.n += 1
        assert self.off <= 212500, f"SBUF overflow at {name}: {self.off}"
        addr = self.TOP - self.off
        return self.nc.alloc_sbuf_tensor_at(f"{name}_{self.n}", shape, dtype, offset=addr)

    def mark(self):
        return self.off

    def reset(self, off):
        self.off = off


def build_nc():
    nc = bass.Bass("TRN2", target_bir_lowering=False)
    dx = nc.dram_tensor("x", [BS, SEQ, D], F32, kind="ExternalInput")
    dout = nc.dram_tensor("out", [BS, SEQ, D], F32, kind="ExternalOutput")
    dWr = nc.dram_tensor("W_r", [D, D], F32, kind="ExternalInput")
    dWk = nc.dram_tensor("W_k", [D, D], F32, kind="ExternalInput")
    dWv = nc.dram_tensor("W_v", [D, D], F32, kind="ExternalInput")
    dWo = nc.dram_tensor("W_o", [D, D], F32, kind="ExternalInput")
    dw1 = nc.dram_tensor("w1", [HID, D], F32, kind="ExternalInput")
    dw2 = nc.dram_tensor("w2", [D, HID], F32, kind="ExternalInput")
    dcols = nc.dram_tensor("cols", [128, 64], F32, kind="ExternalInput")
    dn1row = nc.dram_tensor("n1row", [1, D], F32, kind="ExternalInput")
    dn2row = nc.dram_tensor("n2row", [1, HID], F32, kind="ExternalInput")
    dasb = nc.dram_tensor("asb", [2, 1], F32, kind="ExternalInput")
    didf = nc.dram_tensor("idf", [128, 128], F32, kind="ExternalInput")
    didb = nc.dram_tensor("idb", [128, 128], BF16, kind="ExternalInput")
    dhmask = nc.dram_tensor("hmask", [128, 2], F32, kind="ExternalInput")
    dhexp = nc.dram_tensor("hexp", [2, 128], F32, kind="ExternalInput")
    dones = nc.dram_tensor("ones1", [1, 128], F32, kind="ExternalInput")
    donesc = nc.dram_tensor("onesc", [128, 1], F32, kind="ExternalInput")
    drwkv = nc.dram_tensor("rwkv_s", [BS, CT, 128, SEQ], BF16)
    dx1 = nc.dram_tensor("x1_s", [BS, SEQ, D], BF16)
    dhq = nc.dram_tensor("hq_s", [BS, SEQ // 128, 128, HID], BF16)
    dsc = nc.dram_tensor("scal_s", [1, 8], F32)

    p = Prog(nc)
    E, DMA = p.emit, p.dma
    cx = contextlib.ExitStack()
    ar = Arena(nc)

    cols = ar.alloc("cols", [128, 64], F32)
    ccon = ar.alloc("ccon", [128, 24], F32)
    gnsc = ar.alloc("gnsc", [128, 32], F32)
    gnsh = ar.alloc("gnsh", [128, 32], F32)
    n1rep = ar.alloc("n1rep", [128, D], BF16)
    asbc = ar.alloc("asbc", [128, 2], F32)
    idf = ar.alloc("idf", [128, 128], F32)
    idb = ar.alloc("idb", [128, 128], BF16)
    hmask = ar.alloc("hmask", [128, 2], F32)
    hexp = ar.alloc("hexp", [2, 128], F32)
    ones1 = ar.alloc("ones1", [1, 128], F32)
    onesc = ar.alloc("onesc", [128, 1], F32)
    stS = ar.alloc("stS", [128, CT * BS * NCH], F32)
    stQ = ar.alloc("stQ", [128, CT * BS * NCH], F32)
    carry = ar.alloc("carry", [128, CT * BS * 4], F32)
    wsc = ar.alloc("wsc", [128, 4], F32)
    tiny = ar.alloc("tiny", [128, 32], F32)
    ffc = ar.alloc("ffc", [128, 4], F32)
    kq = ar.alloc("kq", [128, 1], F32)
    kn = ar.alloc("kn", [128, 1], F32)
    c05 = ar.alloc("c05", [128, 1], F32)
    cm05 = ar.alloc("cm05", [128, 1], F32)
    RES_END = ar.mark()
    wTr = ar.alloc("wTr", [128, DT * D], BF16)
    wTk = ar.alloc("wTk", [128, DT * D], BF16)
    wTv = ar.alloc("wTv", [128, DT * D], BF16)
    woT = ar.alloc("woT", [128, CT * D], BF16)
    n1tmp = ar.alloc("n1tmp", [1, D], F32)
    REGA_END = ar.mark()

    pA = cx.enter_context(nc.psum_tensor("pA", [128, 1024], F32))
    pB = cx.enter_context(nc.psum_tensor("pB", [128, 1024], F32))
    pC = cx.enter_context(nc.psum_tensor("pC", [128, 512], F32))
    pT = cx.enter_context(nc.psum_tensor("pT", [128, 512], F32))
    pTB = cx.enter_context(nc.psum_tensor("pTB", [128, 512], BF16))
    pTB2 = cx.enter_context(nc.psum_tensor("pTB2", [128, 512], BF16))
    KpA = [("pA", 0), ("pA", 1)]
    KpB = [("pB", 0), ("pB", 1)]

    def stS_col(ct, s, ci):
        c = ct * BS * NCH + s * NCH + ci
        return stS[:, c:c + 1]

    def stQ_col(ct, s, ci):
        c = ct * BS * NCH + s * NCH + ci
        return stQ[:, c:c + 1]

    DMA("sync", lambda e: e.dma_start(cols[:, :], dcols[:, :]), writes=[cols])
    DMA("sync", lambda e: e.dma_start(idf[:, :], didf[:, :]), writes=[idf])
    DMA("sync", lambda e: e.dma_start(idb[:, :], didb[:, :]), writes=[idb])
    DMA("sync", lambda e: e.dma_start(hmask[:, :], dhmask[:, :]), writes=[hmask])
    DMA("sync", lambda e: e.dma_start(hexp[:, :], dhexp[:, :]), writes=[hexp])
    DMA("sync", lambda e: e.dma_start(ones1[:, :], dones[:, :]), writes=[ones1])
    DMA("sync", lambda e: e.dma_start(onesc[:, :], donesc[:, :]), writes=[onesc])
    DMA("sync", lambda e: e.dma_start(asbc[:, 0:1], dasb[0:1, 0:1].broadcast_to([128, 1])), writes=[asbc])
    DMA("sync", lambda e: e.dma_start(asbc[:, 1:2], dasb[1:2, 0:1].broadcast_to([128, 1])), writes=[asbc])
    DMA("sync", lambda e: e.dma_start(n1tmp[:, :], dn1row[:, :]), writes=[n1tmp])

    E("vector", lambda e: e.memset(kq[:, :], EPS_Q), writes=[kq])
    E("vector", lambda e: e.memset(kn[:, :], EPS_NORM), writes=[kn])
    E("vector", lambda e: e.memset(c05[:, :], 0.5), writes=[c05])
    E("vector", lambda e: e.memset(cm05[:, :], -0.5), writes=[cm05])
    E("vector", lambda e: e.tensor_tensor(ffc[:, 0:1], asbc[:, 0:1], asbc[:, 0:1], ALU.mult),
      reads=[asbc], writes=[ffc])
    E("vector", lambda e: e.tensor_single_scalar(ffc[:, 0:1], ffc[:, 0:1], 1.0 / HID, ALU.mult),
      reads=[ffc], writes=[ffc])
    E("vector", lambda e: e.tensor_tensor(ffc[:, 1:2], asbc[:, 0:1], asbc[:, 1:2], ALU.mult),
      reads=[asbc], writes=[ffc])
    E("vector", lambda e: e.tensor_single_scalar(ffc[:, 1:2], ffc[:, 1:2], 2.0 / HID, ALU.mult),
      reads=[ffc], writes=[ffc])
    E("vector", lambda e: e.tensor_tensor(ffc[:, 2:3], asbc[:, 1:2], asbc[:, 1:2], ALU.mult),
      reads=[asbc], writes=[ffc])

    E("scalar", lambda e: e.activation(ccon[:, 16:24], cols[:, 0:8], AF.Exp),
      reads=[cols], writes=[ccon])
    E("scalar", lambda e: e.activation(ccon[:, 0:8], ccon[:, 16:24], AF.Exp, scale=-1.0),
      reads=[ccon], writes=[ccon])
    E("scalar", lambda e: e.activation(ccon[:, 8:16], cols[:, 8:16], AF.Exp),
      reads=[cols], writes=[ccon])

    def Ew_col(ct):
        return ccon[:, ct:ct + 1]

    def Eu_col(ct):
        return ccon[:, 8 + ct:8 + ct + 1]

    def tm_col(m, ct):
        return cols[:, (2 + m) * 8 + ct:(2 + m) * 8 + ct + 1]

    def gam_col(ct):
        return cols[:, 40 + ct:40 + ct + 1]

    def bet_col(ct):
        return cols[:, 48 + ct:48 + ct + 1]

    for h in range(2):
        E("tensor", lambda e, h=h: e.matmul(pT[:, :512], ones1[:, :],
                                            n1tmp[:, h * 512:(h + 1) * 512], start=True, stop=True),
          reads=[ones1, n1tmp], writes=[pT])
        E("vector", lambda e, h=h: e.tensor_copy(n1rep[:, h * 512:(h + 1) * 512], pT[:, :512]),
          reads=[pT], writes=[n1rep])

    ps_mark = ar.mark()
    ldAb = [ar.alloc(f"ldA{i}", [128, D], F32) for i in range(2)]
    u1pb = [ar.alloc(f"u1p{i}", [128, HID], F32) for i in range(2)]
    wabs = ar.alloc("wabs", [128, 4], F32)
    wabt = ar.alloc("wabt", [128, 2], F32)

    def evac(engname, fn, reads, writes):
        E(engname, fn, reads=reads, writes=writes)

    for wi, (dw, wT) in enumerate([(dWr, wTr), (dWk, wTk), (dWv, wTv), (dWo, woT)]):
        for s in range(8):
            ldA = ldAb[s % 2]
            DMA("sync", lambda e, dw=dw, s=s, ldA=ldA: e.dma_start(ldA[:, :D], dw[s * 128:(s + 1) * 128, :]),
                writes=[ldA])
            for dt in range(DT):
                sl = dt % 4
                E("tensor", lambda e, dt=dt, sl=sl, ldA=ldA: e.transpose(
                    pT[:, sl * 128:(sl + 1) * 128], ldA[:, dt * 128:(dt + 1) * 128], idf[:, :]),
                    reads=[ldA, idf], writes=[pT])
                eng = "vector" if dt % 2 == 0 else "scalar"
                dst = wT[:, dt * D + s * 128:dt * D + (s + 1) * 128]
                if eng == "vector":
                    E(eng, lambda e, dst=dst, sl=sl: e.tensor_copy(dst, pT[:, sl * 128:(sl + 1) * 128]),
                      reads=[pT], writes=[wT])
                else:
                    E(eng, lambda e, dst=dst, sl=sl: e.copy(dst, pT[:, sl * 128:(sl + 1) * 128]),
                      reads=[pT], writes=[wT])

    for wi, (dw, nslab, width) in enumerate([(dw1, 32, D), (dw2, 8, HID)]):
        for s in range(nslab):
            ldA = u1pb[s % 2]
            tcol = wabt[:, s % 2:s % 2 + 1]
            DMA("sync", lambda e, dw=dw, s=s, width=width, ldA=ldA: e.dma_start(ldA[:, :width], dw[s * 128:(s + 1) * 128, :]),
                writes=[ldA])
            E("vector", lambda e, width=width, ldA=ldA, tcol=tcol: e.tensor_reduce(tcol, ldA[:, :width], AX, ALU.add,
                                                               apply_absolute_value=True),
              reads=[ldA], writes=[wabt])
            if s == 0:
                E("vector", lambda e, wi=wi, tcol=tcol: e.tensor_copy(wabs[:, wi:wi + 1], tcol),
                  reads=[wabt], writes=[wabs])
            else:
                E("vector", lambda e, wi=wi, tcol=tcol: e.tensor_add(wabs[:, wi:wi + 1], wabs[:, wi:wi + 1], tcol),
                  reads=[wabt, wabs], writes=[wabs])
    for wi in range(2):
        E("tensor", lambda e, wi=wi: e.matmul(pT[0:1, 0:1], wabs[:, wi:wi + 1], onesc[:, :],
                                              start=True, stop=True),
          reads=[wabs, onesc], writes=[pT])
        E("vector", lambda e, wi=wi: e.tensor_scalar(tiny[0:1, 16 + wi:17 + wi], pT[0:1, 0:1],
                                                     1.0 / (HID * D), kq[0:1, 0:1], ALU.mult, ALU.max),
          reads=[pT], writes=[tiny])
        E("vector", lambda e, wi=wi: e.reciprocal(tiny[0:1, 18 + wi:19 + wi], tiny[0:1, 16 + wi:17 + wi]),
          reads=[tiny], writes=[tiny])
    DMA("scalar", lambda e: e.dma_start(dsc[0:1, 0:4], tiny[0:1, 16:20]), reads=[tiny], writes=[dsc])
    for j in range(4):
        DMA("sync", lambda e, j=j: e.dma_start(wsc[:, j:j + 1], dsc[0:1, j:j + 1].broadcast_to([128, 1])),
            reads=[dsc], writes=[wsc])

    p.barrier_all()
    ar.reset(ps_mark)

    p.scope = "s1"
    s1_mark = ar.mark()
    xT = [[ar.alloc(f"xT{g}_{dt}", [128, 1 + F], BF16) for dt in range(DT)] for g in range(2)]
    xmT = [[ar.alloc(f"xmT{m}_{dt}", [128, F], BF16) for dt in range(DT)] for m in range(3)]
    xin = [ar.alloc(f"xin{i}", [128, D], F32) for i in range(2)]
    difb = ar.alloc("difb", [128, F], BF16)
    NS1 = 3
    Ge = [ar.alloc(f"Ge{i}", [128, 1 + F], F32) for i in range(NS1)]
    Gte = [ar.alloc(f"Gte{i}", [128, 1 + F], F32) for i in range(NS1)]
    Ne = [ar.alloc(f"Ne{i}", [128, 1 + F], F32) for i in range(NS1)]
    SCR = [{n: ar.alloc(f"{n}{i}", [128, F], F32) for n in
            ["ek", "euk", "dD", "mM", "t1", "tq", "rf", "ts1"]} for i in range(NS1)]
    rwb = [ar.alloc(f"rwb{i}", [128, F], BF16) for i in range(NS1)]
    scrapg = ar.alloc("scrapg", [128, F], F32)
    xlast = ar.alloc("xlast", [128, BS * DT], BF16)
    S1_END = ar.mark()

    E("vector", lambda e: e.memset(carry[:, :], 0.0), writes=[carry])

    for ci in range(NCH):
        for s in range(BS):
            g = (ci * BS + s) % 2
            xTg = xT[g]
            for tt in range(F // 128):
                xi = xin[tt % 2]
                t0 = ci * F + tt * 128
                DMA("sync", lambda e, s=s, t0=t0, xi=xi: e.dma_start(xi[:, :], dx[s, t0:t0 + 128, :]),
                    writes=[xi])
                for dt in range(DT):
                    sl = dt % 4
                    E("tensor", lambda e, xi=xi, dt=dt, sl=sl: e.transpose(
                        pT[:, sl * 128:(sl + 1) * 128], xi[:, dt * 128:(dt + 1) * 128], idf[:, :]),
                        reads=[xi, idf], writes=[pT])
                    dst = xTg[dt][:, 1 + tt * 128:1 + (tt + 1) * 128]
                    E("scalar", lambda e, dst=dst, sl=sl: e.copy(dst, pT[:, sl * 128:(sl + 1) * 128]),
                      reads=[pT], writes=[xTg[dt]])
            for dt in range(DT):
                if ci == 0:
                    E("vector", lambda e, dt=dt, xTg=xTg: e.memset(xTg[dt][:, 0:1], 0.0), writes=[xTg[dt]])
                else:
                    E("scalar", lambda e, dt=dt, s=s, xTg=xTg: e.copy(
                        xTg[dt][:, 0:1], xlast[:, s * DT + dt:s * DT + dt + 1]),
                        reads=[xlast], writes=[xTg[dt]])
                E("scalar", lambda e, dt=dt, s=s, xTg=xTg: e.copy(
                    xlast[:, s * DT + dt:s * DT + dt + 1], xTg[dt][:, F:F + 1]),
                    reads=[xTg[dt]], writes=[xlast])
            for dt in range(DT):
                E("vector", lambda e, dt=dt, xTg=xTg: e.tensor_tensor(
                    difb[:, :], xTg[dt][:, 1:1 + F], xTg[dt][:, 0:F], ALU.subtract),
                    reads=[xTg[dt]], writes=[difb])
                for m in range(3):
                    E("vector", lambda e, dt=dt, m=m, xTg=xTg: e.scalar_tensor_tensor(
                        xmT[m][dt][:, :], difb[:, :], tm_col(m, dt), xTg[dt][:, 0:F],
                        ALU.mult, ALU.add),
                        reads=[difb, cols, xTg[dt]], writes=[xmT[m][dt]])
            for ct in range(CT):
                r2 = ct % NS1
                sc = SCR[r2]
                ge, gte, ne = Ge[r2], Gte[r2], Ne[r2]
                h2i = ct % 2
                pk = pA[:, h2i * 512:h2i * 512 + F]
                pv = pB[:, h2i * 512:h2i * 512 + F]
                pr = pC
                kk, kv, kr = KpA[h2i], KpB[h2i], pC
                for wT, pdst, pkey, mix in [(wTk, pk, kk, 1), (wTv, pv, kv, 2), (wTr, pr, kr, 0)]:
                    for dt in range(DT):
                        E("tensor", lambda e, wT=wT, pdst=pdst, mix=mix, dt=dt, ct=ct:
                          e.matmul(pdst[:, 0:F],
                                   wT[:, dt * D + ct * 128:dt * D + (ct + 1) * 128],
                                   xmT[mix][dt][:, :],
                                   start=(dt == 0), stop=(dt == DT - 1)),
                          reads=[wT, xmT[mix][dt]], writes=[pkey])
                co = (ct * BS + s) * 4
                E("scalar", lambda e, ge=ge, co=co: e.copy(ge[:, 0:1], carry[:, co:co + 1]),
                  reads=[carry], writes=[ge])
                E("scalar", lambda e, gte=gte, co=co: e.copy(gte[:, 0:1], carry[:, co + 1:co + 2]),
                  reads=[carry], writes=[gte])
                E("scalar", lambda e, ne=ne, co=co: e.copy(ne[:, 0:1], carry[:, co + 2:co + 3]),
                  reads=[carry], writes=[ne])
                E("scalar", lambda e, sc=sc, pk=pk: e.activation(sc["ek"][:, :], pk[:, 0:F], AF.Exp),
                  reads=[kk], writes=[sc["ek"]])
                E("scalar", lambda e, sc=sc, ct=ct: e.activation(sc["euk"][:, :], sc["ek"][:, :], AF.Copy, scale=Eu_col(ct)),
                  reads=[sc["ek"], ccon], writes=[sc["euk"]])
                E("scalar", lambda e, sc=sc, pr=pr: e.activation(sc["ts1"][:, :], pr[:, 0:F], AF.Exp, scale=-1.0),
                  reads=[pr], writes=[sc["ts1"]])
                E("scalar", lambda e, sc=sc: e.activation(sc["ts1"][:, :], sc["ts1"][:, :], AF.Ln, bias=1.0),
                  reads=[sc["ts1"]], writes=[sc["ts1"]])
                E("scalar", lambda e, sc=sc: e.activation(sc["ts1"][:, :], sc["ts1"][:, :], AF.Exp, scale=-1.0),
                  reads=[sc["ts1"]], writes=[sc["ts1"]])
                E("vector", lambda e, sc=sc, ge=ge, ct=ct: e.tensor_tensor_scan(
                    ge[:, 1:1 + F], Ew_col(ct).broadcast_to([128, F]), sc["ek"][:, :],
                    ge[:, 0:1], ALU.mult, ALU.add),
                    reads=[ccon, sc["ek"], ge], writes=[ge])
                E("vector", lambda e, sc=sc, ge=ge, ct=ct: e.tensor_single_scalar(
                    sc["dD"][:, :], ge[:, 0:F], Ew_col(ct), ALU.mult),
                    reads=[ge, ccon], writes=[sc["dD"]])
                E("vector", lambda e, sc=sc: e.tensor_tensor(sc["mM"][:, :], sc["dD"][:, :], sc["ek"][:, :], ALU.max),
                  reads=[sc["dD"], sc["ek"]], writes=[sc["mM"]])
                E("scalar", lambda e, sc=sc, ge=ge: e.activation(sc["t1"][:, :], ge[:, 1:1 + F], AF.Ln),
                  reads=[ge], writes=[sc["t1"]])
                E("scalar", lambda e, sc=sc: e.activation(sc["t1"][:, :], sc["t1"][:, :], AF.Exp, scale=-1.0),
                  reads=[sc["t1"]], writes=[sc["t1"]])
                E("vector", lambda e, sc=sc: e.tensor_tensor(sc["tq"][:, :], sc["mM"][:, :], sc["t1"][:, :], ALU.mult),
                  reads=[sc["mM"], sc["t1"]], writes=[sc["tq"]])
                E("vector", lambda e, sc=sc: e.tensor_scalar(sc["tq"][:, :], sc["tq"][:, :], EPS, 1.0, ALU.mult, ALU.add),
                  reads=[sc["tq"]], writes=[sc["tq"]])
                E("vector", lambda e, sc=sc, ct=ct: e.tensor_single_scalar(sc["dD"][:, :], sc["tq"][:, :], Ew_col(ct), ALU.mult),
                  reads=[sc["tq"], ccon], writes=[sc["dD"]])
                E("vector", lambda e, sc=sc: e.tensor_tensor(sc["mM"][:, :], sc["tq"][:, :], sc["ek"][:, :], ALU.mult),
                  reads=[sc["tq"], sc["ek"]], writes=[sc["mM"]])
                E("vector", lambda e, sc=sc, gte=gte: e.tensor_tensor_scan(
                    gte[:, 1:1 + F], sc["dD"][:, :], sc["mM"][:, :], gte[:, 0:1], ALU.mult, ALU.add),
                    reads=[sc["dD"], sc["mM"], gte], writes=[gte])
                E("vector", lambda e, sc=sc, gte=gte, ct=ct: e.tensor_single_scalar(
                    sc["dD"][:, :], gte[:, 0:F], Ew_col(ct), ALU.mult),
                    reads=[gte, ccon], writes=[sc["dD"]])
                E("vector", lambda e, sc=sc: e.tensor_tensor(sc["mM"][:, :], sc["dD"][:, :], sc["ek"][:, :], ALU.max),
                  reads=[sc["dD"], sc["ek"]], writes=[sc["mM"]])
                E("vector", lambda e, sc=sc: e.tensor_tensor(sc["tq"][:, :], sc["dD"][:, :], sc["ek"][:, :], ALU.add),
                  reads=[sc["dD"], sc["ek"]], writes=[sc["tq"]])
                E("scalar", lambda e, sc=sc: e.activation(sc["t1"][:, :], sc["mM"][:, :], AF.Ln),
                  reads=[sc["mM"]], writes=[sc["t1"]])
                E("scalar", lambda e, sc=sc: e.activation(sc["t1"][:, :], sc["t1"][:, :], AF.Exp, scale=-1.0),
                  reads=[sc["t1"]], writes=[sc["t1"]])
                E("vector", lambda e, sc=sc: e.tensor_tensor(sc["rf"][:, :], sc["tq"][:, :], sc["t1"][:, :], ALU.mult),
                  reads=[sc["tq"], sc["t1"]], writes=[sc["rf"]])
                E("vector", lambda e, sc=sc, ct=ct: e.tensor_scalar(sc["dD"][:, :], sc["rf"][:, :], EPS, Ew_col(ct), ALU.add, ALU.mult),
                  reads=[sc["rf"], ccon], writes=[sc["dD"]])
                E("vector", lambda e, sc=sc, pv=pv: e.tensor_tensor(sc["mM"][:, :], sc["ek"][:, :], pv[:, 0:F], ALU.mult),
                  reads=[sc["ek"], kv], writes=[sc["mM"]])
                E("vector", lambda e, sc=sc: e.scalar_tensor_tensor(sc["tq"][:, :], sc["rf"][:, :], EPS, sc["mM"][:, :], ALU.add, ALU.mult),
                  reads=[sc["rf"], sc["mM"]], writes=[sc["tq"]])
                E("vector", lambda e, sc=sc, ne=ne: e.tensor_tensor_scan(
                    ne[:, 1:1 + F], sc["dD"][:, :], sc["tq"][:, :], ne[:, 0:1], ALU.mult, ALU.add),
                    reads=[sc["dD"], sc["tq"], ne], writes=[ne])
                E("vector", lambda e, sc=sc, pv=pv: e.tensor_tensor(sc["mM"][:, :], sc["euk"][:, :], pv[:, 0:F], ALU.mult),
                  reads=[sc["euk"], kv], writes=[sc["mM"]])
                E("vector", lambda e, sc=sc, ne=ne: e.tensor_tensor(sc["tq"][:, :], ne[:, 0:F], sc["mM"][:, :], ALU.add),
                  reads=[ne, sc["mM"]], writes=[sc["tq"]])
                E("vector", lambda e, sc=sc, gte=gte: e.tensor_tensor(sc["dD"][:, :], gte[:, 0:F], sc["euk"][:, :], ALU.add),
                  reads=[gte, sc["euk"]], writes=[sc["dD"]])
                E("scalar", lambda e, sc=sc: e.activation(sc["t1"][:, :], sc["dD"][:, :], AF.Ln),
                  reads=[sc["dD"]], writes=[sc["t1"]])
                E("scalar", lambda e, sc=sc: e.activation(sc["t1"][:, :], sc["t1"][:, :], AF.Exp, scale=-1.0),
                  reads=[sc["t1"]], writes=[sc["t1"]])
                E("vector", lambda e, sc=sc: e.tensor_tensor(sc["rf"][:, :], sc["tq"][:, :], sc["t1"][:, :], ALU.mult),
                  reads=[sc["tq"], sc["t1"]], writes=[sc["rf"]])
                rw = rwb[r2]
                E("vector", lambda e, sc=sc, rw=rw: e.tensor_tensor(rw[:, :], sc["rf"][:, :], sc["ts1"][:, :], ALU.mult),
                  reads=[sc["rf"], sc["ts1"]], writes=[rw])
                E("scalar", lambda e, ge=ge, co=co: e.copy(carry[:, co:co + 1], ge[:, F:F + 1]),
                  reads=[ge], writes=[carry])
                E("scalar", lambda e, gte=gte, co=co: e.copy(carry[:, co + 1:co + 2], gte[:, F:F + 1]),
                  reads=[gte], writes=[carry])
                E("scalar", lambda e, ne=ne, co=co: e.copy(carry[:, co + 2:co + 3], ne[:, F:F + 1]),
                  reads=[ne], writes=[carry])
                E("vector", lambda e, rw=rw, ct=ct, s=s, ci=ci: e.tensor_reduce(
                    stS_col(ct, s, ci), rw[:, :], AX, ALU.add),
                    reads=[rw], writes=[stS])
                E("scalar", lambda e, rw=rw, ct=ct, s=s, ci=ci: e.activation(
                    scrapg[:, :], rw[:, :], AF.Square, accum_out=stQ_col(ct, s, ci)),
                    reads=[rw], writes=[scrapg, stQ])
                DMA("scalar", lambda e, rw=rw, s=s, ct=ct, ci=ci: e.dma_start(
                    drwkv[s, ct, :, ci * F:(ci + 1) * F], rw[:, :]),
                    reads=[rw], writes=[drwkv])

    p.barrier_all()
    ar.reset(s1_mark)

    p.scope = "s2a"
    s2a_mark = ar.mark()
    rwin = [ar.alloc(f"rwin{i}", [128, CT * 512], BF16) for i in range(2)]
    nrm = [ar.alloc(f"nrm{i}", [128, CT * 512], BF16) for i in range(2)]
    x2in = [ar.alloc(f"x2in{i}", [128, D], F32) for i in range(2)]
    h1t = [ar.alloc(f"h1t{i}", [128, D], F32) for i in range(2)]
    x1bt = [ar.alloc(f"x1b{i}", [128, D], BF16) for i in range(2)]
    lnstb = [ar.alloc(f"lnst{i}", [128, 8], F32) for i in range(2)]
    scr2b = [ar.alloc(f"scr2{i}", [128, D], F32) for i in range(2)]
    hstat = ar.alloc("hstat", [2, 8], F32)
    S2A_END = ar.mark()

    for ct in range(CT):
        for s in range(BS):
            c0 = ct * BS * NCH + s * NCH
            E("vector", lambda e, c0=c0: e.tensor_reduce(tiny[:, 0:1], stS[:, c0:c0 + NCH], AX, ALU.add),
              reads=[stS], writes=[tiny])
            E("vector", lambda e, c0=c0: e.tensor_reduce(tiny[:, 1:2], stQ[:, c0:c0 + NCH], AX, ALU.add),
              reads=[stQ], writes=[tiny])
            E("tensor", lambda e: e.matmul(pT[0:2, 0:2], hmask[:, :], tiny[:, 0:2],
                                           start=True, stop=True),
              reads=[hmask, tiny], writes=[pT])
            E("vector", lambda e: e.tensor_single_scalar(hstat[0:2, 0:2], pT[0:2, 0:2], 1.0 / GN_N, ALU.mult),
              reads=[pT], writes=[hstat])
            E("vector", lambda e: e.tensor_tensor(hstat[0:2, 2:3], hstat[0:2, 0:1], hstat[0:2, 0:1], ALU.mult),
              reads=[hstat], writes=[hstat])
            E("vector", lambda e: e.tensor_tensor(hstat[0:2, 2:3], hstat[0:2, 1:2], hstat[0:2, 2:3], ALU.subtract),
              reads=[hstat], writes=[hstat])
            E("scalar", lambda e: e.activation(hstat[0:2, 3:4], hstat[0:2, 2:3], AF.Ln, bias=kq[0:2, 0:1]),
              reads=[hstat, kq], writes=[hstat])
            E("scalar", lambda e: e.activation(hstat[0:2, 3:4], hstat[0:2, 3:4], AF.Exp, scale=-0.5),
              reads=[hstat], writes=[hstat])
            E("tensor", lambda e: e.matmul(pT[:, 4:5], hexp[:, :], hstat[0:2, 3:4], start=True, stop=True),
              reads=[hexp, hstat], writes=[pT])
            E("tensor", lambda e: e.matmul(pT[:, 5:6], hexp[:, :], hstat[0:2, 0:1], start=True, stop=True),
              reads=[hexp, hstat], writes=[pT])
            col = ct * BS + s
            E("vector", lambda e, ct=ct, col=col: e.tensor_single_scalar(
                gnsc[:, col:col + 1], pT[:, 4:5], gam_col(ct), ALU.mult),
                reads=[pT, cols], writes=[gnsc])
            E("vector", lambda e, col=col: e.tensor_tensor(
                tiny[:, 2:3], pT[:, 5:6], gnsc[:, col:col + 1], ALU.mult),
                reads=[pT, gnsc], writes=[tiny])
            E("vector", lambda e, ct=ct, col=col: e.tensor_scalar(
                gnsh[:, col:col + 1], tiny[:, 2:3], -1.0, bet_col(ct), ALU.mult, ALU.add),
                reads=[tiny, cols], writes=[gnsh])

    for s in range(BS):
        for gq in range(SEQ // 512):
            g = (s * (SEQ // 512) + gq) % 2
            rw, nm = rwin[g], nrm[g]
            for ct in range(CT):
                DMA("scalar", lambda e, rw=rw, ct=ct, s=s, gq=gq: e.dma_start(
                    rw[:, ct * 512:(ct + 1) * 512], drwkv[s, ct, :, gq * 512:(gq + 1) * 512]),
                    reads=[drwkv], writes=[rw])
                col = ct * BS + s
                E("vector", lambda e, rw=rw, nm=nm, ct=ct, col=col: e.tensor_scalar(
                    nm[:, ct * 512:(ct + 1) * 512], rw[:, ct * 512:(ct + 1) * 512],
                    gnsc[:, col:col + 1], gnsh[:, col:col + 1], ALU.mult, ALU.add),
                    reads=[rw, gnsc, gnsh], writes=[nm])
            for tt in range(4):
                xi = x2in[tt % 2]
                h1 = h1t[tt % 2]
                xb = x1bt[tt % 2]
                lnst = lnstb[tt % 2]
                scr2 = scr2b[tt % 2]
                t0 = gq * 512 + tt * 128
                DMA("sync", lambda e, xi=xi, s=s, t0=t0: e.dma_start(xi[:, :], dx[s, t0:t0 + 128, :]),
                    writes=[xi])
                pat = pA if tt % 2 == 0 else pB
                for ct in range(CT):
                    for hh in range(2):
                        E("tensor", lambda e, nm=nm, ct=ct, tt=tt, hh=hh, pat=pat: e.matmul(
                            pat[:, hh * 512:(hh + 1) * 512],
                            nm[:, ct * 512 + tt * 128:ct * 512 + (tt + 1) * 128],
                            woT[:, ct * D + hh * 512:ct * D + (hh + 1) * 512],
                            start=(ct == 0), stop=(ct == CT - 1)),
                            reads=[nm, woT], writes=[pat])
                E("vector", lambda e, xi=xi, h1=h1, pat=pat: e.tensor_tensor(h1[:, :], xi[:, :], pat[:, :], ALU.add),
                  reads=[xi, pat], writes=[h1])
                E("vector", lambda e, h1=h1, lnst=lnst: e.tensor_reduce(lnst[:, 0:1], h1[:, :], AX, ALU.add),
                  reads=[h1], writes=[lnst])
                E("scalar", lambda e, h1=h1, scr2=scr2, lnst=lnst: e.activation(scr2[:, :], h1[:, :], AF.Square, accum_out=lnst[:, 1:2]),
                  reads=[h1], writes=[scr2, lnst])
                E("vector", lambda e, lnst=lnst: e.tensor_single_scalar(lnst[:, 2:3], lnst[:, 0:1], 1.0 / D, ALU.mult),
                  reads=[lnst], writes=[lnst])
                E("vector", lambda e, lnst=lnst: e.tensor_tensor(lnst[:, 3:4], lnst[:, 2:3], lnst[:, 2:3], ALU.mult),
                  reads=[lnst], writes=[lnst])
                E("vector", lambda e, lnst=lnst: e.tensor_single_scalar(lnst[:, 4:5], lnst[:, 1:2], 1.0 / D, ALU.mult),
                  reads=[lnst], writes=[lnst])
                E("vector", lambda e, lnst=lnst: e.tensor_tensor(lnst[:, 4:5], lnst[:, 4:5], lnst[:, 3:4], ALU.subtract),
                  reads=[lnst], writes=[lnst])
                E("scalar", lambda e, lnst=lnst: e.activation(lnst[:, 5:6], lnst[:, 4:5], AF.Ln, bias=kq[:, 0:1]),
                  reads=[lnst, kq], writes=[lnst])
                E("scalar", lambda e, lnst=lnst: e.activation(lnst[:, 5:6], lnst[:, 5:6], AF.Exp, scale=-0.5),
                  reads=[lnst], writes=[lnst])
                E("vector", lambda e, lnst=lnst: e.tensor_single_scalar(lnst[:, 6:7], lnst[:, 2:3], -1.0, ALU.mult),
                  reads=[lnst], writes=[lnst])
                E("vector", lambda e, h1=h1, xb=xb, lnst=lnst: e.tensor_scalar(
                    xb[:, :], h1[:, :], lnst[:, 6:7], lnst[:, 5:6], ALU.add, ALU.mult),
                    reads=[h1, lnst], writes=[xb])
                DMA("scalar", lambda e, xb=xb, s=s, t0=t0: e.dma_start(dx1[s, t0:t0 + 128, :], xb[:, :]),
                    reads=[xb], writes=[dx1])

    p.barrier_all()
    ar.reset(RES_END)

    p.scope = "s2b"
    s2b_mark = ar.mark()
    hscst = ar.alloc("hscst", [128, 64], F32)
    s2b2_mark = ar.mark()
    w1T = ar.alloc("w1T", [128, DT * HID], BF16)
    n2rep = ar.alloc("n2rep", [128, HID], BF16)
    n2tmp = ar.alloc("n2tmp", [1, 512], F32)
    HH = HID // 2
    Txb = [ar.alloc(f"p1x1_{i}", [128, D], BF16) for i in range(2)]
    Txnb = [ar.alloc(f"p1xn_{i}", [128, D], BF16) for i in range(2)]
    Tyf = [ar.alloc(f"p1yf_{i}", [128, D], BF16) for i in range(2)]
    TxqT = [ar.alloc(f"p1xq_{i}", [128, D], BF16) for i in range(3)]
    Tyi = [ar.alloc(f"p1yi_{i}", [128, D], I32) for i in range(4)]
    Thyq = [ar.alloc(f"p1hq_{i}", [128, HH], BF16) for i in range(2)]
    Thyf = [ar.alloc(f"p1hf_{i}", [128, HH], BF16) for i in range(3)]
    Thu1 = [ar.alloc(f"p1hu_{i}", [128, HH], BF16) for i in range(3)]
    Threl = [ar.alloc(f"p1hr_{i}", [128, HH], F32) for i in range(2)]
    ThrelB = [ar.alloc(f"p1hB_{i}", [128, HH], F32) for i in range(2)]
    ThqS = [[ar.alloc(f"p1hs_{i}_{k}", [128, 512], BF16) for k in range(2)] for i in range(2)]
    Tq = [ar.alloc(f"p1q_{i}", [128, 32], F32) for i in range(4)]
    Tdif = [ar.alloc(f"p1df_{i}", [128, HH], BF16) for i in range(2)]
    asn2 = ar.alloc("asn2", [128, HID], BF16)
    dumS = ar.alloc("dumS", [128, 2048], BF16)
    S2B1_END = ar.mark()

    wldb = [Threl[0][:, 0:D], Threl[0][:, D:2 * D]]
    wu1b = [Thyq[0][:, 0:D], Thyq[0][:, D:2 * D]]
    wu2b = [Thyf[0][:, 0:D], Thyf[0][:, D:2 * D]]
    wtnb = [Thu1[0][:, 0:D], Thu1[0][:, D:2 * D]]
    wkeys = [Threl[0], Thyq[0], Thyf[0], Thu1[0]]
    for s in range(32):
        wld, wu1, wu2, wtn = wldb[s % 2], wu1b[s % 2], wu2b[s % 2], wtnb[s % 2]
        DMA("sync", lambda e, s=s, wld=wld: e.dma_start(wld[:, :], dw1[s * 128:(s + 1) * 128, :]),
            writes=[wkeys[0]])
        E("scalar", lambda e, wld=wld: e.activation(wld[:, :], wld[:, :], AF.Copy, scale=wsc[:, 2:3]),
          reads=[wkeys[0], wsc], writes=[wkeys[0]])
        E("vector", lambda e, wld=wld, wu1=wu1: e.tensor_single_scalar(wu1[:, :], wld[:, :], 0.5, ALU.is_gt),
          reads=[wkeys[0]], writes=[wkeys[1]])
        E("vector", lambda e, wld=wld, wu2=wu2: e.tensor_single_scalar(wu2[:, :], wld[:, :], -0.5, ALU.is_lt),
          reads=[wkeys[0]], writes=[wkeys[2]])
        E("vector", lambda e, wu1=wu1, wu2=wu2, wtn=wtn: e.tensor_tensor(wtn[:, :], wu1[:, :], wu2[:, :], ALU.subtract),
          reads=[wkeys[1], wkeys[2]], writes=[wkeys[3]])
        for bb in range(8):
            sl = bb % 4
            ptb = pTB if (bb // 4) % 2 == 0 else pTB2
            E("tensor", lambda e, bb=bb, sl=sl, wtn=wtn, ptb=ptb: e.transpose(
                ptb[:, sl * 128:(sl + 1) * 128], wtn[:, bb * 128:(bb + 1) * 128], idb[:, :]),
                reads=[wkeys[3], idb], writes=[ptb])
            if bb % 4 == 3:
                eng = "vector" if (bb // 4) % 2 == 0 else "scalar"
                for k in range(4):
                    dstk = w1T[:, (bb - 3 + k) * HID + s * 128:(bb - 3 + k) * HID + (s + 1) * 128]
                    if eng == "vector":
                        E(eng, lambda e, dstk=dstk, k=k, ptb=ptb: e.tensor_copy(dstk, ptb[:, k * 128:(k + 1) * 128]),
                          reads=[ptb], writes=[w1T])
                    else:
                        E(eng, lambda e, dstk=dstk, k=k, ptb=ptb: e.copy(dstk, ptb[:, k * 128:(k + 1) * 128]),
                          reads=[ptb], writes=[w1T])
    for h in range(HID // 512):
        DMA("sync", lambda e, h=h: e.dma_start(n2tmp[:, :], dn2row[:, h * 512:(h + 1) * 512]),
            writes=[n2tmp])
        E("tensor", lambda e, h=h: e.matmul(pT[:, :512], ones1[:, :], n2tmp[:, :], start=True, stop=True),
          reads=[ones1, n2tmp], writes=[pT])
        E("vector", lambda e, h=h: e.tensor_copy(n2rep[:, h * 512:(h + 1) * 512], pT[:, :512]),
          reads=[pT], writes=[n2rep])
    E("vector", lambda e: e.tensor_single_scalar(asn2[:, :], n2rep[:, :], asbc[:, 0:1], ALU.mult),
      reads=[n2rep, asbc], writes=[asn2])
    E("vector", lambda e: e.reciprocal(ffc[:, 3:4], asbc[:, 0:1]), reads=[asbc], writes=[ffc])
    E("vector", lambda e: e.tensor_tensor(ffc[:, 3:4], ffc[:, 3:4], asbc[:, 1:2], ALU.mult),
      reads=[ffc, asbc], writes=[ffc])

    NTC = SEQ // 128
    NUN = BS * NTC
    pCb = pC[:, :].bitcast(BF16)

    def p1_Apre(u):
        s, t0 = u // NTC, (u % NTC) * 128
        xb = Txb[u % 2]
        DMA("sync", lambda e, xb=xb, s=s, t0=t0: e.dma_start(xb[:, :], dx1[s, t0:t0 + 128, :]),
            reads=[dx1], writes=[xb])

    def p1_A0(u):
        xb, q = Txb[u % 2], Tq[u % 4]
        E("scalar", lambda e, xb=xb, q=q: e.activation(dumS[:, 0:D], xb[:, :], AF.Square, accum_out=q[:, 0:1]),
          reads=[xb], writes=[dumS, q])
        E("scalar", lambda e, q=q: e.activation(q[:, 1:2], q[:, 0:1], AF.Ln, bias=kn[:, 0:1], scale=1.0 / D),
          reads=[q, kn], writes=[q])
        E("scalar", lambda e, q=q: e.activation(q[:, 1:2], q[:, 1:2], AF.Exp, scale=-0.5),
          reads=[q], writes=[q])

    def p1_Av(u):
        xb, xnb, yf, yi, q = Txb[u % 2], Txnb[u % 2], Tyf[u % 2], Tyi[u % 4], Tq[u % 4]
        E("vector", lambda e, xb=xb, xnb=xnb: e.tensor_tensor(xnb[:, :], xb[:, :], n1rep[:, :], ALU.mult),
          reads=[xb, n1rep], writes=[xnb])
        E("vector", lambda e, xnb=xnb, q=q: e.tensor_reduce(q[:, 2:3], xnb[:, :], AX, ALU.max, apply_absolute_value=True),
          reads=[xnb], writes=[q])
        E("vector", lambda e, q=q: e.tensor_scalar(q[:, 3:4], q[:, 2:3], q[:, 1:2], kq[:, 0:1], ALU.mult, ALU.max),
          reads=[q, kq], writes=[q])
        E("vector", lambda e, q=q: e.reciprocal(q[:, 4:5], q[:, 3:4]), reads=[q], writes=[q])
        E("vector", lambda e, q=q: e.tensor_scalar(q[:, 6:7], q[:, 4:5], 127.0, q[:, 1:2], ALU.mult, ALU.mult),
          reads=[q], writes=[q])
        E("vector", lambda e, q=q: e.tensor_scalar(q[:, 7:8], q[:, 3:4], 1.0 / 127.0, wsc[:, 0:1], ALU.mult, ALU.mult),
          reads=[q, wsc], writes=[q])
        E("vector", lambda e, xnb=xnb, q=q: e.tensor_single_scalar(xnb[:, :], xnb[:, :], q[:, 6:7], ALU.mult),
          reads=[xnb, q], writes=[xnb])
        E("vector", lambda e, xnb=xnb, yi=yi: e.tensor_copy(yi[:, :], xnb[:, :]), reads=[xnb], writes=[yi])
        E("vector", lambda e, yi=yi, yf=yf: e.tensor_copy(yf[:, :], yi[:, :]), reads=[yi], writes=[yf])
        E("vector", lambda e, xnb=xnb, yf=yf: e.tensor_tensor(xnb[:, :], xnb[:, :], yf[:, :], ALU.subtract),
          reads=[xnb, yf], writes=[xnb])
        E("vector", lambda e, xnb=xnb, yf=yf: e.scalar_tensor_tensor(
            yf[:, :], xnb[:, :], c05[:, 0:1], yf[:, :], ALU.is_gt, ALU.add),
          reads=[xnb, c05, yf], writes=[yf])
        E("vector", lambda e, xnb=xnb, yf=yf: e.scalar_tensor_tensor(
            yf[:, :], xnb[:, :], cm05[:, 0:1], yf[:, :], ALU.is_ge, ALU.add),
          reads=[xnb, cm05, yf], writes=[yf])
        E("vector", lambda e, yf=yf: e.tensor_single_scalar(yf[:, :], yf[:, :], -1.0, ALU.add),
          reads=[yf], writes=[yf])

    pTbc = pT[:, :].bitcast(BF16)

    def p1_Atr(u):
        yf, xqT = Tyf[u % 2], TxqT[u % 3]
        for bb in range(8):
            sl = bb % 4
            if bb < 4:
                E("tensor", lambda e, bb=bb, sl=sl, yf=yf: e.transpose(
                    pTB[:, sl * 128:(sl + 1) * 128], yf[:, bb * 128:(bb + 1) * 128], idb[:, :]),
                    reads=[yf, idb], writes=[pTB])
            else:
                E("tensor", lambda e, bb=bb, sl=sl, yf=yf: e.transpose(
                    pTbc[:, sl * 128:(sl + 1) * 128], yf[:, bb * 128:(bb + 1) * 128], idb[:, :]),
                    reads=[yf, idb], writes=[pT])
        E("vector", lambda e, xqT=xqT: e.tensor_copy(xqT[:, 0:512], pTB[:, 0:512]),
          reads=[pTB], writes=[xqT])
        E("vector", lambda e, xqT=xqT: e.tensor_copy(xqT[:, 512:1024], pTbc[:, 0:512]),
          reads=[pT], writes=[xqT])

    def p1_Bmm(u):
        xqT, q = TxqT[u % 3], Tq[u % 4]
        for jg in range(2):
            hdst = Threl[u % 2] if jg == 0 else ThrelB[u % 2]
            for dt in range(DT):
                for hh in range(4):
                    ph = pA if hh < 2 else pB
                    ho = (hh % 2) * 512
                    E("tensor", lambda e, dt=dt, jg=jg, hh=hh, ph=ph, ho=ho, xqT=xqT: e.matmul(
                        ph[:, ho:ho + 512],
                        xqT[:, dt * 128:(dt + 1) * 128],
                        w1T[:, dt * HID + jg * 2048 + hh * 512:dt * HID + jg * 2048 + (hh + 1) * 512],
                        start=(dt == 0), stop=(dt == DT - 1)),
                        reads=[xqT, w1T], writes=[ph])
            for piece, ph in [(0, pA), (1, pB)]:
                off = piece * 1024
                E("scalar", lambda e, ph=ph, off=off, hdst=hdst, q=q: e.activation(
                    hdst[:, off:off + 1024], ph[:, :], AF.Relu, scale=q[:, 7:8]),
                  reads=[ph, q], writes=[hdst])
            E("scalar", lambda e, jg=jg, hdst=hdst, q=q: e.activation(hdst[:, :], hdst[:, :], AF.Square,
                                                                      accum_out=q[:, 8 + jg:9 + jg]),
              reads=[hdst], writes=[hdst, q])
            E("scalar", lambda e, jg=jg, hdst=hdst, q=q: e.activation(dumS[:, 0:2048], hdst[:, :], AF.Square,
                                                                      accum_out=q[:, 10 + jg:11 + jg]),
              reads=[hdst], writes=[dumS, q])

    def p1_Bv(u):
        hrel, hrelB, hyf, hu1, q = Threl[u % 2], ThrelB[u % 2], Thyf[u % 3], Thu1[u % 3], Tq[u % 4]
        E("vector", lambda e, hrel=hrel, hyf=hyf: e.scalar_tensor_tensor(
            hyf[:, :], hrel[:, :], ffc[:, 3:4], asn2[:, 0:HH], ALU.add, ALU.mult),
          reads=[hrel, ffc, asn2], writes=[hyf])
        E("vector", lambda e, hrelB=hrelB, hu1=hu1: e.scalar_tensor_tensor(
            hu1[:, :], hrelB[:, :], ffc[:, 3:4], asn2[:, HH:HID], ALU.add, ALU.mult),
          reads=[hrelB, ffc, asn2], writes=[hu1])
        E("vector", lambda e, q=q: e.tensor_tensor(q[:, 16:17], q[:, 8:9], q[:, 9:10], ALU.add),
          reads=[q], writes=[q])
        E("vector", lambda e, q=q: e.tensor_tensor(q[:, 17:18], q[:, 10:11], q[:, 11:12], ALU.add),
          reads=[q], writes=[q])
        E("vector", lambda e, q=q: e.tensor_scalar(q[:, 20:21], q[:, 17:18], ffc[:, 0:1], None, ALU.mult),
          reads=[q, ffc], writes=[q])
        E("vector", lambda e, q=q: e.scalar_tensor_tensor(q[:, 20:21], q[:, 16:17], ffc[:, 1:2], q[:, 20:21], ALU.mult, ALU.add),
          reads=[q, ffc], writes=[q])
        E("vector", lambda e, q=q: e.tensor_single_scalar(q[:, 20:21], q[:, 20:21], ffc[:, 2:3], ALU.add),
          reads=[q, ffc], writes=[q])
        E("scalar", lambda e, q=q: e.activation(q[:, 21:22], q[:, 20:21], AF.Ln, bias=kn[:, 0:1]),
          reads=[q, kn], writes=[q])
        E("scalar", lambda e, q=q: e.activation(q[:, 21:22], q[:, 21:22], AF.Exp, scale=-0.5),
          reads=[q], writes=[q])

    def p1_Bq(u):
        hyf, hu1, q = Thyf[u % 3], Thu1[u % 3], Tq[u % 4]
        E("vector", lambda e, hyf=hyf, q=q: e.tensor_reduce(q[:, 14:15], hyf[:, :], AX, ALU.max, apply_absolute_value=True),
          reads=[hyf], writes=[q])
        E("vector", lambda e, hu1=hu1, q=q: e.tensor_reduce(q[:, 15:16], hu1[:, :], AX, ALU.max, apply_absolute_value=True),
          reads=[hu1], writes=[q])
        E("vector", lambda e, q=q: e.tensor_tensor(q[:, 14:15], q[:, 14:15], q[:, 15:16], ALU.max),
          reads=[q], writes=[q])
        E("vector", lambda e, q=q: e.tensor_scalar(q[:, 22:23], q[:, 14:15], q[:, 21:22], kq[:, 0:1], ALU.mult, ALU.max),
          reads=[q, kq], writes=[q])
        E("vector", lambda e, q=q: e.reciprocal(q[:, 23:24], q[:, 22:23]), reads=[q], writes=[q])
        E("vector", lambda e, q=q: e.tensor_scalar(q[:, 25:26], q[:, 23:24], 127.0, q[:, 21:22], ALU.mult, ALU.mult),
          reads=[q], writes=[q])
        E("vector", lambda e, q=q: e.tensor_scalar(q[:, 26:27], q[:, 22:23], 1.0 / 127.0, wsc[:, 1:2], ALU.mult, ALU.mult),
          reads=[q, wsc], writes=[q])
        E("vector", lambda e, q=q, u=u: e.tensor_copy(hscst[:, u:u + 1], q[:, 26:27]),
          reads=[q], writes=[hscst])

    def p1_Cra(u):
        q = Tq[u % 4]
        for jg in range(2):
            rnd = Thyf[u % 3] if jg == 0 else Thu1[u % 3]
            hyq = Thyq[(u + jg) % 2]
            E("vector", lambda e, rnd=rnd, hyq=hyq, q=q: e.tensor_single_scalar(
                hyq[:, :], rnd[:, :], q[:, 25:26], ALU.mult),
              reads=[rnd, q], writes=[hyq])
        for jg in range(2):
            rnd = Thyf[u % 3] if jg == 0 else Thu1[u % 3]
            hyq = Thyq[(u + jg) % 2]
            yi = Tyi[(u + 2 * jg) % 4]
            for qq in range(2):
                E("scalar", lambda e, qq=qq, hyq=hyq, yi=yi: e.copy(yi[:, :], hyq[:, qq * 1024:(qq + 1) * 1024]),
                  reads=[hyq], writes=[yi])
                E("scalar", lambda e, qq=qq, rnd=rnd, yi=yi: e.copy(rnd[:, qq * 1024:(qq + 1) * 1024], yi[:, :]),
                  reads=[yi], writes=[rnd])

    def p1_Crb(u):
        for jg in range(2):
            rnd = Thyf[u % 3] if jg == 0 else Thu1[u % 3]
            hyq = Thyq[(u + jg) % 2]
            dhf = Tdif[(u + jg) % 2]
            E("vector", lambda e, hyq=hyq, rnd=rnd, dhf=dhf: e.tensor_tensor(dhf[:, :], hyq[:, :], rnd[:, :], ALU.subtract),
              reads=[hyq, rnd], writes=[dhf])
            E("vector", lambda e, dhf=dhf, rnd=rnd: e.scalar_tensor_tensor(
                rnd[:, :], dhf[:, :], c05[:, 0:1], rnd[:, :], ALU.is_gt, ALU.add),
              reads=[dhf, c05, rnd], writes=[rnd])
            E("vector", lambda e, dhf=dhf, rnd=rnd: e.scalar_tensor_tensor(
                rnd[:, :], dhf[:, :], cm05[:, 0:1], rnd[:, :], ALU.is_ge, ALU.add),
              reads=[dhf, cm05, rnd], writes=[rnd])
            E("vector", lambda e, rnd=rnd: e.tensor_single_scalar(rnd[:, :], rnd[:, :], -1.0, ALU.add),
              reads=[rnd], writes=[rnd])

    KpC = [("pClo",), ("pChi",)]

    def p1_Ct(u):
        s, tc = u // NTC, u % NTC
        hqS = ThqS[u % 2]
        slotsets = [(pTB2, 0, pTB2), (pCb, 0, KpC[0]), (pCb, 512, KpC[1])]
        for jg in range(2):
            hof = jg * HH
            rnd = Thyf[u % 3] if jg == 0 else Thu1[u % 3]
            for bb in range(HH // 128):
                gg = u * 8 + jg * 4 + bb // 4
                base, off, pkey = slotsets[gg % 3]
                sl = bb % 4
                E("tensor", lambda e, bb=bb, sl=sl, rnd=rnd, base=base, off=off: e.transpose(
                    base[:, off + sl * 128:off + (sl + 1) * 128], rnd[:, bb * 128:(bb + 1) * 128], idb[:, :]),
                    reads=[rnd, idb], writes=[pkey])
                if bb % 4 == 3:
                    hq_s = hqS[gg % 2]
                    E("scalar", lambda e, hq_s=hq_s, base=base, off=off: e.copy(hq_s[:, :], base[:, off:off + 512]),
                      reads=[pkey], writes=[hq_s])
                    DMA("scalar", lambda e, hq_s=hq_s, s=s, tc=tc, hof=hof, bb=bb: e.dma_start(
                        dhq[s, tc, :, hof + (bb - 3) * 128:hof + (bb + 1) * 128], hq_s[:, :]),
                        reads=[hq_s], writes=[dhq])

    p1_Apre(0)
    for c in range(NUN + 4):
        if 0 <= c - 1 < NUN:
            p1_Atr(c - 1)
        if c < NUN:
            p1_A0(c)
        if 0 <= c - 3 < NUN:
            p1_Bq(c - 3)
            p1_Cra(c - 3)
        if 0 <= c - 1 < NUN:
            p1_Bmm(c - 1)
        if c < NUN:
            p1_Av(c)
        if 0 <= c - 3 < NUN:
            p1_Crb(c - 3)
        if 0 <= c - 2 < NUN:
            p1_Bv(c - 2)
        if 0 <= c - 4 < NUN:
            p1_Ct(c - 4)
        if c + 1 < NUN:
            p1_Apre(c + 1)

    p.barrier_all()
    ar.reset(s2b2_mark)

    p.scope = "s2c"
    s2c_mark = ar.mark()
    w2T = ar.alloc("w2T", [128, JT * D], BF16)
    P2 = []
    for i in range(3):
        P2.append({
            "hqT": ar.alloc(f"p2hq_{i}", [128, HID], BF16),
            "x1": ar.alloc(f"p2x1_{i}", [128, D], BF16),
            "xf": ar.alloc(f"p2xf_{i}", [128, D], F32),
            "yo": ar.alloc(f"p2yo_{i}", [128, D], F32),
            "sq": ar.alloc(f"p2sq_{i}", [128, D], F32),
            "q": ar.alloc(f"p2q_{i}", [128, 8], F32),
        })
    S2C_END = ar.mark()

    w2ldb = [ar.alloc(f"w2ld{i}", [128, HID], F32) for i in range(2)]
    w2u1b = [ar.alloc(f"w2u1{i}", [128, HID], BF16) for i in range(2)]
    w2tnb = [ar.alloc(f"w2tn{i}", [128, HID], BF16) for i in range(2)]
    for s in range(8):
        wld, wu1, wtn = w2ldb[s % 2], w2u1b[s % 2], w2tnb[s % 2]
        DMA("sync", lambda e, s=s, wld=wld: e.dma_start(wld[:, :], dw2[s * 128:(s + 1) * 128, :]),
            writes=[wld])
        E("scalar", lambda e, wld=wld: e.activation(wld[:, :], wld[:, :], AF.Copy, scale=wsc[:, 3:4]),
          reads=[wld, wsc], writes=[wld])
        E("vector", lambda e, wld=wld, wtn=wtn: e.tensor_single_scalar(wtn[:, :], wld[:, :], 0.5, ALU.is_gt),
          reads=[wld], writes=[wtn])
        E("vector", lambda e, wld=wld, wu1=wu1: e.tensor_single_scalar(wu1[:, :], wld[:, :], -0.5, ALU.is_lt),
          reads=[wld], writes=[wu1])
        E("vector", lambda e, wu1=wu1, wtn=wtn: e.tensor_tensor(wtn[:, :], wtn[:, :], wu1[:, :], ALU.subtract),
          reads=[wu1, wtn], writes=[wtn])
        for bb in range(JT):
            sl = bb % 4
            ptb = pTB if (bb // 4) % 2 == 0 else pTB2
            E("tensor", lambda e, bb=bb, sl=sl, wtn=wtn, ptb=ptb: e.transpose(
                ptb[:, sl * 128:(sl + 1) * 128], wtn[:, bb * 128:(bb + 1) * 128], idb[:, :]),
                reads=[wtn, idb], writes=[ptb])
            if bb % 4 == 3:
                eng = "vector" if (bb // 4) % 2 == 0 else "scalar"
                for k in range(4):
                    dstk = w2T[:, (bb - 3 + k) * D + s * 128:(bb - 3 + k) * D + (s + 1) * 128]
                    if eng == "vector":
                        E(eng, lambda e, dstk=dstk, k=k, ptb=ptb: e.tensor_copy(dstk, ptb[:, k * 128:(k + 1) * 128]),
                          reads=[ptb], writes=[w2T])
                    else:
                        E(eng, lambda e, dstk=dstk, k=k, ptb=ptb: e.copy(dstk, ptb[:, k * 128:(k + 1) * 128]),
                          reads=[ptb], writes=[w2T])
    for s in range(BS):
        for tc in range(NTC):
            t0 = tc * 128
            S = P2[(s * NTC + tc) % 3]
            hqT, xb, xf, yo, sqt, q = S["hqT"], S["x1"], S["xf"], S["yo"], S["sq"], S["q"]
            cidx = s * NTC + tc
            DMA("sync", lambda e, hqT=hqT, s=s, tc=tc: e.dma_start(hqT[:, :], dhq[s, tc, :, :]),
                reads=[dhq], writes=[hqT])
            DMA("sync", lambda e, xb=xb, s=s, t0=t0: e.dma_start(xb[:, :], dx1[s, t0:t0 + 128, :]),
                reads=[dx1], writes=[xb])
            pcy = pA if (s * NTC + tc) % 2 == 0 else pB
            for jt in range(JT):
                for hh in range(2):
                    E("tensor", lambda e, jt=jt, hh=hh, hqT=hqT, pcy=pcy: e.matmul(
                        pcy[:, hh * 512:(hh + 1) * 512],
                        hqT[:, jt * 128:(jt + 1) * 128],
                        w2T[:, jt * D + hh * 512:jt * D + (hh + 1) * 512],
                        start=(jt == 0), stop=(jt == JT - 1)),
                        reads=[hqT, w2T], writes=[pcy])
            E("scalar", lambda e, xb=xb, xf=xf: e.copy(xf[:, :], xb[:, :]), reads=[xb], writes=[xf])
            E("vector", lambda e, yo=yo, cidx=cidx, pcy=pcy: e.tensor_single_scalar(yo[:, :], pcy[:, :], hscst[:, cidx:cidx + 1], ALU.mult),
              reads=[pcy, hscst], writes=[yo])
            E("vector", lambda e, yo=yo, xf=xf: e.tensor_tensor(yo[:, :], yo[:, :], xf[:, :], ALU.add),
              reads=[yo, xf], writes=[yo])
            E("vector", lambda e, yo=yo, q=q: e.tensor_reduce(q[:, 0:1], yo[:, :], AX, ALU.add),
              reads=[yo], writes=[q])
            E("scalar", lambda e, yo=yo, sqt=sqt, q=q: e.activation(sqt[:, :], yo[:, :], AF.Square, accum_out=q[:, 1:2]),
              reads=[yo], writes=[sqt, q])
            E("vector", lambda e, q=q: e.tensor_single_scalar(q[:, 2:3], q[:, 0:1], 1.0 / D, ALU.mult),
              reads=[q], writes=[q])
            E("vector", lambda e, q=q: e.tensor_tensor(q[:, 3:4], q[:, 2:3], q[:, 2:3], ALU.mult),
              reads=[q], writes=[q])
            E("vector", lambda e, q=q: e.tensor_single_scalar(q[:, 4:5], q[:, 1:2], 1.0 / D, ALU.mult),
              reads=[q], writes=[q])
            E("vector", lambda e, q=q: e.tensor_tensor(q[:, 4:5], q[:, 4:5], q[:, 3:4], ALU.subtract),
              reads=[q], writes=[q])
            E("scalar", lambda e, q=q: e.activation(q[:, 5:6], q[:, 4:5], AF.Ln, bias=kq[:, 0:1]),
              reads=[q, kq], writes=[q])
            E("scalar", lambda e, q=q: e.activation(q[:, 5:6], q[:, 5:6], AF.Exp, scale=-0.5),
              reads=[q], writes=[q])
            E("vector", lambda e, q=q: e.tensor_single_scalar(q[:, 6:7], q[:, 2:3], -1.0, ALU.mult),
              reads=[q], writes=[q])
            E("vector", lambda e, yo=yo, q=q: e.tensor_scalar(yo[:, :], yo[:, :], q[:, 6:7], q[:, 5:6], ALU.add, ALU.mult),
              reads=[yo, q], writes=[yo])
            DMA("scalar", lambda e, s=s, t0=t0, yo=yo: e.dma_start(dout[s, t0:t0 + 128, :], yo[:, :]),
                reads=[yo])

    p.final_wait("sync")
    p.build(cx)
    cx.close()
    return nc


def kernel(**inputs):
    x = np.ascontiguousarray(inputs["L0_attn_x"])

    def colmaj(v):
        return np.ascontiguousarray(np.asarray(v).reshape(8, 128).T.astype(np.float32))

    cols = np.concatenate([
        colmaj(inputs["L0_attn_time_decay"].reshape(D)),
        colmaj(inputs["L0_attn_time_first"].reshape(D)),
        colmaj(inputs["L0_attn_time_mix_r"].reshape(D)),
        colmaj(inputs["L0_attn_time_mix_k"].reshape(D)),
        colmaj(inputs["L0_attn_time_mix_v"].reshape(D)),
        colmaj(inputs["L0_attn_ln_gamma"].reshape(D)),
        colmaj(inputs["L0_attn_ln_beta"].reshape(D)),
        np.zeros((128, 8), np.float32)], axis=1)
    hmask = np.zeros((128, 2), np.float32)
    hmask[:64, 0] = 1.0
    hmask[64:, 1] = 1.0
    common = {
        "W_r": np.ascontiguousarray(inputs["L0_attn_W_r"], np.float32),
        "W_k": np.ascontiguousarray(inputs["L0_attn_W_k"], np.float32),
        "W_v": np.ascontiguousarray(inputs["L0_attn_W_v"], np.float32),
        "W_o": np.ascontiguousarray(inputs["L0_attn_W_o"], np.float32),
        "w1": np.ascontiguousarray(inputs["L0_ffn_w1"], np.float32),
        "w2": np.ascontiguousarray(inputs["L0_ffn_w2"], np.float32),
        "cols": cols,
        "n1row": np.ascontiguousarray(inputs["L0_ffn_norm1_w"].reshape(1, D), np.float32),
        "n2row": np.ascontiguousarray(inputs["L0_ffn_norm2_w"].reshape(1, HID), np.float32),
        "asb": np.ascontiguousarray(
            np.stack([np.asarray(inputs["L0_ffn_act_scale"]).reshape(()),
                      np.asarray(inputs["L0_ffn_act_bias"]).reshape(())]).reshape(2, 1), np.float32),
        "idf": np.eye(128, dtype=np.float32),
        "idb": np.eye(128, dtype=bfdt),
        "hmask": hmask,
        "hexp": np.ascontiguousarray(hmask.T),
        "ones1": np.ones((1, 128), np.float32),
        "onesc": np.ones((128, 1), np.float32),
    }
    xs = x.reshape(NCORE, BS, SEQ, D)
    in_maps = [dict(common, x=np.ascontiguousarray(xs[c])) for c in range(NCORE)]
    nc = build_nc()
    res = run_bass_kernel_spmd(nc, in_maps, core_ids=list(range(NCORE)))
    out = np.concatenate([np.asarray(r["out"]).reshape(1, BS, SEQ, D) for r in res.results], axis=0)
    return np.ascontiguousarray(out.reshape(16, SEQ, D)).astype(np.float32)


# version 21
# speedup vs baseline: 1.0235x; 1.0235x over previous
import numpy as np
import ml_dtypes
import contextlib
import concourse.bass as bass
import concourse.mybir as mybir
from concourse.bass_utils import run_bass_kernel_spmd

F32 = mybir.dt.float32
BF16 = mybir.dt.bfloat16
I32 = mybir.dt.int32
AF = mybir.ActivationFunctionType
ALU = mybir.AluOpType
AX = mybir.AxisListType.X
bfdt = ml_dtypes.bfloat16

D = 1024
SEQ = 4096
BS = 2
NCORE = 8
HID = 4096
F = 512
NCH = SEQ // F
CT = 8
DT = 8
JT = HID // 128
EPS = 1e-8
LN_EPS = 1e-5
EPS_NORM = 1e-8
EPS_Q = 1e-5
GN_N = float(SEQ * 64)


class Prog:
    ENGS = ("vector", "scalar", "gpsimd", "tensor", "sync")

    def __init__(self, nc, dma_pool=8):
        self.nc = nc
        self.scope = "prep"
        self.q = {e: [] for e in self.ENGS}
        self.cnt = {e: 0 for e in self.ENGS}
        self.dma_pool = dma_pool
        self.dma_idx = {e: 0 for e in self.ENGS}
        self.dma_uses = {}
        self.buf = {}
        self.last_wait = {}
        self.sem_keys = set(self.ENGS)

    def _bufstate(self, h):
        return self.buf.setdefault(id(h), {"w": None, "r": []})

    def _collect(self, eng, reads, writes):
        waits = {}

        def need(dep):
            if dep is None:
                return
            key, val = dep
            if key == eng == "tensor":
                return
            if val <= self.last_wait.get((eng, key), 0):
                return
            if waits.get(key, 0) < val:
                waits[key] = val
        for h in reads:
            need(self._bufstate(h)["w"])
        for h in writes:
            st = self._bufstate(h)
            need(st["w"])
            for r in st["r"]:
                need(r)
        for key, val in waits.items():
            self.last_wait[(eng, key)] = val
        return waits

    def emit(self, eng, fn, reads=(), writes=()):
        waits = self._collect(eng, reads, writes)
        self.cnt[eng] += 1
        me = (eng, self.cnt[eng])
        for h in reads:
            self._bufstate(h)["r"].append(me)
        for h in writes:
            st = self._bufstate(h)
            st["w"] = me
            st["r"] = []
        self.q[eng].append((waits, fn, eng, 1, self.scope))
        return me

    def dma(self, eng, fn, reads=(), writes=()):
        waits = self._collect(eng, reads, writes)
        j = self.dma_idx[eng]
        self.dma_idx[eng] = (j + 1) % self.dma_pool
        key = (eng, "dma", j)
        self.sem_keys.add(key)
        uses = self.dma_uses.get(key, 0)
        if uses > 0:
            v = 16 * uses
            if v > self.last_wait.get((eng, key), 0):
                if waits.get(key, 0) < v:
                    waits[key] = v
                self.last_wait[(eng, key)] = v
        self.dma_uses[key] = uses + 1
        me = (key, 16 * (uses + 1))
        for h in reads:
            self._bufstate(h)["r"].append(me)
        for h in writes:
            st = self._bufstate(h)
            st["w"] = me
            st["r"] = []
        self.q[eng].append((waits, fn, key, 16, self.scope))
        return me

    def barrier_all(self):
        snap = [(e, self.cnt[e]) for e in self.ENGS if self.cnt[e] > 0]
        snap += [(k, 16 * u) for k, u in self.dma_uses.items()]
        for eng in self.ENGS:
            waits = {}
            for key, val in snap:
                if key == eng:
                    continue
                if val > self.last_wait.get((eng, key), 0):
                    waits[key] = val
                    self.last_wait[(eng, key)] = val
            if waits:
                self.q[eng].append((waits, None, None, 0, self.scope))

    def final_wait(self, eng):
        waits = {}
        for key, val in [(e, self.cnt[e]) for e in self.ENGS] + \
                        [(k, 16 * u) for k, u in self.dma_uses.items()]:
            if key == eng or val == 0:
                continue
            if val > self.last_wait.get((eng, key), 0):
                waits[key] = val
                self.last_wait[(eng, key)] = val
        self.q[eng].append((waits, None, None, 0, self.scope))

    def build(self, cx):
        sems = {}
        for key in sorted(self.sem_keys, key=str):
            name = key if isinstance(key, str) else f"{key[0]}_d{key[2]}"
            sems[key] = cx.enter_context(self.nc.semaphore(f"s_{name}"))

        def mk(eng):
            def body(e):
                cur = None
                import contextlib as _cl
                stack = _cl.ExitStack()
                for waits, fn, sem_key, inc, scope in self.q[eng]:
                    if scope != cur:
                        stack.close()
                        stack = _cl.ExitStack()
                        stack.enter_context(self.nc.named_scope(f"{scope}_{eng}"))
                        cur = scope
                    for key, val in sorted(waits.items(), key=str):
                        e.wait_ge(sems[key], val)
                    if fn is not None:
                        fn(e).then_inc(sems[sem_key], inc)
                stack.close()
            return body
        with self.nc.Block() as blk:
            for eng in self.ENGS:
                if self.q[eng]:
                    getattr(blk, eng)(mk(eng))


class Arena:
    TOP = 229344

    def __init__(self, nc, base=0):
        self.nc = nc
        self.off = base
        self.n = 0

    def alloc(self, name, shape, dtype):
        per_part = int(np.prod(shape[1:])) * mybir.dt.size(dtype)
        size32 = (per_part + 31) & ~31
        self.off = ((self.off + 31) & ~31) + size32
        self.n += 1
        assert self.off <= 212500, f"SBUF overflow at {name}: {self.off}"
        addr = self.TOP - self.off
        return self.nc.alloc_sbuf_tensor_at(f"{name}_{self.n}", shape, dtype, offset=addr)

    def mark(self):
        return self.off

    def reset(self, off):
        self.off = off


def build_nc():
    nc = bass.Bass("TRN2", target_bir_lowering=False)
    dx = nc.dram_tensor("x", [BS, SEQ, D], F32, kind="ExternalInput")
    dout = nc.dram_tensor("out", [BS, SEQ, D], F32, kind="ExternalOutput")
    dWr = nc.dram_tensor("W_r", [D, D], F32, kind="ExternalInput")
    dWk = nc.dram_tensor("W_k", [D, D], F32, kind="ExternalInput")
    dWv = nc.dram_tensor("W_v", [D, D], F32, kind="ExternalInput")
    dWo = nc.dram_tensor("W_o", [D, D], F32, kind="ExternalInput")
    dw1 = nc.dram_tensor("w1", [HID, D], F32, kind="ExternalInput")
    dw2 = nc.dram_tensor("w2", [D, HID], F32, kind="ExternalInput")
    dcols = nc.dram_tensor("cols", [128, 64], F32, kind="ExternalInput")
    dn1row = nc.dram_tensor("n1row", [1, D], F32, kind="ExternalInput")
    dn2row = nc.dram_tensor("n2row", [1, HID], F32, kind="ExternalInput")
    dasb = nc.dram_tensor("asb", [2, 1], F32, kind="ExternalInput")
    didf = nc.dram_tensor("idf", [128, 128], F32, kind="ExternalInput")
    didb = nc.dram_tensor("idb", [128, 128], BF16, kind="ExternalInput")
    dhmask = nc.dram_tensor("hmask", [128, 2], F32, kind="ExternalInput")
    dhexp = nc.dram_tensor("hexp", [2, 128], F32, kind="ExternalInput")
    dones = nc.dram_tensor("ones1", [1, 128], F32, kind="ExternalInput")
    donesc = nc.dram_tensor("onesc", [128, 1], F32, kind="ExternalInput")
    drwkv = nc.dram_tensor("rwkv_s", [BS, CT, 128, SEQ], BF16)
    dx1 = nc.dram_tensor("x1_s", [BS, SEQ, D], BF16)
    dhq = nc.dram_tensor("hq_s", [BS, SEQ // 128, 128, HID], BF16)
    dsc = nc.dram_tensor("scal_s", [1, 8], F32)

    p = Prog(nc)
    E, DMA = p.emit, p.dma
    cx = contextlib.ExitStack()
    ar = Arena(nc)

    cols = ar.alloc("cols", [128, 64], F32)
    ccon = ar.alloc("ccon", [128, 24], F32)
    gnsc = ar.alloc("gnsc", [128, 32], F32)
    gnsh = ar.alloc("gnsh", [128, 32], F32)
    n1rep = ar.alloc("n1rep", [128, D], BF16)
    asbc = ar.alloc("asbc", [128, 2], F32)
    idf = ar.alloc("idf", [128, 128], F32)
    idb = ar.alloc("idb", [128, 128], BF16)
    hmask = ar.alloc("hmask", [128, 2], F32)
    hexp = ar.alloc("hexp", [2, 128], F32)
    ones1 = ar.alloc("ones1", [1, 128], F32)
    onesc = ar.alloc("onesc", [128, 1], F32)
    stS = ar.alloc("stS", [128, CT * BS * NCH], F32)
    stQ = ar.alloc("stQ", [128, CT * BS * NCH], F32)
    carry = ar.alloc("carry", [128, CT * BS * 4], F32)
    wsc = ar.alloc("wsc", [128, 4], F32)
    tiny = ar.alloc("tiny", [128, 32], F32)
    ffc = ar.alloc("ffc", [128, 4], F32)
    kq = ar.alloc("kq", [128, 1], F32)
    kn = ar.alloc("kn", [128, 1], F32)
    c05 = ar.alloc("c05", [128, 1], F32)
    cm05 = ar.alloc("cm05", [128, 1], F32)
    RES_END = ar.mark()
    wTr = ar.alloc("wTr", [128, DT * D], BF16)
    wTk = ar.alloc("wTk", [128, DT * D], BF16)
    wTv = ar.alloc("wTv", [128, DT * D], BF16)
    woT = ar.alloc("woT", [128, CT * D], BF16)
    n1tmp = ar.alloc("n1tmp", [1, D], F32)
    REGA_END = ar.mark()

    pA = cx.enter_context(nc.psum_tensor("pA", [128, 1024], F32))
    pB = cx.enter_context(nc.psum_tensor("pB", [128, 1024], F32))
    pC = cx.enter_context(nc.psum_tensor("pC", [128, 512], F32))
    pT = cx.enter_context(nc.psum_tensor("pT", [128, 512], F32))
    pTB = cx.enter_context(nc.psum_tensor("pTB", [128, 512], BF16))
    pTB2 = cx.enter_context(nc.psum_tensor("pTB2", [128, 512], BF16))
    KpA = [("pA", 0), ("pA", 1)]
    KpB = [("pB", 0), ("pB", 1)]

    def stS_col(ct, s, ci):
        c = ct * BS * NCH + s * NCH + ci
        return stS[:, c:c + 1]

    def stQ_col(ct, s, ci):
        c = ct * BS * NCH + s * NCH + ci
        return stQ[:, c:c + 1]

    DMA("sync", lambda e: e.dma_start(cols[:, :], dcols[:, :]), writes=[cols])
    DMA("sync", lambda e: e.dma_start(idf[:, :], didf[:, :]), writes=[idf])
    DMA("sync", lambda e: e.dma_start(idb[:, :], didb[:, :]), writes=[idb])
    DMA("sync", lambda e: e.dma_start(hmask[:, :], dhmask[:, :]), writes=[hmask])
    DMA("sync", lambda e: e.dma_start(hexp[:, :], dhexp[:, :]), writes=[hexp])
    DMA("sync", lambda e: e.dma_start(ones1[:, :], dones[:, :]), writes=[ones1])
    DMA("sync", lambda e: e.dma_start(onesc[:, :], donesc[:, :]), writes=[onesc])
    DMA("sync", lambda e: e.dma_start(asbc[:, 0:1], dasb[0:1, 0:1].broadcast_to([128, 1])), writes=[asbc])
    DMA("sync", lambda e: e.dma_start(asbc[:, 1:2], dasb[1:2, 0:1].broadcast_to([128, 1])), writes=[asbc])
    DMA("sync", lambda e: e.dma_start(n1tmp[:, :], dn1row[:, :]), writes=[n1tmp])

    E("vector", lambda e: e.memset(kq[:, :], EPS_Q), writes=[kq])
    E("vector", lambda e: e.memset(kn[:, :], EPS_NORM), writes=[kn])
    E("vector", lambda e: e.memset(c05[:, :], 0.5), writes=[c05])
    E("vector", lambda e: e.memset(cm05[:, :], -0.5), writes=[cm05])
    E("vector", lambda e: e.tensor_tensor(ffc[:, 0:1], asbc[:, 0:1], asbc[:, 0:1], ALU.mult),
      reads=[asbc], writes=[ffc])
    E("vector", lambda e: e.tensor_single_scalar(ffc[:, 0:1], ffc[:, 0:1], 1.0 / HID, ALU.mult),
      reads=[ffc], writes=[ffc])
    E("vector", lambda e: e.tensor_tensor(ffc[:, 1:2], asbc[:, 0:1], asbc[:, 1:2], ALU.mult),
      reads=[asbc], writes=[ffc])
    E("vector", lambda e: e.tensor_single_scalar(ffc[:, 1:2], ffc[:, 1:2], 2.0 / HID, ALU.mult),
      reads=[ffc], writes=[ffc])
    E("vector", lambda e: e.tensor_tensor(ffc[:, 2:3], asbc[:, 1:2], asbc[:, 1:2], ALU.mult),
      reads=[asbc], writes=[ffc])

    E("scalar", lambda e: e.activation(ccon[:, 16:24], cols[:, 0:8], AF.Exp),
      reads=[cols], writes=[ccon])
    E("scalar", lambda e: e.activation(ccon[:, 0:8], ccon[:, 16:24], AF.Exp, scale=-1.0),
      reads=[ccon], writes=[ccon])
    E("scalar", lambda e: e.activation(ccon[:, 8:16], cols[:, 8:16], AF.Exp),
      reads=[cols], writes=[ccon])

    def Ew_col(ct):
        return ccon[:, ct:ct + 1]

    def Eu_col(ct):
        return ccon[:, 8 + ct:8 + ct + 1]

    def tm_col(m, ct):
        return cols[:, (2 + m) * 8 + ct:(2 + m) * 8 + ct + 1]

    def gam_col(ct):
        return cols[:, 40 + ct:40 + ct + 1]

    def bet_col(ct):
        return cols[:, 48 + ct:48 + ct + 1]

    for h in range(2):
        E("tensor", lambda e, h=h: e.matmul(pT[:, :512], ones1[:, :],
                                            n1tmp[:, h * 512:(h + 1) * 512], start=True, stop=True),
          reads=[ones1, n1tmp], writes=[pT])
        E("vector", lambda e, h=h: e.tensor_copy(n1rep[:, h * 512:(h + 1) * 512], pT[:, :512]),
          reads=[pT], writes=[n1rep])

    ps_mark = ar.mark()
    ldAb = [ar.alloc(f"ldA{i}", [128, D], F32) for i in range(2)]
    u1pb = [ar.alloc(f"u1p{i}", [128, HID], F32) for i in range(2)]
    wabs = ar.alloc("wabs", [128, 4], F32)
    wabt = ar.alloc("wabt", [128, 2], F32)

    def evac(engname, fn, reads, writes):
        E(engname, fn, reads=reads, writes=writes)

    for wi, (dw, wT) in enumerate([(dWr, wTr), (dWk, wTk), (dWv, wTv), (dWo, woT)]):
        for s in range(8):
            ldA = ldAb[s % 2]
            DMA("sync", lambda e, dw=dw, s=s, ldA=ldA: e.dma_start(ldA[:, :D], dw[s * 128:(s + 1) * 128, :]),
                writes=[ldA])
            for dt in range(DT):
                sl = dt % 4
                E("tensor", lambda e, dt=dt, sl=sl, ldA=ldA: e.transpose(
                    pT[:, sl * 128:(sl + 1) * 128], ldA[:, dt * 128:(dt + 1) * 128], idf[:, :]),
                    reads=[ldA, idf], writes=[pT])
                eng = "vector" if dt % 2 == 0 else "scalar"
                dst = wT[:, dt * D + s * 128:dt * D + (s + 1) * 128]
                if eng == "vector":
                    E(eng, lambda e, dst=dst, sl=sl: e.tensor_copy(dst, pT[:, sl * 128:(sl + 1) * 128]),
                      reads=[pT], writes=[wT])
                else:
                    E(eng, lambda e, dst=dst, sl=sl: e.copy(dst, pT[:, sl * 128:(sl + 1) * 128]),
                      reads=[pT], writes=[wT])

    for wi, (dw, nslab, width) in enumerate([(dw1, 32, D), (dw2, 8, HID)]):
        for s in range(nslab):
            ldA = u1pb[s % 2]
            tcol = wabt[:, s % 2:s % 2 + 1]
            DMA("sync", lambda e, dw=dw, s=s, width=width, ldA=ldA: e.dma_start(ldA[:, :width], dw[s * 128:(s + 1) * 128, :]),
                writes=[ldA])
            E("vector", lambda e, width=width, ldA=ldA, tcol=tcol: e.tensor_reduce(tcol, ldA[:, :width], AX, ALU.add,
                                                               apply_absolute_value=True),
              reads=[ldA], writes=[wabt])
            if s == 0:
                E("vector", lambda e, wi=wi, tcol=tcol: e.tensor_copy(wabs[:, wi:wi + 1], tcol),
                  reads=[wabt], writes=[wabs])
            else:
                E("vector", lambda e, wi=wi, tcol=tcol: e.tensor_add(wabs[:, wi:wi + 1], wabs[:, wi:wi + 1], tcol),
                  reads=[wabt, wabs], writes=[wabs])
    for wi in range(2):
        E("tensor", lambda e, wi=wi: e.matmul(pT[0:1, 0:1], wabs[:, wi:wi + 1], onesc[:, :],
                                              start=True, stop=True),
          reads=[wabs, onesc], writes=[pT])
        E("vector", lambda e, wi=wi: e.tensor_scalar(tiny[0:1, 16 + wi:17 + wi], pT[0:1, 0:1],
                                                     1.0 / (HID * D), kq[0:1, 0:1], ALU.mult, ALU.max),
          reads=[pT], writes=[tiny])
        E("vector", lambda e, wi=wi: e.reciprocal(tiny[0:1, 18 + wi:19 + wi], tiny[0:1, 16 + wi:17 + wi]),
          reads=[tiny], writes=[tiny])
    DMA("scalar", lambda e: e.dma_start(dsc[0:1, 0:4], tiny[0:1, 16:20]), reads=[tiny], writes=[dsc])
    for j in range(4):
        DMA("sync", lambda e, j=j: e.dma_start(wsc[:, j:j + 1], dsc[0:1, j:j + 1].broadcast_to([128, 1])),
            reads=[dsc], writes=[wsc])

    p.barrier_all()
    ar.reset(ps_mark)

    p.scope = "s1"
    s1_mark = ar.mark()
    xT = [[ar.alloc(f"xT{g}_{dt}", [128, 1 + F], BF16) for dt in range(DT)] for g in range(2)]
    xmT = [[ar.alloc(f"xmT{m}_{dt}", [128, F], BF16) for dt in range(DT)] for m in range(3)]
    xin = [ar.alloc(f"xin{i}", [128, D], F32) for i in range(2)]
    difb = ar.alloc("difb", [128, F], BF16)
    NS1 = 3
    Ge = [ar.alloc(f"Ge{i}", [128, 1 + F], F32) for i in range(NS1)]
    Gte = [ar.alloc(f"Gte{i}", [128, 1 + F], F32) for i in range(NS1)]
    Ne = [ar.alloc(f"Ne{i}", [128, 1 + F], F32) for i in range(NS1)]
    SCR = [{n: ar.alloc(f"{n}{i}", [128, F], F32) for n in
            ["ek", "euk", "dD", "mM", "t1", "tq", "rf", "ts1"]} for i in range(NS1)]
    rwb = [ar.alloc(f"rwb{i}", [128, F], BF16) for i in range(NS1)]
    scrapg = ar.alloc("scrapg", [128, F], F32)
    xlast = ar.alloc("xlast", [128, BS * DT], BF16)
    S1_END = ar.mark()

    E("vector", lambda e: e.memset(carry[:, :], 0.0), writes=[carry])

    for ci in range(NCH):
        for s in range(BS):
            g = (ci * BS + s) % 2
            xTg = xT[g]
            for tt in range(F // 128):
                xi = xin[tt % 2]
                t0 = ci * F + tt * 128
                DMA("sync", lambda e, s=s, t0=t0, xi=xi: e.dma_start(xi[:, :], dx[s, t0:t0 + 128, :]),
                    writes=[xi])
                for dt in range(DT):
                    sl = dt % 4
                    E("tensor", lambda e, xi=xi, dt=dt, sl=sl: e.transpose(
                        pT[:, sl * 128:(sl + 1) * 128], xi[:, dt * 128:(dt + 1) * 128], idf[:, :]),
                        reads=[xi, idf], writes=[pT])
                    dst = xTg[dt][:, 1 + tt * 128:1 + (tt + 1) * 128]
                    E("scalar", lambda e, dst=dst, sl=sl: e.copy(dst, pT[:, sl * 128:(sl + 1) * 128]),
                      reads=[pT], writes=[xTg[dt]])
            for dt in range(DT):
                if ci == 0:
                    E("vector", lambda e, dt=dt, xTg=xTg: e.memset(xTg[dt][:, 0:1], 0.0), writes=[xTg[dt]])
                else:
                    E("gpsimd", lambda e, dt=dt, s=s, xTg=xTg: e.tensor_copy(
                        xTg[dt][:, 0:1], xlast[:, s * DT + dt:s * DT + dt + 1]),
                        reads=[xlast], writes=[xTg[dt]])
                E("gpsimd", lambda e, dt=dt, s=s, xTg=xTg: e.tensor_copy(
                    xlast[:, s * DT + dt:s * DT + dt + 1], xTg[dt][:, F:F + 1]),
                    reads=[xTg[dt]], writes=[xlast])
            for dt in range(DT):
                E("vector", lambda e, dt=dt, xTg=xTg: e.tensor_tensor(
                    difb[:, :], xTg[dt][:, 1:1 + F], xTg[dt][:, 0:F], ALU.subtract),
                    reads=[xTg[dt]], writes=[difb])
                for m in range(3):
                    E("vector", lambda e, dt=dt, m=m, xTg=xTg: e.scalar_tensor_tensor(
                        xmT[m][dt][:, :], difb[:, :], tm_col(m, dt), xTg[dt][:, 0:F],
                        ALU.mult, ALU.add),
                        reads=[difb, cols, xTg[dt]], writes=[xmT[m][dt]])
            for ct in range(CT):
                r2 = ct % NS1
                sc = SCR[r2]
                ge, gte, ne = Ge[r2], Gte[r2], Ne[r2]
                h2i = ct % 2
                pk = pA[:, h2i * 512:h2i * 512 + F]
                pv = pB[:, h2i * 512:h2i * 512 + F]
                pr = pC
                kk, kv, kr = KpA[h2i], KpB[h2i], pC
                for wT, pdst, pkey, mix in [(wTk, pk, kk, 1), (wTv, pv, kv, 2), (wTr, pr, kr, 0)]:
                    for dt in range(DT):
                        E("tensor", lambda e, wT=wT, pdst=pdst, mix=mix, dt=dt, ct=ct:
                          e.matmul(pdst[:, 0:F],
                                   wT[:, dt * D + ct * 128:dt * D + (ct + 1) * 128],
                                   xmT[mix][dt][:, :],
                                   start=(dt == 0), stop=(dt == DT - 1)),
                          reads=[wT, xmT[mix][dt]], writes=[pkey])
                co = (ct * BS + s) * 4
                E("gpsimd", lambda e, ge=ge, co=co: e.tensor_copy(ge[:, 0:1], carry[:, co:co + 1]),
                  reads=[carry], writes=[ge])
                E("gpsimd", lambda e, gte=gte, co=co: e.tensor_copy(gte[:, 0:1], carry[:, co + 1:co + 2]),
                  reads=[carry], writes=[gte])
                E("gpsimd", lambda e, ne=ne, co=co: e.tensor_copy(ne[:, 0:1], carry[:, co + 2:co + 3]),
                  reads=[carry], writes=[ne])
                E("scalar", lambda e, sc=sc, pk=pk: e.activation(sc["ek"][:, :], pk[:, 0:F], AF.Exp),
                  reads=[kk], writes=[sc["ek"]])
                E("scalar", lambda e, sc=sc, ct=ct: e.activation(sc["euk"][:, :], sc["ek"][:, :], AF.Copy, scale=Eu_col(ct)),
                  reads=[sc["ek"], ccon], writes=[sc["euk"]])
                E("scalar", lambda e, sc=sc, pr=pr: e.activation(sc["ts1"][:, :], pr[:, 0:F], AF.Exp, scale=-1.0),
                  reads=[pr], writes=[sc["ts1"]])
                E("scalar", lambda e, sc=sc: e.activation(sc["ts1"][:, :], sc["ts1"][:, :], AF.Ln, bias=1.0),
                  reads=[sc["ts1"]], writes=[sc["ts1"]])
                E("scalar", lambda e, sc=sc: e.activation(sc["ts1"][:, :], sc["ts1"][:, :], AF.Exp, scale=-1.0),
                  reads=[sc["ts1"]], writes=[sc["ts1"]])
                E("vector", lambda e, sc=sc, ge=ge, ct=ct: e.tensor_tensor_scan(
                    ge[:, 1:1 + F], Ew_col(ct).broadcast_to([128, F]), sc["ek"][:, :],
                    ge[:, 0:1], ALU.mult, ALU.add),
                    reads=[ccon, sc["ek"], ge], writes=[ge])
                E("vector", lambda e, sc=sc, ge=ge, ct=ct: e.tensor_single_scalar(
                    sc["dD"][:, :], ge[:, 0:F], Ew_col(ct), ALU.mult),
                    reads=[ge, ccon], writes=[sc["dD"]])
                E("vector", lambda e, sc=sc: e.tensor_tensor(sc["mM"][:, :], sc["dD"][:, :], sc["ek"][:, :], ALU.max),
                  reads=[sc["dD"], sc["ek"]], writes=[sc["mM"]])
                E("scalar", lambda e, sc=sc, ge=ge: e.activation(sc["t1"][:, :], ge[:, 1:1 + F], AF.Ln),
                  reads=[ge], writes=[sc["t1"]])
                E("scalar", lambda e, sc=sc: e.activation(sc["t1"][:, :], sc["t1"][:, :], AF.Exp, scale=-1.0),
                  reads=[sc["t1"]], writes=[sc["t1"]])
                E("vector", lambda e, sc=sc: e.tensor_tensor(sc["tq"][:, :], sc["mM"][:, :], sc["t1"][:, :], ALU.mult),
                  reads=[sc["mM"], sc["t1"]], writes=[sc["tq"]])
                E("vector", lambda e, sc=sc: e.tensor_scalar(sc["tq"][:, :], sc["tq"][:, :], EPS, 1.0, ALU.mult, ALU.add),
                  reads=[sc["tq"]], writes=[sc["tq"]])
                E("vector", lambda e, sc=sc, ct=ct: e.tensor_single_scalar(sc["dD"][:, :], sc["tq"][:, :], Ew_col(ct), ALU.mult),
                  reads=[sc["tq"], ccon], writes=[sc["dD"]])
                E("vector", lambda e, sc=sc: e.tensor_tensor(sc["mM"][:, :], sc["tq"][:, :], sc["ek"][:, :], ALU.mult),
                  reads=[sc["tq"], sc["ek"]], writes=[sc["mM"]])
                E("vector", lambda e, sc=sc, gte=gte: e.tensor_tensor_scan(
                    gte[:, 1:1 + F], sc["dD"][:, :], sc["mM"][:, :], gte[:, 0:1], ALU.mult, ALU.add),
                    reads=[sc["dD"], sc["mM"], gte], writes=[gte])
                E("vector", lambda e, sc=sc, gte=gte, ct=ct: e.tensor_single_scalar(
                    sc["dD"][:, :], gte[:, 0:F], Ew_col(ct), ALU.mult),
                    reads=[gte, ccon], writes=[sc["dD"]])
                E("vector", lambda e, sc=sc: e.tensor_tensor(sc["mM"][:, :], sc["dD"][:, :], sc["ek"][:, :], ALU.max),
                  reads=[sc["dD"], sc["ek"]], writes=[sc["mM"]])
                E("vector", lambda e, sc=sc: e.tensor_tensor(sc["tq"][:, :], sc["dD"][:, :], sc["ek"][:, :], ALU.add),
                  reads=[sc["dD"], sc["ek"]], writes=[sc["tq"]])
                E("scalar", lambda e, sc=sc: e.activation(sc["t1"][:, :], sc["mM"][:, :], AF.Ln),
                  reads=[sc["mM"]], writes=[sc["t1"]])
                E("scalar", lambda e, sc=sc: e.activation(sc["t1"][:, :], sc["t1"][:, :], AF.Exp, scale=-1.0),
                  reads=[sc["t1"]], writes=[sc["t1"]])
                E("vector", lambda e, sc=sc: e.tensor_tensor(sc["rf"][:, :], sc["tq"][:, :], sc["t1"][:, :], ALU.mult),
                  reads=[sc["tq"], sc["t1"]], writes=[sc["rf"]])
                E("vector", lambda e, sc=sc, ct=ct: e.tensor_scalar(sc["dD"][:, :], sc["rf"][:, :], EPS, Ew_col(ct), ALU.add, ALU.mult),
                  reads=[sc["rf"], ccon], writes=[sc["dD"]])
                E("vector", lambda e, sc=sc, pv=pv: e.tensor_tensor(sc["mM"][:, :], sc["ek"][:, :], pv[:, 0:F], ALU.mult),
                  reads=[sc["ek"], kv], writes=[sc["mM"]])
                E("vector", lambda e, sc=sc: e.scalar_tensor_tensor(sc["tq"][:, :], sc["rf"][:, :], EPS, sc["mM"][:, :], ALU.add, ALU.mult),
                  reads=[sc["rf"], sc["mM"]], writes=[sc["tq"]])
                E("vector", lambda e, sc=sc, ne=ne: e.tensor_tensor_scan(
                    ne[:, 1:1 + F], sc["dD"][:, :], sc["tq"][:, :], ne[:, 0:1], ALU.mult, ALU.add),
                    reads=[sc["dD"], sc["tq"], ne], writes=[ne])
                E("vector", lambda e, sc=sc, pv=pv: e.tensor_tensor(sc["mM"][:, :], sc["euk"][:, :], pv[:, 0:F], ALU.mult),
                  reads=[sc["euk"], kv], writes=[sc["mM"]])
                E("vector", lambda e, sc=sc, ne=ne: e.tensor_tensor(sc["tq"][:, :], ne[:, 0:F], sc["mM"][:, :], ALU.add),
                  reads=[ne, sc["mM"]], writes=[sc["tq"]])
                E("vector", lambda e, sc=sc, gte=gte: e.tensor_tensor(sc["dD"][:, :], gte[:, 0:F], sc["euk"][:, :], ALU.add),
                  reads=[gte, sc["euk"]], writes=[sc["dD"]])
                E("scalar", lambda e, sc=sc: e.activation(sc["t1"][:, :], sc["dD"][:, :], AF.Ln),
                  reads=[sc["dD"]], writes=[sc["t1"]])
                E("scalar", lambda e, sc=sc: e.activation(sc["t1"][:, :], sc["t1"][:, :], AF.Exp, scale=-1.0),
                  reads=[sc["t1"]], writes=[sc["t1"]])
                E("vector", lambda e, sc=sc: e.tensor_tensor(sc["rf"][:, :], sc["tq"][:, :], sc["t1"][:, :], ALU.mult),
                  reads=[sc["tq"], sc["t1"]], writes=[sc["rf"]])
                rw = rwb[r2]
                E("vector", lambda e, sc=sc, rw=rw: e.tensor_tensor(rw[:, :], sc["rf"][:, :], sc["ts1"][:, :], ALU.mult),
                  reads=[sc["rf"], sc["ts1"]], writes=[rw])
                E("gpsimd", lambda e, ge=ge, co=co: e.tensor_copy(carry[:, co:co + 1], ge[:, F:F + 1]),
                  reads=[ge], writes=[carry])
                E("gpsimd", lambda e, gte=gte, co=co: e.tensor_copy(carry[:, co + 1:co + 2], gte[:, F:F + 1]),
                  reads=[gte], writes=[carry])
                E("gpsimd", lambda e, ne=ne, co=co: e.tensor_copy(carry[:, co + 2:co + 3], ne[:, F:F + 1]),
                  reads=[ne], writes=[carry])
                E("vector", lambda e, rw=rw, ct=ct, s=s, ci=ci: e.tensor_reduce(
                    stS_col(ct, s, ci), rw[:, :], AX, ALU.add),
                    reads=[rw], writes=[stS])
                E("scalar", lambda e, rw=rw, ct=ct, s=s, ci=ci: e.activation(
                    scrapg[:, :], rw[:, :], AF.Square, accum_out=stQ_col(ct, s, ci)),
                    reads=[rw], writes=[scrapg, stQ])
                DMA("scalar", lambda e, rw=rw, s=s, ct=ct, ci=ci: e.dma_start(
                    drwkv[s, ct, :, ci * F:(ci + 1) * F], rw[:, :]),
                    reads=[rw], writes=[drwkv])

    p.barrier_all()
    ar.reset(s1_mark)

    p.scope = "s2a"
    s2a_mark = ar.mark()
    rwin = [ar.alloc(f"rwin{i}", [128, CT * 512], BF16) for i in range(2)]
    nrm = [ar.alloc(f"nrm{i}", [128, CT * 512], BF16) for i in range(2)]
    x2in = [ar.alloc(f"x2in{i}", [128, D], F32) for i in range(2)]
    h1t = [ar.alloc(f"h1t{i}", [128, D], F32) for i in range(2)]
    x1bt = [ar.alloc(f"x1b{i}", [128, D], BF16) for i in range(2)]
    lnstb = [ar.alloc(f"lnst{i}", [128, 8], F32) for i in range(2)]
    scr2b = [ar.alloc(f"scr2{i}", [128, D], F32) for i in range(2)]
    hstat = ar.alloc("hstat", [2, 8], F32)
    S2A_END = ar.mark()

    for ct in range(CT):
        for s in range(BS):
            c0 = ct * BS * NCH + s * NCH
            E("vector", lambda e, c0=c0: e.tensor_reduce(tiny[:, 0:1], stS[:, c0:c0 + NCH], AX, ALU.add),
              reads=[stS], writes=[tiny])
            E("vector", lambda e, c0=c0: e.tensor_reduce(tiny[:, 1:2], stQ[:, c0:c0 + NCH], AX, ALU.add),
              reads=[stQ], writes=[tiny])
            E("tensor", lambda e: e.matmul(pT[0:2, 0:2], hmask[:, :], tiny[:, 0:2],
                                           start=True, stop=True),
              reads=[hmask, tiny], writes=[pT])
            E("vector", lambda e: e.tensor_single_scalar(hstat[0:2, 0:2], pT[0:2, 0:2], 1.0 / GN_N, ALU.mult),
              reads=[pT], writes=[hstat])
            E("vector", lambda e: e.tensor_tensor(hstat[0:2, 2:3], hstat[0:2, 0:1], hstat[0:2, 0:1], ALU.mult),
              reads=[hstat], writes=[hstat])
            E("vector", lambda e: e.tensor_tensor(hstat[0:2, 2:3], hstat[0:2, 1:2], hstat[0:2, 2:3], ALU.subtract),
              reads=[hstat], writes=[hstat])
            E("scalar", lambda e: e.activation(hstat[0:2, 3:4], hstat[0:2, 2:3], AF.Ln, bias=kq[0:2, 0:1]),
              reads=[hstat, kq], writes=[hstat])
            E("scalar", lambda e: e.activation(hstat[0:2, 3:4], hstat[0:2, 3:4], AF.Exp, scale=-0.5),
              reads=[hstat], writes=[hstat])
            E("tensor", lambda e: e.matmul(pT[:, 4:5], hexp[:, :], hstat[0:2, 3:4], start=True, stop=True),
              reads=[hexp, hstat], writes=[pT])
            E("tensor", lambda e: e.matmul(pT[:, 5:6], hexp[:, :], hstat[0:2, 0:1], start=True, stop=True),
              reads=[hexp, hstat], writes=[pT])
            col = ct * BS + s
            E("vector", lambda e, ct=ct, col=col: e.tensor_single_scalar(
                gnsc[:, col:col + 1], pT[:, 4:5], gam_col(ct), ALU.mult),
                reads=[pT, cols], writes=[gnsc])
            E("vector", lambda e, col=col: e.tensor_tensor(
                tiny[:, 2:3], pT[:, 5:6], gnsc[:, col:col + 1], ALU.mult),
                reads=[pT, gnsc], writes=[tiny])
            E("vector", lambda e, ct=ct, col=col: e.tensor_scalar(
                gnsh[:, col:col + 1], tiny[:, 2:3], -1.0, bet_col(ct), ALU.mult, ALU.add),
                reads=[tiny, cols], writes=[gnsh])

    for s in range(BS):
        for gq in range(SEQ // 512):
            g = (s * (SEQ // 512) + gq) % 2
            rw, nm = rwin[g], nrm[g]
            for ct in range(CT):
                DMA("scalar", lambda e, rw=rw, ct=ct, s=s, gq=gq: e.dma_start(
                    rw[:, ct * 512:(ct + 1) * 512], drwkv[s, ct, :, gq * 512:(gq + 1) * 512]),
                    reads=[drwkv], writes=[rw])
                col = ct * BS + s
                E("vector", lambda e, rw=rw, nm=nm, ct=ct, col=col: e.tensor_scalar(
                    nm[:, ct * 512:(ct + 1) * 512], rw[:, ct * 512:(ct + 1) * 512],
                    gnsc[:, col:col + 1], gnsh[:, col:col + 1], ALU.mult, ALU.add),
                    reads=[rw, gnsc, gnsh], writes=[nm])
            def s2a_p1(tt):
                xi = x2in[tt % 2]
                h1 = h1t[tt % 2]
                lnst = lnstb[tt % 2]
                scr2 = scr2b[tt % 2]
                t0 = gq * 512 + tt * 128
                DMA("sync", lambda e, xi=xi, s=s, t0=t0: e.dma_start(xi[:, :], dx[s, t0:t0 + 128, :]),
                    writes=[xi])
                pat = pA if tt % 2 == 0 else pB
                for ct in range(CT):
                    for hh in range(2):
                        E("tensor", lambda e, nm=nm, ct=ct, tt=tt, hh=hh, pat=pat: e.matmul(
                            pat[:, hh * 512:(hh + 1) * 512],
                            nm[:, ct * 512 + tt * 128:ct * 512 + (tt + 1) * 128],
                            woT[:, ct * D + hh * 512:ct * D + (hh + 1) * 512],
                            start=(ct == 0), stop=(ct == CT - 1)),
                            reads=[nm, woT], writes=[pat])
                E("vector", lambda e, xi=xi, h1=h1, pat=pat: e.tensor_tensor(h1[:, :], xi[:, :], pat[:, :], ALU.add),
                  reads=[xi, pat], writes=[h1])
                E("vector", lambda e, h1=h1, lnst=lnst: e.tensor_reduce(lnst[:, 0:1], h1[:, :], AX, ALU.add),
                  reads=[h1], writes=[lnst])
                E("scalar", lambda e, h1=h1, scr2=scr2, lnst=lnst: e.activation(scr2[:, :], h1[:, :], AF.Square, accum_out=lnst[:, 1:2]),
                  reads=[h1], writes=[scr2, lnst])
                E("vector", lambda e, lnst=lnst: e.tensor_single_scalar(lnst[:, 2:3], lnst[:, 0:1], 1.0 / D, ALU.mult),
                  reads=[lnst], writes=[lnst])
                E("vector", lambda e, lnst=lnst: e.tensor_tensor(lnst[:, 3:4], lnst[:, 2:3], lnst[:, 2:3], ALU.mult),
                  reads=[lnst], writes=[lnst])
                E("vector", lambda e, lnst=lnst: e.tensor_single_scalar(lnst[:, 4:5], lnst[:, 1:2], 1.0 / D, ALU.mult),
                  reads=[lnst], writes=[lnst])
                E("vector", lambda e, lnst=lnst: e.tensor_tensor(lnst[:, 4:5], lnst[:, 4:5], lnst[:, 3:4], ALU.subtract),
                  reads=[lnst], writes=[lnst])
                E("scalar", lambda e, lnst=lnst: e.activation(lnst[:, 5:6], lnst[:, 4:5], AF.Ln, bias=kq[:, 0:1]),
                  reads=[lnst, kq], writes=[lnst])
                E("scalar", lambda e, lnst=lnst: e.activation(lnst[:, 5:6], lnst[:, 5:6], AF.Exp, scale=-0.5),
                  reads=[lnst], writes=[lnst])

            def s2a_p2(tt):
                h1 = h1t[tt % 2]
                xb = x1bt[tt % 2]
                lnst = lnstb[tt % 2]
                t0 = gq * 512 + tt * 128
                E("vector", lambda e, lnst=lnst: e.tensor_single_scalar(lnst[:, 6:7], lnst[:, 2:3], -1.0, ALU.mult),
                  reads=[lnst], writes=[lnst])
                E("vector", lambda e, h1=h1, xb=xb, lnst=lnst: e.tensor_scalar(
                    xb[:, :], h1[:, :], lnst[:, 6:7], lnst[:, 5:6], ALU.add, ALU.mult),
                    reads=[h1, lnst], writes=[xb])
                DMA("scalar", lambda e, xb=xb, s=s, t0=t0: e.dma_start(dx1[s, t0:t0 + 128, :], xb[:, :]),
                    reads=[xb], writes=[dx1])

            for tt in range(5):
                if tt < 4:
                    s2a_p1(tt)
                if tt >= 1:
                    s2a_p2(tt - 1)

    p.barrier_all()
    ar.reset(RES_END)

    p.scope = "s2b"
    s2b_mark = ar.mark()
    hscst = ar.alloc("hscst", [128, 64], F32)
    s2b2_mark = ar.mark()
    w1T = ar.alloc("w1T", [128, DT * HID], BF16)
    n2rep = ar.alloc("n2rep", [128, HID], BF16)
    n2tmp = ar.alloc("n2tmp", [1, 512], F32)
    HH = HID // 2
    Txb = [ar.alloc(f"p1x1_{i}", [128, D], BF16) for i in range(2)]
    Txnb = [ar.alloc(f"p1xn_{i}", [128, D], BF16) for i in range(2)]
    Tyf = [ar.alloc(f"p1yf_{i}", [128, D], BF16) for i in range(2)]
    TxqT = [ar.alloc(f"p1xq_{i}", [128, D], BF16) for i in range(3)]
    Tyi = [ar.alloc(f"p1yi_{i}", [128, D], I32) for i in range(4)]
    Thyq = [ar.alloc(f"p1hq_{i}", [128, HH], BF16) for i in range(2)]
    Thyf = [ar.alloc(f"p1hf_{i}", [128, HH], BF16) for i in range(3)]
    Thu1 = [ar.alloc(f"p1hu_{i}", [128, HH], BF16) for i in range(3)]
    Threl = [ar.alloc(f"p1hr_{i}", [128, HH], F32) for i in range(2)]
    ThrelB = [ar.alloc(f"p1hB_{i}", [128, HH], F32) for i in range(2)]
    ThqS = [[ar.alloc(f"p1hs_{i}_{k}", [128, 512], BF16) for k in range(2)] for i in range(2)]
    Tq = [ar.alloc(f"p1q_{i}", [128, 32], F32) for i in range(4)]
    Tdif = [ar.alloc(f"p1df_{i}", [128, HH], BF16) for i in range(2)]
    asn2 = ar.alloc("asn2", [128, HID], BF16)
    dumS = ar.alloc("dumS", [128, 2048], BF16)
    S2B1_END = ar.mark()

    wldb = [Threl[0][:, 0:D], Threl[0][:, D:2 * D]]
    wu1b = [Thyq[0][:, 0:D], Thyq[0][:, D:2 * D]]
    wu2b = [Thyf[0][:, 0:D], Thyf[0][:, D:2 * D]]
    wtnb = [Thu1[0][:, 0:D], Thu1[0][:, D:2 * D]]
    wkeys = [Threl[0], Thyq[0], Thyf[0], Thu1[0]]
    for s in range(32):
        wld, wu1, wu2, wtn = wldb[s % 2], wu1b[s % 2], wu2b[s % 2], wtnb[s % 2]
        DMA("sync", lambda e, s=s, wld=wld: e.dma_start(wld[:, :], dw1[s * 128:(s + 1) * 128, :]),
            writes=[wkeys[0]])
        E("scalar", lambda e, wld=wld: e.activation(wld[:, :], wld[:, :], AF.Copy, scale=wsc[:, 2:3]),
          reads=[wkeys[0], wsc], writes=[wkeys[0]])
        E("vector", lambda e, wld=wld, wu1=wu1: e.tensor_single_scalar(wu1[:, :], wld[:, :], 0.5, ALU.is_gt),
          reads=[wkeys[0]], writes=[wkeys[1]])
        E("vector", lambda e, wld=wld, wu2=wu2: e.tensor_single_scalar(wu2[:, :], wld[:, :], -0.5, ALU.is_lt),
          reads=[wkeys[0]], writes=[wkeys[2]])
        E("vector", lambda e, wu1=wu1, wu2=wu2, wtn=wtn: e.tensor_tensor(wtn[:, :], wu1[:, :], wu2[:, :], ALU.subtract),
          reads=[wkeys[1], wkeys[2]], writes=[wkeys[3]])
        for bb in range(8):
            sl = bb % 4
            ptb = pTB if (bb // 4) % 2 == 0 else pTB2
            E("tensor", lambda e, bb=bb, sl=sl, wtn=wtn, ptb=ptb: e.transpose(
                ptb[:, sl * 128:(sl + 1) * 128], wtn[:, bb * 128:(bb + 1) * 128], idb[:, :]),
                reads=[wkeys[3], idb], writes=[ptb])
            if bb % 4 == 3:
                eng = "vector" if (bb // 4) % 2 == 0 else "scalar"
                for k in range(4):
                    dstk = w1T[:, (bb - 3 + k) * HID + s * 128:(bb - 3 + k) * HID + (s + 1) * 128]
                    if eng == "vector":
                        E(eng, lambda e, dstk=dstk, k=k, ptb=ptb: e.tensor_copy(dstk, ptb[:, k * 128:(k + 1) * 128]),
                          reads=[ptb], writes=[w1T])
                    else:
                        E(eng, lambda e, dstk=dstk, k=k, ptb=ptb: e.copy(dstk, ptb[:, k * 128:(k + 1) * 128]),
                          reads=[ptb], writes=[w1T])
    for h in range(HID // 512):
        DMA("sync", lambda e, h=h: e.dma_start(n2tmp[:, :], dn2row[:, h * 512:(h + 1) * 512]),
            writes=[n2tmp])
        E("tensor", lambda e, h=h: e.matmul(pT[:, :512], ones1[:, :], n2tmp[:, :], start=True, stop=True),
          reads=[ones1, n2tmp], writes=[pT])
        E("vector", lambda e, h=h: e.tensor_copy(n2rep[:, h * 512:(h + 1) * 512], pT[:, :512]),
          reads=[pT], writes=[n2rep])
    E("vector", lambda e: e.tensor_single_scalar(asn2[:, :], n2rep[:, :], asbc[:, 0:1], ALU.mult),
      reads=[n2rep, asbc], writes=[asn2])
    E("vector", lambda e: e.reciprocal(ffc[:, 3:4], asbc[:, 0:1]), reads=[asbc], writes=[ffc])
    E("vector", lambda e: e.tensor_tensor(ffc[:, 3:4], ffc[:, 3:4], asbc[:, 1:2], ALU.mult),
      reads=[ffc, asbc], writes=[ffc])

    NTC = SEQ // 128
    NUN = BS * NTC
    pCb = pC[:, :].bitcast(BF16)

    def p1_Apre(u):
        s, t0 = u // NTC, (u % NTC) * 128
        xb = Txb[u % 2]
        DMA("sync", lambda e, xb=xb, s=s, t0=t0: e.dma_start(xb[:, :], dx1[s, t0:t0 + 128, :]),
            reads=[dx1], writes=[xb])

    def p1_A0(u):
        xb, q = Txb[u % 2], Tq[u % 4]
        E("scalar", lambda e, xb=xb, q=q: e.activation(dumS[:, 0:D], xb[:, :], AF.Square, accum_out=q[:, 0:1]),
          reads=[xb], writes=[dumS, q])
        E("scalar", lambda e, q=q: e.activation(q[:, 1:2], q[:, 0:1], AF.Ln, bias=kn[:, 0:1], scale=1.0 / D),
          reads=[q, kn], writes=[q])
        E("scalar", lambda e, q=q: e.activation(q[:, 1:2], q[:, 1:2], AF.Exp, scale=-0.5),
          reads=[q], writes=[q])

    def p1_Av(u):
        xb, xnb, yf, yi, q = Txb[u % 2], Txnb[u % 2], Tyf[u % 2], Tyi[u % 4], Tq[u % 4]
        E("vector", lambda e, xb=xb, xnb=xnb: e.tensor_tensor(xnb[:, :], xb[:, :], n1rep[:, :], ALU.mult),
          reads=[xb, n1rep], writes=[xnb])
        E("vector", lambda e, xnb=xnb, q=q: e.tensor_reduce(q[:, 2:3], xnb[:, :], AX, ALU.max, apply_absolute_value=True),
          reads=[xnb], writes=[q])
        E("vector", lambda e, q=q: e.tensor_scalar(q[:, 3:4], q[:, 2:3], q[:, 1:2], kq[:, 0:1], ALU.mult, ALU.max),
          reads=[q, kq], writes=[q])
        E("vector", lambda e, q=q: e.reciprocal(q[:, 4:5], q[:, 3:4]), reads=[q], writes=[q])
        E("vector", lambda e, q=q: e.tensor_scalar(q[:, 6:7], q[:, 4:5], 127.0, q[:, 1:2], ALU.mult, ALU.mult),
          reads=[q], writes=[q])
        E("vector", lambda e, q=q: e.tensor_scalar(q[:, 7:8], q[:, 3:4], 1.0 / 127.0, wsc[:, 0:1], ALU.mult, ALU.mult),
          reads=[q, wsc], writes=[q])
        E("vector", lambda e, xnb=xnb, q=q: e.tensor_single_scalar(xnb[:, :], xnb[:, :], q[:, 6:7], ALU.mult),
          reads=[xnb, q], writes=[xnb])
        E("vector", lambda e, xnb=xnb, yi=yi: e.tensor_copy(yi[:, :], xnb[:, :]), reads=[xnb], writes=[yi])
        E("vector", lambda e, yi=yi, yf=yf: e.tensor_copy(yf[:, :], yi[:, :]), reads=[yi], writes=[yf])
        E("vector", lambda e, xnb=xnb, yf=yf: e.tensor_tensor(xnb[:, :], xnb[:, :], yf[:, :], ALU.subtract),
          reads=[xnb, yf], writes=[xnb])
        E("vector", lambda e, xnb=xnb, yf=yf: e.scalar_tensor_tensor(
            yf[:, :], xnb[:, :], c05[:, 0:1], yf[:, :], ALU.is_gt, ALU.add),
          reads=[xnb, c05, yf], writes=[yf])
        E("vector", lambda e, xnb=xnb, yf=yf: e.scalar_tensor_tensor(
            yf[:, :], xnb[:, :], cm05[:, 0:1], yf[:, :], ALU.is_ge, ALU.add),
          reads=[xnb, cm05, yf], writes=[yf])
        E("vector", lambda e, yf=yf: e.tensor_single_scalar(yf[:, :], yf[:, :], -1.0, ALU.add),
          reads=[yf], writes=[yf])

    pTbc = pT[:, :].bitcast(BF16)

    def p1_Atr(u):
        yf, xqT = Tyf[u % 2], TxqT[u % 3]
        for bb in range(8):
            sl = bb % 4
            if bb < 4:
                E("tensor", lambda e, bb=bb, sl=sl, yf=yf: e.transpose(
                    pTB[:, sl * 128:(sl + 1) * 128], yf[:, bb * 128:(bb + 1) * 128], idb[:, :]),
                    reads=[yf, idb], writes=[pTB])
            else:
                E("tensor", lambda e, bb=bb, sl=sl, yf=yf: e.transpose(
                    pTbc[:, sl * 128:(sl + 1) * 128], yf[:, bb * 128:(bb + 1) * 128], idb[:, :]),
                    reads=[yf, idb], writes=[pT])
        E("vector", lambda e, xqT=xqT: e.tensor_copy(xqT[:, 0:512], pTB[:, 0:512]),
          reads=[pTB], writes=[xqT])
        E("vector", lambda e, xqT=xqT: e.tensor_copy(xqT[:, 512:1024], pTbc[:, 0:512]),
          reads=[pT], writes=[xqT])

    def p1_Bmm(u):
        xqT, q = TxqT[u % 3], Tq[u % 4]
        for jg in range(2):
            hdst = Threl[u % 2] if jg == 0 else ThrelB[u % 2]
            for dt in range(DT):
                for hh in range(4):
                    ph = pA if hh < 2 else pB
                    ho = (hh % 2) * 512
                    E("tensor", lambda e, dt=dt, jg=jg, hh=hh, ph=ph, ho=ho, xqT=xqT: e.matmul(
                        ph[:, ho:ho + 512],
                        xqT[:, dt * 128:(dt + 1) * 128],
                        w1T[:, dt * HID + jg * 2048 + hh * 512:dt * HID + jg * 2048 + (hh + 1) * 512],
                        start=(dt == 0), stop=(dt == DT - 1)),
                        reads=[xqT, w1T], writes=[ph])
            for piece, ph in [(0, pA), (1, pB)]:
                off = piece * 1024
                E("scalar", lambda e, ph=ph, off=off, hdst=hdst, q=q: e.activation(
                    hdst[:, off:off + 1024], ph[:, :], AF.Relu, scale=q[:, 7:8]),
                  reads=[ph, q], writes=[hdst])
            E("scalar", lambda e, jg=jg, hdst=hdst, q=q: e.activation(hdst[:, :], hdst[:, :], AF.Square,
                                                                      accum_out=q[:, 8 + jg:9 + jg]),
              reads=[hdst], writes=[hdst, q])
            E("scalar", lambda e, jg=jg, hdst=hdst, q=q: e.activation(dumS[:, 0:2048], hdst[:, :], AF.Square,
                                                                      accum_out=q[:, 10 + jg:11 + jg]),
              reads=[hdst], writes=[dumS, q])

    def p1_Bv(u):
        hrel, hrelB, hyf, hu1, q = Threl[u % 2], ThrelB[u % 2], Thyf[u % 3], Thu1[u % 3], Tq[u % 4]
        E("vector", lambda e, hrel=hrel, hyf=hyf: e.scalar_tensor_tensor(
            hyf[:, :], hrel[:, :], ffc[:, 3:4], asn2[:, 0:HH], ALU.add, ALU.mult),
          reads=[hrel, ffc, asn2], writes=[hyf])
        E("vector", lambda e, hrelB=hrelB, hu1=hu1: e.scalar_tensor_tensor(
            hu1[:, :], hrelB[:, :], ffc[:, 3:4], asn2[:, HH:HID], ALU.add, ALU.mult),
          reads=[hrelB, ffc, asn2], writes=[hu1])
        E("vector", lambda e, q=q: e.tensor_tensor(q[:, 16:17], q[:, 8:9], q[:, 9:10], ALU.add),
          reads=[q], writes=[q])
        E("vector", lambda e, q=q: e.tensor_tensor(q[:, 17:18], q[:, 10:11], q[:, 11:12], ALU.add),
          reads=[q], writes=[q])
        E("vector", lambda e, q=q: e.tensor_scalar(q[:, 20:21], q[:, 17:18], ffc[:, 0:1], None, ALU.mult),
          reads=[q, ffc], writes=[q])
        E("vector", lambda e, q=q: e.scalar_tensor_tensor(q[:, 20:21], q[:, 16:17], ffc[:, 1:2], q[:, 20:21], ALU.mult, ALU.add),
          reads=[q, ffc], writes=[q])
        E("vector", lambda e, q=q: e.tensor_single_scalar(q[:, 20:21], q[:, 20:21], ffc[:, 2:3], ALU.add),
          reads=[q, ffc], writes=[q])
        E("scalar", lambda e, q=q: e.activation(q[:, 21:22], q[:, 20:21], AF.Ln, bias=kn[:, 0:1]),
          reads=[q, kn], writes=[q])
        E("scalar", lambda e, q=q: e.activation(q[:, 21:22], q[:, 21:22], AF.Exp, scale=-0.5),
          reads=[q], writes=[q])

    def p1_Bq(u):
        hyf, hu1, q = Thyf[u % 3], Thu1[u % 3], Tq[u % 4]
        E("vector", lambda e, hyf=hyf, q=q: e.tensor_reduce(q[:, 14:15], hyf[:, :], AX, ALU.max, apply_absolute_value=True),
          reads=[hyf], writes=[q])
        E("vector", lambda e, hu1=hu1, q=q: e.tensor_reduce(q[:, 15:16], hu1[:, :], AX, ALU.max, apply_absolute_value=True),
          reads=[hu1], writes=[q])
        E("vector", lambda e, q=q: e.tensor_tensor(q[:, 14:15], q[:, 14:15], q[:, 15:16], ALU.max),
          reads=[q], writes=[q])
        E("vector", lambda e, q=q: e.tensor_scalar(q[:, 22:23], q[:, 14:15], q[:, 21:22], kq[:, 0:1], ALU.mult, ALU.max),
          reads=[q, kq], writes=[q])
        E("vector", lambda e, q=q: e.reciprocal(q[:, 23:24], q[:, 22:23]), reads=[q], writes=[q])
        E("vector", lambda e, q=q: e.tensor_scalar(q[:, 25:26], q[:, 23:24], 127.0, q[:, 21:22], ALU.mult, ALU.mult),
          reads=[q], writes=[q])
        E("vector", lambda e, q=q: e.tensor_scalar(q[:, 26:27], q[:, 22:23], 1.0 / 127.0, wsc[:, 1:2], ALU.mult, ALU.mult),
          reads=[q, wsc], writes=[q])
        E("vector", lambda e, q=q, u=u: e.tensor_copy(hscst[:, u:u + 1], q[:, 26:27]),
          reads=[q], writes=[hscst])

    def p1_Cra(u):
        q = Tq[u % 4]
        for jg in range(2):
            rnd = Thyf[u % 3] if jg == 0 else Thu1[u % 3]
            hyq = Thyq[(u + jg) % 2]
            E("vector", lambda e, rnd=rnd, hyq=hyq, q=q: e.tensor_single_scalar(
                hyq[:, :], rnd[:, :], q[:, 25:26], ALU.mult),
              reads=[rnd, q], writes=[hyq])
        for jg in range(2):
            rnd = Thyf[u % 3] if jg == 0 else Thu1[u % 3]
            hyq = Thyq[(u + jg) % 2]
            yi = Tyi[(u + 2 * jg) % 4]
            for qq in range(2):
                E("scalar", lambda e, qq=qq, hyq=hyq, yi=yi: e.copy(yi[:, :], hyq[:, qq * 1024:(qq + 1) * 1024]),
                  reads=[hyq], writes=[yi])
                E("scalar", lambda e, qq=qq, rnd=rnd, yi=yi: e.copy(rnd[:, qq * 1024:(qq + 1) * 1024], yi[:, :]),
                  reads=[yi], writes=[rnd])

    def p1_Crb(u):
        for jg in range(2):
            rnd = Thyf[u % 3] if jg == 0 else Thu1[u % 3]
            hyq = Thyq[(u + jg) % 2]
            dhf = Tdif[(u + jg) % 2]
            E("vector", lambda e, hyq=hyq, rnd=rnd, dhf=dhf: e.tensor_tensor(dhf[:, :], hyq[:, :], rnd[:, :], ALU.subtract),
              reads=[hyq, rnd], writes=[dhf])
            E("vector", lambda e, dhf=dhf, rnd=rnd: e.scalar_tensor_tensor(
                rnd[:, :], dhf[:, :], c05[:, 0:1], rnd[:, :], ALU.is_gt, ALU.add),
              reads=[dhf, c05, rnd], writes=[rnd])
            E("vector", lambda e, dhf=dhf, rnd=rnd: e.scalar_tensor_tensor(
                rnd[:, :], dhf[:, :], cm05[:, 0:1], rnd[:, :], ALU.is_ge, ALU.add),
              reads=[dhf, cm05, rnd], writes=[rnd])
            E("vector", lambda e, rnd=rnd: e.tensor_single_scalar(rnd[:, :], rnd[:, :], -1.0, ALU.add),
              reads=[rnd], writes=[rnd])

    KpC = [("pClo",), ("pChi",)]

    def p1_Ct(u):
        s, tc = u // NTC, u % NTC
        hqS = ThqS[u % 2]
        slotsets = [(pTB2, 0, pTB2), (pCb, 0, KpC[0]), (pCb, 512, KpC[1])]
        for jg in range(2):
            hof = jg * HH
            rnd = Thyf[u % 3] if jg == 0 else Thu1[u % 3]
            for bb in range(HH // 128):
                gg = u * 8 + jg * 4 + bb // 4
                base, off, pkey = slotsets[gg % 3]
                sl = bb % 4
                E("tensor", lambda e, bb=bb, sl=sl, rnd=rnd, base=base, off=off: e.transpose(
                    base[:, off + sl * 128:off + (sl + 1) * 128], rnd[:, bb * 128:(bb + 1) * 128], idb[:, :]),
                    reads=[rnd, idb], writes=[pkey])
                if bb % 4 == 3:
                    hq_s = hqS[gg % 2]
                    E("scalar", lambda e, hq_s=hq_s, base=base, off=off: e.copy(hq_s[:, :], base[:, off:off + 512]),
                      reads=[pkey], writes=[hq_s])
                    DMA("scalar", lambda e, hq_s=hq_s, s=s, tc=tc, hof=hof, bb=bb: e.dma_start(
                        dhq[s, tc, :, hof + (bb - 3) * 128:hof + (bb + 1) * 128], hq_s[:, :]),
                        reads=[hq_s], writes=[dhq])

    p1_Apre(0)
    for c in range(NUN + 4):
        if 0 <= c - 1 < NUN:
            p1_Atr(c - 1)
        if c < NUN:
            p1_A0(c)
        if 0 <= c - 3 < NUN:
            p1_Bq(c - 3)
            p1_Cra(c - 3)
        if 0 <= c - 1 < NUN:
            p1_Bmm(c - 1)
        if c < NUN:
            p1_Av(c)
        if 0 <= c - 3 < NUN:
            p1_Crb(c - 3)
        if 0 <= c - 2 < NUN:
            p1_Bv(c - 2)
        if 0 <= c - 4 < NUN:
            p1_Ct(c - 4)
        if c + 1 < NUN:
            p1_Apre(c + 1)

    p.barrier_all()
    ar.reset(s2b2_mark)

    p.scope = "s2c"
    s2c_mark = ar.mark()
    w2T = ar.alloc("w2T", [128, JT * D], BF16)
    P2 = []
    for i in range(3):
        P2.append({
            "hqT": ar.alloc(f"p2hq_{i}", [128, HID], BF16),
            "x1": ar.alloc(f"p2x1_{i}", [128, D], BF16),
            "xf": ar.alloc(f"p2xf_{i}", [128, D], F32),
            "yo": ar.alloc(f"p2yo_{i}", [128, D], F32),
            "sq": ar.alloc(f"p2sq_{i}", [128, D], F32),
            "q": ar.alloc(f"p2q_{i}", [128, 8], F32),
        })
    S2C_END = ar.mark()

    w2ldb = [ar.alloc(f"w2ld{i}", [128, HID], F32) for i in range(2)]
    w2u1b = [ar.alloc(f"w2u1{i}", [128, HID], BF16) for i in range(2)]
    w2tnb = [ar.alloc(f"w2tn{i}", [128, HID], BF16) for i in range(2)]
    for s in range(8):
        wld, wu1, wtn = w2ldb[s % 2], w2u1b[s % 2], w2tnb[s % 2]
        DMA("sync", lambda e, s=s, wld=wld: e.dma_start(wld[:, :], dw2[s * 128:(s + 1) * 128, :]),
            writes=[wld])
        E("scalar", lambda e, wld=wld: e.activation(wld[:, :], wld[:, :], AF.Copy, scale=wsc[:, 3:4]),
          reads=[wld, wsc], writes=[wld])
        E("vector", lambda e, wld=wld, wtn=wtn: e.tensor_single_scalar(wtn[:, :], wld[:, :], 0.5, ALU.is_gt),
          reads=[wld], writes=[wtn])
        E("vector", lambda e, wld=wld, wu1=wu1: e.tensor_single_scalar(wu1[:, :], wld[:, :], -0.5, ALU.is_lt),
          reads=[wld], writes=[wu1])
        E("vector", lambda e, wu1=wu1, wtn=wtn: e.tensor_tensor(wtn[:, :], wtn[:, :], wu1[:, :], ALU.subtract),
          reads=[wu1, wtn], writes=[wtn])
        for bb in range(JT):
            sl = bb % 4
            ptb = pTB if (bb // 4) % 2 == 0 else pTB2
            E("tensor", lambda e, bb=bb, sl=sl, wtn=wtn, ptb=ptb: e.transpose(
                ptb[:, sl * 128:(sl + 1) * 128], wtn[:, bb * 128:(bb + 1) * 128], idb[:, :]),
                reads=[wtn, idb], writes=[ptb])
            if bb % 4 == 3:
                eng = "vector" if (bb // 4) % 2 == 0 else "scalar"
                for k in range(4):
                    dstk = w2T[:, (bb - 3 + k) * D + s * 128:(bb - 3 + k) * D + (s + 1) * 128]
                    if eng == "vector":
                        E(eng, lambda e, dstk=dstk, k=k, ptb=ptb: e.tensor_copy(dstk, ptb[:, k * 128:(k + 1) * 128]),
                          reads=[ptb], writes=[w2T])
                    else:
                        E(eng, lambda e, dstk=dstk, k=k, ptb=ptb: e.copy(dstk, ptb[:, k * 128:(k + 1) * 128]),
                          reads=[ptb], writes=[w2T])
    for s in range(BS):
        for tc in range(NTC):
            t0 = tc * 128
            S = P2[(s * NTC + tc) % 3]
            hqT, xb, xf, yo, sqt, q = S["hqT"], S["x1"], S["xf"], S["yo"], S["sq"], S["q"]
            cidx = s * NTC + tc
            DMA("sync", lambda e, hqT=hqT, s=s, tc=tc: e.dma_start(hqT[:, :], dhq[s, tc, :, :]),
                reads=[dhq], writes=[hqT])
            DMA("sync", lambda e, xb=xb, s=s, t0=t0: e.dma_start(xb[:, :], dx1[s, t0:t0 + 128, :]),
                reads=[dx1], writes=[xb])
            pcy = pA if (s * NTC + tc) % 2 == 0 else pB
            for jt in range(JT):
                for hh in range(2):
                    E("tensor", lambda e, jt=jt, hh=hh, hqT=hqT, pcy=pcy: e.matmul(
                        pcy[:, hh * 512:(hh + 1) * 512],
                        hqT[:, jt * 128:(jt + 1) * 128],
                        w2T[:, jt * D + hh * 512:jt * D + (hh + 1) * 512],
                        start=(jt == 0), stop=(jt == JT - 1)),
                        reads=[hqT, w2T], writes=[pcy])
            E("scalar", lambda e, xb=xb, xf=xf: e.copy(xf[:, :], xb[:, :]), reads=[xb], writes=[xf])
            E("vector", lambda e, yo=yo, cidx=cidx, pcy=pcy: e.tensor_single_scalar(yo[:, :], pcy[:, :], hscst[:, cidx:cidx + 1], ALU.mult),
              reads=[pcy, hscst], writes=[yo])
            E("vector", lambda e, yo=yo, xf=xf: e.tensor_tensor(yo[:, :], yo[:, :], xf[:, :], ALU.add),
              reads=[yo, xf], writes=[yo])
            E("vector", lambda e, yo=yo, q=q: e.tensor_reduce(q[:, 0:1], yo[:, :], AX, ALU.add),
              reads=[yo], writes=[q])
            E("scalar", lambda e, yo=yo, sqt=sqt, q=q: e.activation(sqt[:, :], yo[:, :], AF.Square, accum_out=q[:, 1:2]),
              reads=[yo], writes=[sqt, q])
            E("vector", lambda e, q=q: e.tensor_single_scalar(q[:, 2:3], q[:, 0:1], 1.0 / D, ALU.mult),
              reads=[q], writes=[q])
            E("vector", lambda e, q=q: e.tensor_tensor(q[:, 3:4], q[:, 2:3], q[:, 2:3], ALU.mult),
              reads=[q], writes=[q])
            E("vector", lambda e, q=q: e.tensor_single_scalar(q[:, 4:5], q[:, 1:2], 1.0 / D, ALU.mult),
              reads=[q], writes=[q])
            E("vector", lambda e, q=q: e.tensor_tensor(q[:, 4:5], q[:, 4:5], q[:, 3:4], ALU.subtract),
              reads=[q], writes=[q])
            E("scalar", lambda e, q=q: e.activation(q[:, 5:6], q[:, 4:5], AF.Ln, bias=kq[:, 0:1]),
              reads=[q, kq], writes=[q])
            E("scalar", lambda e, q=q: e.activation(q[:, 5:6], q[:, 5:6], AF.Exp, scale=-0.5),
              reads=[q], writes=[q])
            E("vector", lambda e, q=q: e.tensor_single_scalar(q[:, 6:7], q[:, 2:3], -1.0, ALU.mult),
              reads=[q], writes=[q])
            E("vector", lambda e, yo=yo, q=q: e.tensor_scalar(yo[:, :], yo[:, :], q[:, 6:7], q[:, 5:6], ALU.add, ALU.mult),
              reads=[yo, q], writes=[yo])
            DMA("scalar", lambda e, s=s, t0=t0, yo=yo: e.dma_start(dout[s, t0:t0 + 128, :], yo[:, :]),
                reads=[yo])

    p.final_wait("sync")
    p.build(cx)
    cx.close()
    return nc


def kernel(**inputs):
    x = np.ascontiguousarray(inputs["L0_attn_x"])

    def colmaj(v):
        return np.ascontiguousarray(np.asarray(v).reshape(8, 128).T.astype(np.float32))

    cols = np.concatenate([
        colmaj(inputs["L0_attn_time_decay"].reshape(D)),
        colmaj(inputs["L0_attn_time_first"].reshape(D)),
        colmaj(inputs["L0_attn_time_mix_r"].reshape(D)),
        colmaj(inputs["L0_attn_time_mix_k"].reshape(D)),
        colmaj(inputs["L0_attn_time_mix_v"].reshape(D)),
        colmaj(inputs["L0_attn_ln_gamma"].reshape(D)),
        colmaj(inputs["L0_attn_ln_beta"].reshape(D)),
        np.zeros((128, 8), np.float32)], axis=1)
    hmask = np.zeros((128, 2), np.float32)
    hmask[:64, 0] = 1.0
    hmask[64:, 1] = 1.0
    common = {
        "W_r": np.ascontiguousarray(inputs["L0_attn_W_r"], np.float32),
        "W_k": np.ascontiguousarray(inputs["L0_attn_W_k"], np.float32),
        "W_v": np.ascontiguousarray(inputs["L0_attn_W_v"], np.float32),
        "W_o": np.ascontiguousarray(inputs["L0_attn_W_o"], np.float32),
        "w1": np.ascontiguousarray(inputs["L0_ffn_w1"], np.float32),
        "w2": np.ascontiguousarray(inputs["L0_ffn_w2"], np.float32),
        "cols": cols,
        "n1row": np.ascontiguousarray(inputs["L0_ffn_norm1_w"].reshape(1, D), np.float32),
        "n2row": np.ascontiguousarray(inputs["L0_ffn_norm2_w"].reshape(1, HID), np.float32),
        "asb": np.ascontiguousarray(
            np.stack([np.asarray(inputs["L0_ffn_act_scale"]).reshape(()),
                      np.asarray(inputs["L0_ffn_act_bias"]).reshape(())]).reshape(2, 1), np.float32),
        "idf": np.eye(128, dtype=np.float32),
        "idb": np.eye(128, dtype=bfdt),
        "hmask": hmask,
        "hexp": np.ascontiguousarray(hmask.T),
        "ones1": np.ones((1, 128), np.float32),
        "onesc": np.ones((128, 1), np.float32),
    }
    xs = x.reshape(NCORE, BS, SEQ, D)
    in_maps = [dict(common, x=np.ascontiguousarray(xs[c])) for c in range(NCORE)]
    nc = build_nc()
    res = run_bass_kernel_spmd(nc, in_maps, core_ids=list(range(NCORE)))
    out = np.concatenate([np.asarray(r["out"]).reshape(1, BS, SEQ, D) for r in res.results], axis=0)
    return np.ascontiguousarray(out.reshape(16, SEQ, D)).astype(np.float32)


# version 22
# speedup vs baseline: 1.0433x; 1.0193x over previous
import numpy as np
import ml_dtypes
import contextlib
import concourse.bass as bass
import concourse.mybir as mybir
from concourse.bass_utils import run_bass_kernel_spmd

F32 = mybir.dt.float32
BF16 = mybir.dt.bfloat16
I32 = mybir.dt.int32
AF = mybir.ActivationFunctionType
ALU = mybir.AluOpType
AX = mybir.AxisListType.X
bfdt = ml_dtypes.bfloat16

D = 1024
SEQ = 4096
BS = 2
NCORE = 8
HID = 4096
F = 512
NCH = SEQ // F
CT = 8
DT = 8
JT = HID // 128
EPS = 1e-8
LN_EPS = 1e-5
EPS_NORM = 1e-8
EPS_Q = 1e-5
GN_N = float(SEQ * 64)


class Prog:
    ENGS = ("vector", "scalar", "gpsimd", "tensor", "sync")

    def __init__(self, nc, dma_pool=8):
        self.nc = nc
        self.scope = "prep"
        self.q = {e: [] for e in self.ENGS}
        self.cnt = {e: 0 for e in self.ENGS}
        self.dma_pool = dma_pool
        self.dma_idx = {e: 0 for e in self.ENGS}
        self.dma_uses = {}
        self.buf = {}
        self.last_wait = {}
        self.sem_keys = set(self.ENGS)

    def _bufstate(self, h):
        return self.buf.setdefault(id(h), {"w": None, "r": []})

    def _collect(self, eng, reads, writes):
        waits = {}

        def need(dep):
            if dep is None:
                return
            key, val = dep
            if key == eng == "tensor":
                return
            if val <= self.last_wait.get((eng, key), 0):
                return
            if waits.get(key, 0) < val:
                waits[key] = val
        for h in reads:
            need(self._bufstate(h)["w"])
        for h in writes:
            st = self._bufstate(h)
            need(st["w"])
            for r in st["r"]:
                need(r)
        for key, val in waits.items():
            self.last_wait[(eng, key)] = val
        return waits

    def emit(self, eng, fn, reads=(), writes=()):
        waits = self._collect(eng, reads, writes)
        self.cnt[eng] += 1
        me = (eng, self.cnt[eng])
        for h in reads:
            self._bufstate(h)["r"].append(me)
        for h in writes:
            st = self._bufstate(h)
            st["w"] = me
            st["r"] = []
        self.q[eng].append((waits, fn, eng, 1, self.scope))
        return me

    def dma(self, eng, fn, reads=(), writes=()):
        waits = self._collect(eng, reads, writes)
        j = self.dma_idx[eng]
        self.dma_idx[eng] = (j + 1) % self.dma_pool
        key = (eng, "dma", j)
        self.sem_keys.add(key)
        uses = self.dma_uses.get(key, 0)
        if uses > 0:
            v = 16 * uses
            if v > self.last_wait.get((eng, key), 0):
                if waits.get(key, 0) < v:
                    waits[key] = v
                self.last_wait[(eng, key)] = v
        self.dma_uses[key] = uses + 1
        me = (key, 16 * (uses + 1))
        for h in reads:
            self._bufstate(h)["r"].append(me)
        for h in writes:
            st = self._bufstate(h)
            st["w"] = me
            st["r"] = []
        self.q[eng].append((waits, fn, key, 16, self.scope))
        return me

    def barrier_all(self):
        snap = [(e, self.cnt[e]) for e in self.ENGS if self.cnt[e] > 0]
        snap += [(k, 16 * u) for k, u in self.dma_uses.items()]
        for eng in self.ENGS:
            waits = {}
            for key, val in snap:
                if key == eng:
                    continue
                if val > self.last_wait.get((eng, key), 0):
                    waits[key] = val
                    self.last_wait[(eng, key)] = val
            if waits:
                self.q[eng].append((waits, None, None, 0, self.scope))

    def final_wait(self, eng):
        waits = {}
        for key, val in [(e, self.cnt[e]) for e in self.ENGS] + \
                        [(k, 16 * u) for k, u in self.dma_uses.items()]:
            if key == eng or val == 0:
                continue
            if val > self.last_wait.get((eng, key), 0):
                waits[key] = val
                self.last_wait[(eng, key)] = val
        self.q[eng].append((waits, None, None, 0, self.scope))

    def build(self, cx):
        sems = {}
        for key in sorted(self.sem_keys, key=str):
            name = key if isinstance(key, str) else f"{key[0]}_d{key[2]}"
            sems[key] = cx.enter_context(self.nc.semaphore(f"s_{name}"))

        def mk(eng):
            def body(e):
                cur = None
                import contextlib as _cl
                stack = _cl.ExitStack()
                for waits, fn, sem_key, inc, scope in self.q[eng]:
                    if scope != cur:
                        stack.close()
                        stack = _cl.ExitStack()
                        stack.enter_context(self.nc.named_scope(f"{scope}_{eng}"))
                        cur = scope
                    for key, val in sorted(waits.items(), key=str):
                        e.wait_ge(sems[key], val)
                    if fn is not None:
                        fn(e).then_inc(sems[sem_key], inc)
                stack.close()
            return body
        with self.nc.Block() as blk:
            for eng in self.ENGS:
                if self.q[eng]:
                    getattr(blk, eng)(mk(eng))


class Arena:
    TOP = 229344

    def __init__(self, nc, base=0):
        self.nc = nc
        self.off = base
        self.n = 0

    def alloc(self, name, shape, dtype):
        per_part = int(np.prod(shape[1:])) * mybir.dt.size(dtype)
        size32 = (per_part + 31) & ~31
        self.off = ((self.off + 31) & ~31) + size32
        self.n += 1
        assert self.off <= 212500, f"SBUF overflow at {name}: {self.off}"
        addr = self.TOP - self.off
        return self.nc.alloc_sbuf_tensor_at(f"{name}_{self.n}", shape, dtype, offset=addr)

    def mark(self):
        return self.off

    def reset(self, off):
        self.off = off


def build_nc():
    nc = bass.Bass("TRN2", target_bir_lowering=False)
    dx = nc.dram_tensor("x", [BS, SEQ, D], F32, kind="ExternalInput")
    dout = nc.dram_tensor("out", [BS, SEQ, D], F32, kind="ExternalOutput")
    dWr = nc.dram_tensor("W_r", [D, D], F32, kind="ExternalInput")
    dWk = nc.dram_tensor("W_k", [D, D], F32, kind="ExternalInput")
    dWv = nc.dram_tensor("W_v", [D, D], F32, kind="ExternalInput")
    dWo = nc.dram_tensor("W_o", [D, D], F32, kind="ExternalInput")
    dw1 = nc.dram_tensor("w1", [HID, D], F32, kind="ExternalInput")
    dw2 = nc.dram_tensor("w2", [D, HID], F32, kind="ExternalInput")
    dcols = nc.dram_tensor("cols", [128, 64], F32, kind="ExternalInput")
    dn1row = nc.dram_tensor("n1row", [1, D], F32, kind="ExternalInput")
    dn2row = nc.dram_tensor("n2row", [1, HID], F32, kind="ExternalInput")
    dasb = nc.dram_tensor("asb", [2, 1], F32, kind="ExternalInput")
    didf = nc.dram_tensor("idf", [128, 128], F32, kind="ExternalInput")
    didb = nc.dram_tensor("idb", [128, 128], BF16, kind="ExternalInput")
    dhmask = nc.dram_tensor("hmask", [128, 2], F32, kind="ExternalInput")
    dhexp = nc.dram_tensor("hexp", [2, 128], F32, kind="ExternalInput")
    dones = nc.dram_tensor("ones1", [1, 128], F32, kind="ExternalInput")
    donesc = nc.dram_tensor("onesc", [128, 1], F32, kind="ExternalInput")
    drwkv = nc.dram_tensor("rwkv_s", [BS, CT, 128, SEQ], BF16)
    dx1 = nc.dram_tensor("x1_s", [BS, SEQ, D], BF16)
    dhq = nc.dram_tensor("hq_s", [BS, SEQ // 128, 128, HID], BF16)
    dsc = nc.dram_tensor("scal_s", [1, 8], F32)

    p = Prog(nc)
    E, DMA = p.emit, p.dma
    cx = contextlib.ExitStack()
    ar = Arena(nc)

    cols = ar.alloc("cols", [128, 64], F32)
    ccon = ar.alloc("ccon", [128, 24], F32)
    gnsc = ar.alloc("gnsc", [128, 32], F32)
    gnsh = ar.alloc("gnsh", [128, 32], F32)
    n1rep = ar.alloc("n1rep", [128, D], BF16)
    asbc = ar.alloc("asbc", [128, 2], F32)
    idf = ar.alloc("idf", [128, 128], F32)
    idb = ar.alloc("idb", [128, 128], BF16)
    hmask = ar.alloc("hmask", [128, 2], F32)
    hexp = ar.alloc("hexp", [2, 128], F32)
    ones1 = ar.alloc("ones1", [1, 128], F32)
    onesc = ar.alloc("onesc", [128, 1], F32)
    stS = ar.alloc("stS", [128, CT * BS * NCH], F32)
    stQ = ar.alloc("stQ", [128, CT * BS * NCH], F32)
    carry = ar.alloc("carry", [128, CT * BS * 4], F32)
    wsc = ar.alloc("wsc", [128, 4], F32)
    tiny = ar.alloc("tiny", [128, 32], F32)
    ffc = ar.alloc("ffc", [128, 4], F32)
    kq = ar.alloc("kq", [128, 1], F32)
    kn = ar.alloc("kn", [128, 1], F32)
    c05 = ar.alloc("c05", [128, 1], F32)
    cm05 = ar.alloc("cm05", [128, 1], F32)
    RES_END = ar.mark()
    wTr = ar.alloc("wTr", [128, DT * D], BF16)
    wTk = ar.alloc("wTk", [128, DT * D], BF16)
    wTv = ar.alloc("wTv", [128, DT * D], BF16)
    woT = ar.alloc("woT", [128, CT * D], BF16)
    n1tmp = ar.alloc("n1tmp", [1, D], F32)
    REGA_END = ar.mark()

    pA = cx.enter_context(nc.psum_tensor("pA", [128, 1024], F32))
    pB = cx.enter_context(nc.psum_tensor("pB", [128, 1024], F32))
    pC = cx.enter_context(nc.psum_tensor("pC", [128, 512], F32))
    pT = cx.enter_context(nc.psum_tensor("pT", [128, 512], F32))
    pTB = cx.enter_context(nc.psum_tensor("pTB", [128, 512], BF16))
    pTB2 = cx.enter_context(nc.psum_tensor("pTB2", [128, 512], BF16))
    KpA = [("pA", 0), ("pA", 1)]
    KpB = [("pB", 0), ("pB", 1)]

    def stS_col(ct, s, ci):
        c = ct * BS * NCH + s * NCH + ci
        return stS[:, c:c + 1]

    def stQ_col(ct, s, ci):
        c = ct * BS * NCH + s * NCH + ci
        return stQ[:, c:c + 1]

    DMA("sync", lambda e: e.dma_start(cols[:, :], dcols[:, :]), writes=[cols])
    DMA("sync", lambda e: e.dma_start(idf[:, :], didf[:, :]), writes=[idf])
    DMA("sync", lambda e: e.dma_start(idb[:, :], didb[:, :]), writes=[idb])
    DMA("sync", lambda e: e.dma_start(hmask[:, :], dhmask[:, :]), writes=[hmask])
    DMA("sync", lambda e: e.dma_start(hexp[:, :], dhexp[:, :]), writes=[hexp])
    DMA("sync", lambda e: e.dma_start(ones1[:, :], dones[:, :]), writes=[ones1])
    DMA("sync", lambda e: e.dma_start(onesc[:, :], donesc[:, :]), writes=[onesc])
    DMA("sync", lambda e: e.dma_start(asbc[:, 0:1], dasb[0:1, 0:1].broadcast_to([128, 1])), writes=[asbc])
    DMA("sync", lambda e: e.dma_start(asbc[:, 1:2], dasb[1:2, 0:1].broadcast_to([128, 1])), writes=[asbc])
    DMA("sync", lambda e: e.dma_start(n1tmp[:, :], dn1row[:, :]), writes=[n1tmp])

    E("vector", lambda e: e.memset(kq[:, :], EPS_Q), writes=[kq])
    E("vector", lambda e: e.memset(kn[:, :], EPS_NORM), writes=[kn])
    E("vector", lambda e: e.memset(c05[:, :], 0.5), writes=[c05])
    E("vector", lambda e: e.memset(cm05[:, :], -0.5), writes=[cm05])
    E("vector", lambda e: e.tensor_tensor(ffc[:, 0:1], asbc[:, 0:1], asbc[:, 0:1], ALU.mult),
      reads=[asbc], writes=[ffc])
    E("vector", lambda e: e.tensor_single_scalar(ffc[:, 0:1], ffc[:, 0:1], 1.0 / HID, ALU.mult),
      reads=[ffc], writes=[ffc])
    E("vector", lambda e: e.tensor_tensor(ffc[:, 1:2], asbc[:, 0:1], asbc[:, 1:2], ALU.mult),
      reads=[asbc], writes=[ffc])
    E("vector", lambda e: e.tensor_single_scalar(ffc[:, 1:2], ffc[:, 1:2], 2.0 / HID, ALU.mult),
      reads=[ffc], writes=[ffc])
    E("vector", lambda e: e.tensor_tensor(ffc[:, 2:3], asbc[:, 1:2], asbc[:, 1:2], ALU.mult),
      reads=[asbc], writes=[ffc])

    E("scalar", lambda e: e.activation(ccon[:, 16:24], cols[:, 0:8], AF.Exp),
      reads=[cols], writes=[ccon])
    E("scalar", lambda e: e.activation(ccon[:, 0:8], ccon[:, 16:24], AF.Exp, scale=-1.0),
      reads=[ccon], writes=[ccon])
    E("scalar", lambda e: e.activation(ccon[:, 8:16], cols[:, 8:16], AF.Exp),
      reads=[cols], writes=[ccon])

    def Ew_col(ct):
        return ccon[:, ct:ct + 1]

    def Eu_col(ct):
        return ccon[:, 8 + ct:8 + ct + 1]

    def tm_col(m, ct):
        return cols[:, (2 + m) * 8 + ct:(2 + m) * 8 + ct + 1]

    def gam_col(ct):
        return cols[:, 40 + ct:40 + ct + 1]

    def bet_col(ct):
        return cols[:, 48 + ct:48 + ct + 1]

    for h in range(2):
        E("tensor", lambda e, h=h: e.matmul(pT[:, :512], ones1[:, :],
                                            n1tmp[:, h * 512:(h + 1) * 512], start=True, stop=True),
          reads=[ones1, n1tmp], writes=[pT])
        E("vector", lambda e, h=h: e.tensor_copy(n1rep[:, h * 512:(h + 1) * 512], pT[:, :512]),
          reads=[pT], writes=[n1rep])

    ps_mark = ar.mark()
    ldAb = [ar.alloc(f"ldA{i}", [128, D], F32) for i in range(2)]
    u1pb = [ar.alloc(f"u1p{i}", [128, HID], F32) for i in range(2)]
    wabs = ar.alloc("wabs", [128, 4], F32)
    wabt = ar.alloc("wabt", [128, 2], F32)

    def evac(engname, fn, reads, writes):
        E(engname, fn, reads=reads, writes=writes)

    for wi, (dw, wT) in enumerate([(dWr, wTr), (dWk, wTk), (dWv, wTv), (dWo, woT)]):
        for s in range(8):
            ldA = ldAb[s % 2]
            DMA("sync", lambda e, dw=dw, s=s, ldA=ldA: e.dma_start(ldA[:, :D], dw[s * 128:(s + 1) * 128, :]),
                writes=[ldA])
            for dt in range(DT):
                sl = dt % 4
                E("tensor", lambda e, dt=dt, sl=sl, ldA=ldA: e.transpose(
                    pT[:, sl * 128:(sl + 1) * 128], ldA[:, dt * 128:(dt + 1) * 128], idf[:, :]),
                    reads=[ldA, idf], writes=[pT])
                eng = "vector" if dt % 2 == 0 else "scalar"
                dst = wT[:, dt * D + s * 128:dt * D + (s + 1) * 128]
                if eng == "vector":
                    E(eng, lambda e, dst=dst, sl=sl: e.tensor_copy(dst, pT[:, sl * 128:(sl + 1) * 128]),
                      reads=[pT], writes=[wT])
                else:
                    E(eng, lambda e, dst=dst, sl=sl: e.copy(dst, pT[:, sl * 128:(sl + 1) * 128]),
                      reads=[pT], writes=[wT])

    for wi, (dw, nslab, width) in enumerate([(dw1, 32, D), (dw2, 8, HID)]):
        for s in range(nslab):
            ldA = u1pb[s % 2]
            tcol = wabt[:, s % 2:s % 2 + 1]
            DMA("sync", lambda e, dw=dw, s=s, width=width, ldA=ldA: e.dma_start(ldA[:, :width], dw[s * 128:(s + 1) * 128, :]),
                writes=[ldA])
            E("vector", lambda e, width=width, ldA=ldA, tcol=tcol: e.tensor_reduce(tcol, ldA[:, :width], AX, ALU.add,
                                                               apply_absolute_value=True),
              reads=[ldA], writes=[wabt])
            if s == 0:
                E("vector", lambda e, wi=wi, tcol=tcol: e.tensor_copy(wabs[:, wi:wi + 1], tcol),
                  reads=[wabt], writes=[wabs])
            else:
                E("vector", lambda e, wi=wi, tcol=tcol: e.tensor_add(wabs[:, wi:wi + 1], wabs[:, wi:wi + 1], tcol),
                  reads=[wabt, wabs], writes=[wabs])
    for wi in range(2):
        E("tensor", lambda e, wi=wi: e.matmul(pT[0:1, 0:1], wabs[:, wi:wi + 1], onesc[:, :],
                                              start=True, stop=True),
          reads=[wabs, onesc], writes=[pT])
        E("vector", lambda e, wi=wi: e.tensor_scalar(tiny[0:1, 16 + wi:17 + wi], pT[0:1, 0:1],
                                                     1.0 / (HID * D), kq[0:1, 0:1], ALU.mult, ALU.max),
          reads=[pT], writes=[tiny])
        E("vector", lambda e, wi=wi: e.reciprocal(tiny[0:1, 18 + wi:19 + wi], tiny[0:1, 16 + wi:17 + wi]),
          reads=[tiny], writes=[tiny])
    DMA("scalar", lambda e: e.dma_start(dsc[0:1, 0:4], tiny[0:1, 16:20]), reads=[tiny], writes=[dsc])
    for j in range(4):
        DMA("sync", lambda e, j=j: e.dma_start(wsc[:, j:j + 1], dsc[0:1, j:j + 1].broadcast_to([128, 1])),
            reads=[dsc], writes=[wsc])

    p.barrier_all()
    ar.reset(ps_mark)

    p.scope = "s1"
    s1_mark = ar.mark()
    xT = [[ar.alloc(f"xT{g}_{dt}", [128, 1 + F], BF16) for dt in range(DT)] for g in range(2)]
    xmT = [[ar.alloc(f"xmT{m}_{dt}", [128, F], BF16) for dt in range(DT)] for m in range(3)]
    xin = [ar.alloc(f"xin{i}", [128, D], F32) for i in range(2)]
    difb = ar.alloc("difb", [128, F], BF16)
    NS1 = 3
    Ge = [ar.alloc(f"Ge{i}", [128, 1 + F], F32) for i in range(NS1)]
    Gte = [ar.alloc(f"Gte{i}", [128, 1 + F], F32) for i in range(NS1)]
    Ne = [ar.alloc(f"Ne{i}", [128, 1 + F], F32) for i in range(NS1)]
    SCR = [{n: ar.alloc(f"{n}{i}", [128, F], F32) for n in
            ["ek", "euk", "dD", "mM", "t1", "tq", "rf", "ts1"]} for i in range(NS1)]
    rwb = [ar.alloc(f"rwb{i}", [128, F], BF16) for i in range(NS1)]
    scrapg = ar.alloc("scrapg", [128, F], F32)
    xlast = ar.alloc("xlast", [128, BS * DT], BF16)
    S1_END = ar.mark()

    E("vector", lambda e: e.memset(carry[:, :], 0.0), writes=[carry])

    for ci in range(NCH):
        for s in range(BS):
            g = (ci * BS + s) % 2
            xTg = xT[g]
            for tt in range(F // 128):
                xi = xin[tt % 2]
                t0 = ci * F + tt * 128
                DMA("sync", lambda e, s=s, t0=t0, xi=xi: e.dma_start(xi[:, :], dx[s, t0:t0 + 128, :]),
                    writes=[xi])
                for dt in range(DT):
                    sl = dt % 4
                    E("tensor", lambda e, xi=xi, dt=dt, sl=sl: e.transpose(
                        pT[:, sl * 128:(sl + 1) * 128], xi[:, dt * 128:(dt + 1) * 128], idf[:, :]),
                        reads=[xi, idf], writes=[pT])
                    dst = xTg[dt][:, 1 + tt * 128:1 + (tt + 1) * 128]
                    E("scalar", lambda e, dst=dst, sl=sl: e.copy(dst, pT[:, sl * 128:(sl + 1) * 128]),
                      reads=[pT], writes=[xTg[dt]])
            for dt in range(DT):
                if ci == 0:
                    E("vector", lambda e, dt=dt, xTg=xTg: e.memset(xTg[dt][:, 0:1], 0.0), writes=[xTg[dt]])
                else:
                    E("gpsimd", lambda e, dt=dt, s=s, xTg=xTg: e.tensor_copy(
                        xTg[dt][:, 0:1], xlast[:, s * DT + dt:s * DT + dt + 1]),
                        reads=[xlast], writes=[xTg[dt]])
                E("gpsimd", lambda e, dt=dt, s=s, xTg=xTg: e.tensor_copy(
                    xlast[:, s * DT + dt:s * DT + dt + 1], xTg[dt][:, F:F + 1]),
                    reads=[xTg[dt]], writes=[xlast])
            for dt in range(DT):
                E("vector", lambda e, dt=dt, xTg=xTg: e.tensor_tensor(
                    difb[:, :], xTg[dt][:, 1:1 + F], xTg[dt][:, 0:F], ALU.subtract),
                    reads=[xTg[dt]], writes=[difb])
                for m in range(3):
                    E("vector", lambda e, dt=dt, m=m, xTg=xTg: e.scalar_tensor_tensor(
                        xmT[m][dt][:, :], difb[:, :], tm_col(m, dt), xTg[dt][:, 0:F],
                        ALU.mult, ALU.add),
                        reads=[difb, cols, xTg[dt]], writes=[xmT[m][dt]])
            for ct in range(CT):
                r2 = ct % NS1
                sc = SCR[r2]
                ge, gte, ne = Ge[r2], Gte[r2], Ne[r2]
                h2i = ct % 2
                pk = pA[:, h2i * 512:h2i * 512 + F]
                pv = pB[:, h2i * 512:h2i * 512 + F]
                pr = pC
                kk, kv, kr = KpA[h2i], KpB[h2i], pC
                for wT, pdst, pkey, mix in [(wTk, pk, kk, 1), (wTv, pv, kv, 2), (wTr, pr, kr, 0)]:
                    for dt in range(DT):
                        E("tensor", lambda e, wT=wT, pdst=pdst, mix=mix, dt=dt, ct=ct:
                          e.matmul(pdst[:, 0:F],
                                   wT[:, dt * D + ct * 128:dt * D + (ct + 1) * 128],
                                   xmT[mix][dt][:, :],
                                   start=(dt == 0), stop=(dt == DT - 1)),
                          reads=[wT, xmT[mix][dt]], writes=[pkey])
                co = (ct * BS + s) * 4
                E("gpsimd", lambda e, ge=ge, co=co: e.tensor_copy(ge[:, 0:1], carry[:, co:co + 1]),
                  reads=[carry], writes=[ge])
                E("gpsimd", lambda e, gte=gte, co=co: e.tensor_copy(gte[:, 0:1], carry[:, co + 1:co + 2]),
                  reads=[carry], writes=[gte])
                E("gpsimd", lambda e, ne=ne, co=co: e.tensor_copy(ne[:, 0:1], carry[:, co + 2:co + 3]),
                  reads=[carry], writes=[ne])
                E("scalar", lambda e, sc=sc, pk=pk: e.activation(sc["ek"][:, :], pk[:, 0:F], AF.Exp),
                  reads=[kk], writes=[sc["ek"]])
                E("scalar", lambda e, sc=sc, ct=ct: e.activation(sc["euk"][:, :], sc["ek"][:, :], AF.Copy, scale=Eu_col(ct)),
                  reads=[sc["ek"], ccon], writes=[sc["euk"]])
                E("scalar", lambda e, sc=sc, pr=pr: e.activation(sc["ts1"][:, :], pr[:, 0:F], AF.Exp, scale=-1.0),
                  reads=[pr], writes=[sc["ts1"]])
                E("scalar", lambda e, sc=sc: e.activation(sc["ts1"][:, :], sc["ts1"][:, :], AF.Ln, bias=1.0),
                  reads=[sc["ts1"]], writes=[sc["ts1"]])
                E("scalar", lambda e, sc=sc: e.activation(sc["ts1"][:, :], sc["ts1"][:, :], AF.Exp, scale=-1.0),
                  reads=[sc["ts1"]], writes=[sc["ts1"]])
                E("vector", lambda e, sc=sc, ge=ge, ct=ct: e.tensor_tensor_scan(
                    ge[:, 1:1 + F], Ew_col(ct).broadcast_to([128, F]), sc["ek"][:, :],
                    ge[:, 0:1], ALU.mult, ALU.add),
                    reads=[ccon, sc["ek"], ge], writes=[ge])
                E("vector", lambda e, sc=sc, ge=ge, ct=ct: e.tensor_single_scalar(
                    sc["dD"][:, :], ge[:, 0:F], Ew_col(ct), ALU.mult),
                    reads=[ge, ccon], writes=[sc["dD"]])
                E("vector", lambda e, sc=sc: e.tensor_tensor(sc["mM"][:, :], sc["dD"][:, :], sc["ek"][:, :], ALU.max),
                  reads=[sc["dD"], sc["ek"]], writes=[sc["mM"]])
                E("scalar", lambda e, sc=sc, ge=ge: e.activation(sc["t1"][:, :], ge[:, 1:1 + F], AF.Ln),
                  reads=[ge], writes=[sc["t1"]])
                E("scalar", lambda e, sc=sc: e.activation(sc["t1"][:, :], sc["t1"][:, :], AF.Exp, scale=-1.0),
                  reads=[sc["t1"]], writes=[sc["t1"]])
                E("vector", lambda e, sc=sc: e.tensor_tensor(sc["tq"][:, :], sc["mM"][:, :], sc["t1"][:, :], ALU.mult),
                  reads=[sc["mM"], sc["t1"]], writes=[sc["tq"]])
                E("vector", lambda e, sc=sc: e.tensor_scalar(sc["tq"][:, :], sc["tq"][:, :], EPS, 1.0, ALU.mult, ALU.add),
                  reads=[sc["tq"]], writes=[sc["tq"]])
                E("vector", lambda e, sc=sc, ct=ct: e.tensor_single_scalar(sc["dD"][:, :], sc["tq"][:, :], Ew_col(ct), ALU.mult),
                  reads=[sc["tq"], ccon], writes=[sc["dD"]])
                E("vector", lambda e, sc=sc: e.tensor_tensor(sc["mM"][:, :], sc["tq"][:, :], sc["ek"][:, :], ALU.mult),
                  reads=[sc["tq"], sc["ek"]], writes=[sc["mM"]])
                E("vector", lambda e, sc=sc, gte=gte: e.tensor_tensor_scan(
                    gte[:, 1:1 + F], sc["dD"][:, :], sc["mM"][:, :], gte[:, 0:1], ALU.mult, ALU.add),
                    reads=[sc["dD"], sc["mM"], gte], writes=[gte])
                E("vector", lambda e, sc=sc, gte=gte, ct=ct: e.tensor_single_scalar(
                    sc["dD"][:, :], gte[:, 0:F], Ew_col(ct), ALU.mult),
                    reads=[gte, ccon], writes=[sc["dD"]])
                E("vector", lambda e, sc=sc: e.tensor_tensor(sc["mM"][:, :], sc["dD"][:, :], sc["ek"][:, :], ALU.max),
                  reads=[sc["dD"], sc["ek"]], writes=[sc["mM"]])
                E("vector", lambda e, sc=sc: e.tensor_tensor(sc["tq"][:, :], sc["dD"][:, :], sc["ek"][:, :], ALU.add),
                  reads=[sc["dD"], sc["ek"]], writes=[sc["tq"]])
                E("scalar", lambda e, sc=sc: e.activation(sc["t1"][:, :], sc["mM"][:, :], AF.Ln),
                  reads=[sc["mM"]], writes=[sc["t1"]])
                E("scalar", lambda e, sc=sc: e.activation(sc["t1"][:, :], sc["t1"][:, :], AF.Exp, scale=-1.0),
                  reads=[sc["t1"]], writes=[sc["t1"]])
                E("vector", lambda e, sc=sc: e.tensor_tensor(sc["rf"][:, :], sc["tq"][:, :], sc["t1"][:, :], ALU.mult),
                  reads=[sc["tq"], sc["t1"]], writes=[sc["rf"]])
                E("vector", lambda e, sc=sc, ct=ct: e.tensor_scalar(sc["dD"][:, :], sc["rf"][:, :], EPS, Ew_col(ct), ALU.add, ALU.mult),
                  reads=[sc["rf"], ccon], writes=[sc["dD"]])
                E("vector", lambda e, sc=sc, pv=pv: e.tensor_tensor(sc["mM"][:, :], sc["ek"][:, :], pv[:, 0:F], ALU.mult),
                  reads=[sc["ek"], kv], writes=[sc["mM"]])
                E("vector", lambda e, sc=sc: e.scalar_tensor_tensor(sc["tq"][:, :], sc["rf"][:, :], EPS, sc["mM"][:, :], ALU.add, ALU.mult),
                  reads=[sc["rf"], sc["mM"]], writes=[sc["tq"]])
                E("vector", lambda e, sc=sc, ne=ne: e.tensor_tensor_scan(
                    ne[:, 1:1 + F], sc["dD"][:, :], sc["tq"][:, :], ne[:, 0:1], ALU.mult, ALU.add),
                    reads=[sc["dD"], sc["tq"], ne], writes=[ne])
                E("vector", lambda e, sc=sc, pv=pv: e.tensor_tensor(sc["mM"][:, :], sc["euk"][:, :], pv[:, 0:F], ALU.mult),
                  reads=[sc["euk"], kv], writes=[sc["mM"]])
                E("vector", lambda e, sc=sc, ne=ne: e.tensor_tensor(sc["tq"][:, :], ne[:, 0:F], sc["mM"][:, :], ALU.add),
                  reads=[ne, sc["mM"]], writes=[sc["tq"]])
                E("vector", lambda e, sc=sc, gte=gte: e.tensor_tensor(sc["dD"][:, :], gte[:, 0:F], sc["euk"][:, :], ALU.add),
                  reads=[gte, sc["euk"]], writes=[sc["dD"]])
                E("scalar", lambda e, sc=sc: e.activation(sc["t1"][:, :], sc["dD"][:, :], AF.Ln),
                  reads=[sc["dD"]], writes=[sc["t1"]])
                E("scalar", lambda e, sc=sc: e.activation(sc["t1"][:, :], sc["t1"][:, :], AF.Exp, scale=-1.0),
                  reads=[sc["t1"]], writes=[sc["t1"]])
                E("vector", lambda e, sc=sc: e.tensor_tensor(sc["rf"][:, :], sc["tq"][:, :], sc["t1"][:, :], ALU.mult),
                  reads=[sc["tq"], sc["t1"]], writes=[sc["rf"]])
                rw = rwb[r2]
                E("vector", lambda e, sc=sc, rw=rw: e.tensor_tensor(rw[:, :], sc["rf"][:, :], sc["ts1"][:, :], ALU.mult),
                  reads=[sc["rf"], sc["ts1"]], writes=[rw])
                E("gpsimd", lambda e, ge=ge, co=co: e.tensor_copy(carry[:, co:co + 1], ge[:, F:F + 1]),
                  reads=[ge], writes=[carry])
                E("gpsimd", lambda e, gte=gte, co=co: e.tensor_copy(carry[:, co + 1:co + 2], gte[:, F:F + 1]),
                  reads=[gte], writes=[carry])
                E("gpsimd", lambda e, ne=ne, co=co: e.tensor_copy(carry[:, co + 2:co + 3], ne[:, F:F + 1]),
                  reads=[ne], writes=[carry])
                E("vector", lambda e, rw=rw, ct=ct, s=s, ci=ci: e.tensor_reduce(
                    stS_col(ct, s, ci), rw[:, :], AX, ALU.add),
                    reads=[rw], writes=[stS])
                E("scalar", lambda e, rw=rw, ct=ct, s=s, ci=ci: e.activation(
                    scrapg[:, :], rw[:, :], AF.Square, accum_out=stQ_col(ct, s, ci)),
                    reads=[rw], writes=[scrapg, stQ])
                DMA("scalar", lambda e, rw=rw, s=s, ct=ct, ci=ci: e.dma_start(
                    drwkv[s, ct, :, ci * F:(ci + 1) * F], rw[:, :]),
                    reads=[rw], writes=[drwkv])

    p.barrier_all()
    ar.reset(s1_mark)

    p.scope = "s2a"
    s2a_mark = ar.mark()
    rwin = [ar.alloc(f"rwin{i}", [128, CT * 512], BF16) for i in range(2)]
    nrm = [ar.alloc(f"nrm{i}", [128, CT * 512], BF16) for i in range(2)]
    x2in = [ar.alloc(f"x2in{i}", [128, D], F32) for i in range(2)]
    h1t = [ar.alloc(f"h1t{i}", [128, D], F32) for i in range(2)]
    x1bt = [ar.alloc(f"x1b{i}", [128, D], BF16) for i in range(2)]
    lnstb = [ar.alloc(f"lnst{i}", [128, 8], F32) for i in range(2)]
    scr2b = [ar.alloc(f"scr2{i}", [128, D], F32) for i in range(2)]
    hstat = ar.alloc("hstat", [2, 8], F32)
    S2A_END = ar.mark()

    for ct in range(CT):
        for s in range(BS):
            c0 = ct * BS * NCH + s * NCH
            E("vector", lambda e, c0=c0: e.tensor_reduce(tiny[:, 0:1], stS[:, c0:c0 + NCH], AX, ALU.add),
              reads=[stS], writes=[tiny])
            E("vector", lambda e, c0=c0: e.tensor_reduce(tiny[:, 1:2], stQ[:, c0:c0 + NCH], AX, ALU.add),
              reads=[stQ], writes=[tiny])
            E("tensor", lambda e: e.matmul(pT[0:2, 0:2], hmask[:, :], tiny[:, 0:2],
                                           start=True, stop=True),
              reads=[hmask, tiny], writes=[pT])
            E("vector", lambda e: e.tensor_single_scalar(hstat[0:2, 0:2], pT[0:2, 0:2], 1.0 / GN_N, ALU.mult),
              reads=[pT], writes=[hstat])
            E("vector", lambda e: e.tensor_tensor(hstat[0:2, 2:3], hstat[0:2, 0:1], hstat[0:2, 0:1], ALU.mult),
              reads=[hstat], writes=[hstat])
            E("vector", lambda e: e.tensor_tensor(hstat[0:2, 2:3], hstat[0:2, 1:2], hstat[0:2, 2:3], ALU.subtract),
              reads=[hstat], writes=[hstat])
            E("scalar", lambda e: e.activation(hstat[0:2, 3:4], hstat[0:2, 2:3], AF.Ln, bias=kq[0:2, 0:1]),
              reads=[hstat, kq], writes=[hstat])
            E("scalar", lambda e: e.activation(hstat[0:2, 3:4], hstat[0:2, 3:4], AF.Exp, scale=-0.5),
              reads=[hstat], writes=[hstat])
            E("tensor", lambda e: e.matmul(pT[:, 4:5], hexp[:, :], hstat[0:2, 3:4], start=True, stop=True),
              reads=[hexp, hstat], writes=[pT])
            E("tensor", lambda e: e.matmul(pT[:, 5:6], hexp[:, :], hstat[0:2, 0:1], start=True, stop=True),
              reads=[hexp, hstat], writes=[pT])
            col = ct * BS + s
            E("vector", lambda e, ct=ct, col=col: e.tensor_single_scalar(
                gnsc[:, col:col + 1], pT[:, 4:5], gam_col(ct), ALU.mult),
                reads=[pT, cols], writes=[gnsc])
            E("vector", lambda e, col=col: e.tensor_tensor(
                tiny[:, 2:3], pT[:, 5:6], gnsc[:, col:col + 1], ALU.mult),
                reads=[pT, gnsc], writes=[tiny])
            E("vector", lambda e, ct=ct, col=col: e.tensor_scalar(
                gnsh[:, col:col + 1], tiny[:, 2:3], -1.0, bet_col(ct), ALU.mult, ALU.add),
                reads=[tiny, cols], writes=[gnsh])

    for s in range(BS):
        for gq in range(SEQ // 512):
            g = (s * (SEQ // 512) + gq) % 2
            rw, nm = rwin[g], nrm[g]
            for ct in range(CT):
                DMA("scalar", lambda e, rw=rw, ct=ct, s=s, gq=gq: e.dma_start(
                    rw[:, ct * 512:(ct + 1) * 512], drwkv[s, ct, :, gq * 512:(gq + 1) * 512]),
                    reads=[drwkv], writes=[rw])
                col = ct * BS + s
                E("vector", lambda e, rw=rw, nm=nm, ct=ct, col=col: e.tensor_scalar(
                    nm[:, ct * 512:(ct + 1) * 512], rw[:, ct * 512:(ct + 1) * 512],
                    gnsc[:, col:col + 1], gnsh[:, col:col + 1], ALU.mult, ALU.add),
                    reads=[rw, gnsc, gnsh], writes=[nm])
            def s2a_p1(tt):
                xi = x2in[tt % 2]
                h1 = h1t[tt % 2]
                lnst = lnstb[tt % 2]
                scr2 = scr2b[tt % 2]
                t0 = gq * 512 + tt * 128
                DMA("sync", lambda e, xi=xi, s=s, t0=t0: e.dma_start(xi[:, :], dx[s, t0:t0 + 128, :]),
                    writes=[xi])
                pat = pA if tt % 2 == 0 else pB
                for ct in range(CT):
                    for hh in range(2):
                        E("tensor", lambda e, nm=nm, ct=ct, tt=tt, hh=hh, pat=pat: e.matmul(
                            pat[:, hh * 512:(hh + 1) * 512],
                            nm[:, ct * 512 + tt * 128:ct * 512 + (tt + 1) * 128],
                            woT[:, ct * D + hh * 512:ct * D + (hh + 1) * 512],
                            start=(ct == 0), stop=(ct == CT - 1)),
                            reads=[nm, woT], writes=[pat])
                E("vector", lambda e, xi=xi, h1=h1, pat=pat: e.tensor_tensor(h1[:, :], xi[:, :], pat[:, :], ALU.add),
                  reads=[xi, pat], writes=[h1])
                E("vector", lambda e, h1=h1, lnst=lnst: e.tensor_reduce(lnst[:, 0:1], h1[:, :], AX, ALU.add),
                  reads=[h1], writes=[lnst])
                E("scalar", lambda e, h1=h1, scr2=scr2, lnst=lnst: e.activation(scr2[:, :], h1[:, :], AF.Square, accum_out=lnst[:, 1:2]),
                  reads=[h1], writes=[scr2, lnst])
                E("vector", lambda e, lnst=lnst: e.tensor_single_scalar(lnst[:, 2:3], lnst[:, 0:1], 1.0 / D, ALU.mult),
                  reads=[lnst], writes=[lnst])
                E("vector", lambda e, lnst=lnst: e.tensor_tensor(lnst[:, 3:4], lnst[:, 2:3], lnst[:, 2:3], ALU.mult),
                  reads=[lnst], writes=[lnst])
                E("vector", lambda e, lnst=lnst: e.tensor_single_scalar(lnst[:, 4:5], lnst[:, 1:2], 1.0 / D, ALU.mult),
                  reads=[lnst], writes=[lnst])
                E("vector", lambda e, lnst=lnst: e.tensor_tensor(lnst[:, 4:5], lnst[:, 4:5], lnst[:, 3:4], ALU.subtract),
                  reads=[lnst], writes=[lnst])
                E("scalar", lambda e, lnst=lnst: e.activation(lnst[:, 5:6], lnst[:, 4:5], AF.Ln, bias=kq[:, 0:1]),
                  reads=[lnst, kq], writes=[lnst])
                E("scalar", lambda e, lnst=lnst: e.activation(lnst[:, 5:6], lnst[:, 5:6], AF.Exp, scale=-0.5),
                  reads=[lnst], writes=[lnst])

            def s2a_p2(tt):
                h1 = h1t[tt % 2]
                xb = x1bt[tt % 2]
                lnst = lnstb[tt % 2]
                t0 = gq * 512 + tt * 128
                E("vector", lambda e, lnst=lnst: e.tensor_single_scalar(lnst[:, 6:7], lnst[:, 2:3], -1.0, ALU.mult),
                  reads=[lnst], writes=[lnst])
                E("vector", lambda e, h1=h1, xb=xb, lnst=lnst: e.tensor_scalar(
                    xb[:, :], h1[:, :], lnst[:, 6:7], lnst[:, 5:6], ALU.add, ALU.mult),
                    reads=[h1, lnst], writes=[xb])
                DMA("scalar", lambda e, xb=xb, s=s, t0=t0: e.dma_start(dx1[s, t0:t0 + 128, :], xb[:, :]),
                    reads=[xb], writes=[dx1])

            for tt in range(5):
                if tt < 4:
                    s2a_p1(tt)
                if tt >= 1:
                    s2a_p2(tt - 1)

    p.barrier_all()
    ar.reset(RES_END)

    p.scope = "s2b"
    s2b_mark = ar.mark()
    hscst = ar.alloc("hscst", [128, 64], F32)
    s2b2_mark = ar.mark()
    w1T = ar.alloc("w1T", [128, DT * HID], BF16)
    n2rep = ar.alloc("n2rep", [128, HID], BF16)
    n2tmp = ar.alloc("n2tmp", [1, 512], F32)
    HH = HID // 2
    Txb = [ar.alloc(f"p1x1_{i}", [128, D], BF16) for i in range(2)]
    Txnb = [ar.alloc(f"p1xn_{i}", [128, D], BF16) for i in range(2)]
    Tyf = [ar.alloc(f"p1yf_{i}", [128, D], BF16) for i in range(2)]
    TxqT = [ar.alloc(f"p1xq_{i}", [128, D], BF16) for i in range(3)]
    Tyi = [ar.alloc(f"p1yi_{i}", [128, D], I32) for i in range(4)]
    Thyq = [ar.alloc(f"p1hq_{i}", [128, HH], BF16) for i in range(2)]
    Thyf = [ar.alloc(f"p1hf_{i}", [128, HH], BF16) for i in range(3)]
    Thu1 = [ar.alloc(f"p1hu_{i}", [128, HH], BF16) for i in range(3)]
    Threl = [ar.alloc(f"p1hr_{i}", [128, HH], F32) for i in range(2)]
    ThrelB = [ar.alloc(f"p1hB_{i}", [128, HH], F32) for i in range(2)]
    ThqS = [[ar.alloc(f"p1hs_{i}_{k}", [128, 512], BF16) for k in range(2)] for i in range(2)]
    Tq = [ar.alloc(f"p1q_{i}", [128, 32], F32) for i in range(4)]
    Tdif = [ar.alloc(f"p1df_{i}", [128, HH], BF16) for i in range(2)]
    asn2 = ar.alloc("asn2", [128, HID], BF16)
    dumS = ar.alloc("dumS", [128, 2048], BF16)
    S2B1_END = ar.mark()

    wldb = [Threl[0][:, 0:D], Threl[0][:, D:2 * D]]
    wu1b = [Thyq[0][:, 0:D], Thyq[0][:, D:2 * D]]
    wu2b = [Thyf[0][:, 0:D], Thyf[0][:, D:2 * D]]
    wtnb = [Thu1[0][:, 0:D], Thu1[0][:, D:2 * D]]
    wkeys = [Threl[0], Thyq[0], Thyf[0], Thu1[0]]
    for s in range(32):
        wld, wu1, wu2, wtn = wldb[s % 2], wu1b[s % 2], wu2b[s % 2], wtnb[s % 2]
        DMA("sync", lambda e, s=s, wld=wld: e.dma_start(wld[:, :], dw1[s * 128:(s + 1) * 128, :]),
            writes=[wkeys[0]])
        E("scalar", lambda e, wld=wld: e.activation(wld[:, :], wld[:, :], AF.Copy, scale=wsc[:, 2:3]),
          reads=[wkeys[0], wsc], writes=[wkeys[0]])
        E("vector", lambda e, wld=wld, wu1=wu1: e.tensor_single_scalar(wu1[:, :], wld[:, :], 0.5, ALU.is_gt),
          reads=[wkeys[0]], writes=[wkeys[1]])
        E("vector", lambda e, wld=wld, wu2=wu2: e.tensor_single_scalar(wu2[:, :], wld[:, :], -0.5, ALU.is_lt),
          reads=[wkeys[0]], writes=[wkeys[2]])
        E("vector", lambda e, wu1=wu1, wu2=wu2, wtn=wtn: e.tensor_tensor(wtn[:, :], wu1[:, :], wu2[:, :], ALU.subtract),
          reads=[wkeys[1], wkeys[2]], writes=[wkeys[3]])
        for bb in range(8):
            sl = bb % 4
            ptb = pTB if (bb // 4) % 2 == 0 else pTB2
            E("tensor", lambda e, bb=bb, sl=sl, wtn=wtn, ptb=ptb: e.transpose(
                ptb[:, sl * 128:(sl + 1) * 128], wtn[:, bb * 128:(bb + 1) * 128], idb[:, :]),
                reads=[wkeys[3], idb], writes=[ptb])
            if bb % 4 == 3:
                eng = "vector" if (bb // 4) % 2 == 0 else "scalar"
                for k in range(4):
                    dstk = w1T[:, (bb - 3 + k) * HID + s * 128:(bb - 3 + k) * HID + (s + 1) * 128]
                    if eng == "vector":
                        E(eng, lambda e, dstk=dstk, k=k, ptb=ptb: e.tensor_copy(dstk, ptb[:, k * 128:(k + 1) * 128]),
                          reads=[ptb], writes=[w1T])
                    else:
                        E(eng, lambda e, dstk=dstk, k=k, ptb=ptb: e.copy(dstk, ptb[:, k * 128:(k + 1) * 128]),
                          reads=[ptb], writes=[w1T])
    for h in range(HID // 512):
        DMA("sync", lambda e, h=h: e.dma_start(n2tmp[:, :], dn2row[:, h * 512:(h + 1) * 512]),
            writes=[n2tmp])
        E("tensor", lambda e, h=h: e.matmul(pT[:, :512], ones1[:, :], n2tmp[:, :], start=True, stop=True),
          reads=[ones1, n2tmp], writes=[pT])
        E("vector", lambda e, h=h: e.tensor_copy(n2rep[:, h * 512:(h + 1) * 512], pT[:, :512]),
          reads=[pT], writes=[n2rep])
    E("vector", lambda e: e.tensor_single_scalar(asn2[:, :], n2rep[:, :], asbc[:, 0:1], ALU.mult),
      reads=[n2rep, asbc], writes=[asn2])
    E("vector", lambda e: e.reciprocal(ffc[:, 3:4], asbc[:, 0:1]), reads=[asbc], writes=[ffc])
    E("vector", lambda e: e.tensor_tensor(ffc[:, 3:4], ffc[:, 3:4], asbc[:, 1:2], ALU.mult),
      reads=[ffc, asbc], writes=[ffc])

    NTC = SEQ // 128
    NUN = BS * NTC
    pCb = pC[:, :].bitcast(BF16)

    def p1_Apre(u):
        s, t0 = u // NTC, (u % NTC) * 128
        xb = Txb[u % 2]
        DMA("sync", lambda e, xb=xb, s=s, t0=t0: e.dma_start(xb[:, :], dx1[s, t0:t0 + 128, :]),
            reads=[dx1], writes=[xb])

    def p1_A0(u):
        xb, q = Txb[u % 2], Tq[u % 4]
        E("scalar", lambda e, xb=xb, q=q: e.activation(dumS[:, 0:D], xb[:, :], AF.Square, accum_out=q[:, 0:1]),
          reads=[xb], writes=[dumS, q])
        E("scalar", lambda e, q=q: e.activation(q[:, 1:2], q[:, 0:1], AF.Ln, bias=kn[:, 0:1], scale=1.0 / D),
          reads=[q, kn], writes=[q])
        E("scalar", lambda e, q=q: e.activation(q[:, 1:2], q[:, 1:2], AF.Exp, scale=-0.5),
          reads=[q], writes=[q])

    def p1_Av(u):
        xb, xnb, yf, yi, q = Txb[u % 2], Txnb[u % 2], Tyf[u % 2], Tyi[u % 4], Tq[u % 4]
        E("vector", lambda e, xb=xb, xnb=xnb: e.tensor_tensor(xnb[:, :], xb[:, :], n1rep[:, :], ALU.mult),
          reads=[xb, n1rep], writes=[xnb])
        E("vector", lambda e, xnb=xnb, q=q: e.tensor_reduce(q[:, 2:3], xnb[:, :], AX, ALU.max, apply_absolute_value=True),
          reads=[xnb], writes=[q])
        E("vector", lambda e, q=q: e.tensor_scalar(q[:, 3:4], q[:, 2:3], q[:, 1:2], kq[:, 0:1], ALU.mult, ALU.max),
          reads=[q, kq], writes=[q])
        E("vector", lambda e, q=q: e.reciprocal(q[:, 4:5], q[:, 3:4]), reads=[q], writes=[q])
        E("vector", lambda e, q=q: e.tensor_scalar(q[:, 6:7], q[:, 4:5], 127.0, q[:, 1:2], ALU.mult, ALU.mult),
          reads=[q], writes=[q])
        E("vector", lambda e, q=q: e.tensor_scalar(q[:, 7:8], q[:, 3:4], 1.0 / 127.0, wsc[:, 0:1], ALU.mult, ALU.mult),
          reads=[q, wsc], writes=[q])
        E("vector", lambda e, xnb=xnb, q=q: e.tensor_single_scalar(xnb[:, :], xnb[:, :], q[:, 6:7], ALU.mult),
          reads=[xnb, q], writes=[xnb])
        E("vector", lambda e, xnb=xnb, yi=yi: e.tensor_copy(yi[:, :], xnb[:, :]), reads=[xnb], writes=[yi])
        E("vector", lambda e, yi=yi, yf=yf: e.tensor_copy(yf[:, :], yi[:, :]), reads=[yi], writes=[yf])
        E("vector", lambda e, xnb=xnb, yf=yf: e.tensor_tensor(xnb[:, :], xnb[:, :], yf[:, :], ALU.subtract),
          reads=[xnb, yf], writes=[xnb])
        E("vector", lambda e, xnb=xnb, yf=yf: e.scalar_tensor_tensor(
            yf[:, :], xnb[:, :], c05[:, 0:1], yf[:, :], ALU.is_gt, ALU.add),
          reads=[xnb, c05, yf], writes=[yf])
        E("vector", lambda e, xnb=xnb, yf=yf: e.scalar_tensor_tensor(
            yf[:, :], xnb[:, :], cm05[:, 0:1], yf[:, :], ALU.is_ge, ALU.add),
          reads=[xnb, cm05, yf], writes=[yf])
        E("vector", lambda e, yf=yf: e.tensor_single_scalar(yf[:, :], yf[:, :], -1.0, ALU.add),
          reads=[yf], writes=[yf])

    pTbc = pT[:, :].bitcast(BF16)

    def p1_Atr(u):
        yf, xqT = Tyf[u % 2], TxqT[u % 3]
        for bb in range(8):
            sl = bb % 4
            if bb < 4:
                E("tensor", lambda e, bb=bb, sl=sl, yf=yf: e.transpose(
                    pTB[:, sl * 128:(sl + 1) * 128], yf[:, bb * 128:(bb + 1) * 128], idb[:, :]),
                    reads=[yf, idb], writes=[pTB])
            else:
                E("tensor", lambda e, bb=bb, sl=sl, yf=yf: e.transpose(
                    pTbc[:, sl * 128:(sl + 1) * 128], yf[:, bb * 128:(bb + 1) * 128], idb[:, :]),
                    reads=[yf, idb], writes=[pT])
        E("vector", lambda e, xqT=xqT: e.tensor_copy(xqT[:, 0:512], pTB[:, 0:512]),
          reads=[pTB], writes=[xqT])
        E("vector", lambda e, xqT=xqT: e.tensor_copy(xqT[:, 512:1024], pTbc[:, 0:512]),
          reads=[pT], writes=[xqT])

    def p1_Bmm(u):
        xqT, q = TxqT[u % 3], Tq[u % 4]
        for jg in range(2):
            hdst = Threl[u % 2] if jg == 0 else ThrelB[u % 2]
            for dt in range(DT):
                for hh in range(4):
                    ph = pA if hh < 2 else pB
                    ho = (hh % 2) * 512
                    E("tensor", lambda e, dt=dt, jg=jg, hh=hh, ph=ph, ho=ho, xqT=xqT: e.matmul(
                        ph[:, ho:ho + 512],
                        xqT[:, dt * 128:(dt + 1) * 128],
                        w1T[:, dt * HID + jg * 2048 + hh * 512:dt * HID + jg * 2048 + (hh + 1) * 512],
                        start=(dt == 0), stop=(dt == DT - 1)),
                        reads=[xqT, w1T], writes=[ph])
            for piece, ph in [(0, pA), (1, pB)]:
                off = piece * 1024
                E("scalar", lambda e, ph=ph, off=off, hdst=hdst, q=q: e.activation(
                    hdst[:, off:off + 1024], ph[:, :], AF.Relu, scale=q[:, 7:8]),
                  reads=[ph, q], writes=[hdst])
            E("scalar", lambda e, jg=jg, hdst=hdst, q=q: e.activation(hdst[:, :], hdst[:, :], AF.Square,
                                                                      accum_out=q[:, 8 + jg:9 + jg]),
              reads=[hdst], writes=[hdst, q])
            E("scalar", lambda e, jg=jg, hdst=hdst, q=q: e.activation(dumS[:, 0:2048], hdst[:, :], AF.Square,
                                                                      accum_out=q[:, 10 + jg:11 + jg]),
              reads=[hdst], writes=[dumS, q])

    def p1_Bv(u):
        hrel, hrelB, hyf, hu1, q = Threl[u % 2], ThrelB[u % 2], Thyf[u % 3], Thu1[u % 3], Tq[u % 4]
        E("vector", lambda e, hrel=hrel, hyf=hyf: e.scalar_tensor_tensor(
            hyf[:, :], hrel[:, :], ffc[:, 3:4], asn2[:, 0:HH], ALU.add, ALU.mult),
          reads=[hrel, ffc, asn2], writes=[hyf])
        E("vector", lambda e, hrelB=hrelB, hu1=hu1: e.scalar_tensor_tensor(
            hu1[:, :], hrelB[:, :], ffc[:, 3:4], asn2[:, HH:HID], ALU.add, ALU.mult),
          reads=[hrelB, ffc, asn2], writes=[hu1])
        E("vector", lambda e, q=q: e.tensor_tensor(q[:, 16:17], q[:, 8:9], q[:, 9:10], ALU.add),
          reads=[q], writes=[q])
        E("vector", lambda e, q=q: e.tensor_tensor(q[:, 17:18], q[:, 10:11], q[:, 11:12], ALU.add),
          reads=[q], writes=[q])
        E("vector", lambda e, q=q: e.tensor_scalar(q[:, 20:21], q[:, 17:18], ffc[:, 0:1], None, ALU.mult),
          reads=[q, ffc], writes=[q])
        E("vector", lambda e, q=q: e.scalar_tensor_tensor(q[:, 20:21], q[:, 16:17], ffc[:, 1:2], q[:, 20:21], ALU.mult, ALU.add),
          reads=[q, ffc], writes=[q])
        E("vector", lambda e, q=q: e.tensor_single_scalar(q[:, 20:21], q[:, 20:21], ffc[:, 2:3], ALU.add),
          reads=[q, ffc], writes=[q])
        E("scalar", lambda e, q=q: e.activation(q[:, 21:22], q[:, 20:21], AF.Ln, bias=kn[:, 0:1]),
          reads=[q, kn], writes=[q])
        E("scalar", lambda e, q=q: e.activation(q[:, 21:22], q[:, 21:22], AF.Exp, scale=-0.5),
          reads=[q], writes=[q])

    def p1_Bq(u):
        hyf, hu1, q = Thyf[u % 3], Thu1[u % 3], Tq[u % 4]
        E("vector", lambda e, hyf=hyf, q=q: e.tensor_reduce(q[:, 14:15], hyf[:, :], AX, ALU.max, apply_absolute_value=True),
          reads=[hyf], writes=[q])
        E("vector", lambda e, hu1=hu1, q=q: e.tensor_reduce(q[:, 15:16], hu1[:, :], AX, ALU.max, apply_absolute_value=True),
          reads=[hu1], writes=[q])
        E("vector", lambda e, q=q: e.tensor_tensor(q[:, 14:15], q[:, 14:15], q[:, 15:16], ALU.max),
          reads=[q], writes=[q])
        E("vector", lambda e, q=q: e.tensor_scalar(q[:, 22:23], q[:, 14:15], q[:, 21:22], kq[:, 0:1], ALU.mult, ALU.max),
          reads=[q, kq], writes=[q])
        E("vector", lambda e, q=q: e.reciprocal(q[:, 23:24], q[:, 22:23]), reads=[q], writes=[q])
        E("vector", lambda e, q=q: e.tensor_scalar(q[:, 25:26], q[:, 23:24], 127.0, q[:, 21:22], ALU.mult, ALU.mult),
          reads=[q], writes=[q])
        E("vector", lambda e, q=q: e.tensor_scalar(q[:, 26:27], q[:, 22:23], 1.0 / 127.0, wsc[:, 1:2], ALU.mult, ALU.mult),
          reads=[q, wsc], writes=[q])
        E("vector", lambda e, q=q, u=u: e.tensor_copy(hscst[:, u:u + 1], q[:, 26:27]),
          reads=[q], writes=[hscst])

    def p1_Cra(u):
        q = Tq[u % 4]
        for jg in range(2):
            rnd = Thyf[u % 3] if jg == 0 else Thu1[u % 3]
            hyq = Thyq[(u + jg) % 2]
            E("vector", lambda e, rnd=rnd, hyq=hyq, q=q: e.tensor_single_scalar(
                hyq[:, :], rnd[:, :], q[:, 25:26], ALU.mult),
              reads=[rnd, q], writes=[hyq])
        for jg in range(2):
            rnd = Thyf[u % 3] if jg == 0 else Thu1[u % 3]
            hyq = Thyq[(u + jg) % 2]
            yi = Tyi[(u + 2 * jg) % 4]
            for qq in range(2):
                E("scalar", lambda e, qq=qq, hyq=hyq, yi=yi: e.copy(yi[:, :], hyq[:, qq * 1024:(qq + 1) * 1024]),
                  reads=[hyq], writes=[yi])
                E("scalar", lambda e, qq=qq, rnd=rnd, yi=yi: e.copy(rnd[:, qq * 1024:(qq + 1) * 1024], yi[:, :]),
                  reads=[yi], writes=[rnd])

    def p1_Crb(u):
        for jg in range(2):
            rnd = Thyf[u % 3] if jg == 0 else Thu1[u % 3]
            hyq = Thyq[(u + jg) % 2]
            dhf = Tdif[(u + jg) % 2]
            E("vector", lambda e, hyq=hyq, rnd=rnd, dhf=dhf: e.tensor_tensor(dhf[:, :], hyq[:, :], rnd[:, :], ALU.subtract),
              reads=[hyq, rnd], writes=[dhf])
            E("vector", lambda e, dhf=dhf, rnd=rnd: e.scalar_tensor_tensor(
                rnd[:, :], dhf[:, :], c05[:, 0:1], rnd[:, :], ALU.is_gt, ALU.add),
              reads=[dhf, c05, rnd], writes=[rnd])
            E("vector", lambda e, dhf=dhf, rnd=rnd: e.scalar_tensor_tensor(
                rnd[:, :], dhf[:, :], cm05[:, 0:1], rnd[:, :], ALU.is_ge, ALU.add),
              reads=[dhf, cm05, rnd], writes=[rnd])
            E("vector", lambda e, rnd=rnd: e.tensor_single_scalar(rnd[:, :], rnd[:, :], -1.0, ALU.add),
              reads=[rnd], writes=[rnd])

    KpC = [("pClo",), ("pChi",), ("pThi",)]

    def p1_Ct(u, jg):
        s, tc = u // NTC, u % NTC
        hqS = ThqS[u % 2]
        slotsets = [(pTB2, 0, pTB2), (pCb, 0, KpC[0]), (pCb, 512, KpC[1]), (pTbc, 512, KpC[2])]
        hof = jg * HH
        rnd = Thyf[u % 3] if jg == 0 else Thu1[u % 3]
        for bb in range(HH // 128):
            gg = u * 8 + jg * 4 + bb // 4
            base, off, pkey = slotsets[gg % 4]
            sl = bb % 4
            E("tensor", lambda e, bb=bb, sl=sl, rnd=rnd, base=base, off=off: e.transpose(
                base[:, off + sl * 128:off + (sl + 1) * 128], rnd[:, bb * 128:(bb + 1) * 128], idb[:, :]),
                reads=[rnd, idb], writes=[pkey])
            if bb % 4 == 3:
                hq_s = hqS[gg % 2]
                E("scalar", lambda e, hq_s=hq_s, base=base, off=off: e.copy(hq_s[:, :], base[:, off:off + 512]),
                  reads=[pkey], writes=[hq_s])
                DMA("scalar", lambda e, hq_s=hq_s, s=s, tc=tc, hof=hof, bb=bb: e.dma_start(
                    dhq[s, tc, :, hof + (bb - 3) * 128:hof + (bb + 1) * 128], hq_s[:, :]),
                    reads=[hq_s], writes=[dhq])

    p1_Apre(0)
    for c in range(NUN + 4):
        if 0 <= c - 1 < NUN:
            p1_Atr(c - 1)
        if c < NUN:
            p1_A0(c)
        if 0 <= c - 3 < NUN:
            p1_Bq(c - 3)
            p1_Cra(c - 3)
        if 0 <= c - 4 < NUN:
            p1_Ct(c - 4, 0)
        if 0 <= c - 1 < NUN:
            p1_Bmm(c - 1)
        if c < NUN:
            p1_Av(c)
        if 0 <= c - 3 < NUN:
            p1_Crb(c - 3)
        if 0 <= c - 2 < NUN:
            p1_Bv(c - 2)
        if 0 <= c - 4 < NUN:
            p1_Ct(c - 4, 1)
        if c + 1 < NUN:
            p1_Apre(c + 1)

    p.barrier_all()
    ar.reset(s2b2_mark)

    p.scope = "s2c"
    s2c_mark = ar.mark()
    w2T = ar.alloc("w2T", [128, JT * D], BF16)
    P2 = []
    for i in range(3):
        P2.append({
            "hqT": ar.alloc(f"p2hq_{i}", [128, HID], BF16),
            "x1": ar.alloc(f"p2x1_{i}", [128, D], BF16),
            "xf": ar.alloc(f"p2xf_{i}", [128, D], F32),
            "yo": ar.alloc(f"p2yo_{i}", [128, D], F32),
            "sq": ar.alloc(f"p2sq_{i}", [128, D], F32),
            "q": ar.alloc(f"p2q_{i}", [128, 8], F32),
        })
    S2C_END = ar.mark()

    w2ldb = [ar.alloc(f"w2ld{i}", [128, HID], F32) for i in range(2)]
    w2u1b = [ar.alloc(f"w2u1{i}", [128, HID], BF16) for i in range(2)]
    w2tnb = [ar.alloc(f"w2tn{i}", [128, HID], BF16) for i in range(2)]
    for s in range(8):
        wld, wu1, wtn = w2ldb[s % 2], w2u1b[s % 2], w2tnb[s % 2]
        DMA("sync", lambda e, s=s, wld=wld: e.dma_start(wld[:, :], dw2[s * 128:(s + 1) * 128, :]),
            writes=[wld])
        E("scalar", lambda e, wld=wld: e.activation(wld[:, :], wld[:, :], AF.Copy, scale=wsc[:, 3:4]),
          reads=[wld, wsc], writes=[wld])
        E("vector", lambda e, wld=wld, wtn=wtn: e.tensor_single_scalar(wtn[:, :], wld[:, :], 0.5, ALU.is_gt),
          reads=[wld], writes=[wtn])
        E("vector", lambda e, wld=wld, wu1=wu1: e.tensor_single_scalar(wu1[:, :], wld[:, :], -0.5, ALU.is_lt),
          reads=[wld], writes=[wu1])
        E("vector", lambda e, wu1=wu1, wtn=wtn: e.tensor_tensor(wtn[:, :], wtn[:, :], wu1[:, :], ALU.subtract),
          reads=[wu1, wtn], writes=[wtn])
        for bb in range(JT):
            sl = bb % 4
            ptb = pTB if (bb // 4) % 2 == 0 else pTB2
            E("tensor", lambda e, bb=bb, sl=sl, wtn=wtn, ptb=ptb: e.transpose(
                ptb[:, sl * 128:(sl + 1) * 128], wtn[:, bb * 128:(bb + 1) * 128], idb[:, :]),
                reads=[wtn, idb], writes=[ptb])
            if bb % 4 == 3:
                eng = "vector" if (bb // 4) % 2 == 0 else "scalar"
                for k in range(4):
                    dstk = w2T[:, (bb - 3 + k) * D + s * 128:(bb - 3 + k) * D + (s + 1) * 128]
                    if eng == "vector":
                        E(eng, lambda e, dstk=dstk, k=k, ptb=ptb: e.tensor_copy(dstk, ptb[:, k * 128:(k + 1) * 128]),
                          reads=[ptb], writes=[w2T])
                    else:
                        E(eng, lambda e, dstk=dstk, k=k, ptb=ptb: e.copy(dstk, ptb[:, k * 128:(k + 1) * 128]),
                          reads=[ptb], writes=[w2T])
    for s in range(BS):
        for tc in range(NTC):
            t0 = tc * 128
            S = P2[(s * NTC + tc) % 3]
            hqT, xb, xf, yo, sqt, q = S["hqT"], S["x1"], S["xf"], S["yo"], S["sq"], S["q"]
            cidx = s * NTC + tc
            DMA("sync", lambda e, hqT=hqT, s=s, tc=tc: e.dma_start(hqT[:, :], dhq[s, tc, :, :]),
                reads=[dhq], writes=[hqT])
            DMA("sync", lambda e, xb=xb, s=s, t0=t0: e.dma_start(xb[:, :], dx1[s, t0:t0 + 128, :]),
                reads=[dx1], writes=[xb])
            pcy = pA if (s * NTC + tc) % 2 == 0 else pB
            for jt in range(JT):
                for hh in range(2):
                    E("tensor", lambda e, jt=jt, hh=hh, hqT=hqT, pcy=pcy: e.matmul(
                        pcy[:, hh * 512:(hh + 1) * 512],
                        hqT[:, jt * 128:(jt + 1) * 128],
                        w2T[:, jt * D + hh * 512:jt * D + (hh + 1) * 512],
                        start=(jt == 0), stop=(jt == JT - 1)),
                        reads=[hqT, w2T], writes=[pcy])
            E("scalar", lambda e, xb=xb, xf=xf: e.copy(xf[:, :], xb[:, :]), reads=[xb], writes=[xf])
            E("vector", lambda e, yo=yo, cidx=cidx, pcy=pcy: e.tensor_single_scalar(yo[:, :], pcy[:, :], hscst[:, cidx:cidx + 1], ALU.mult),
              reads=[pcy, hscst], writes=[yo])
            E("vector", lambda e, yo=yo, xf=xf: e.tensor_tensor(yo[:, :], yo[:, :], xf[:, :], ALU.add),
              reads=[yo, xf], writes=[yo])
            E("vector", lambda e, yo=yo, q=q: e.tensor_reduce(q[:, 0:1], yo[:, :], AX, ALU.add),
              reads=[yo], writes=[q])
            E("scalar", lambda e, yo=yo, sqt=sqt, q=q: e.activation(sqt[:, :], yo[:, :], AF.Square, accum_out=q[:, 1:2]),
              reads=[yo], writes=[sqt, q])
            E("vector", lambda e, q=q: e.tensor_single_scalar(q[:, 2:3], q[:, 0:1], 1.0 / D, ALU.mult),
              reads=[q], writes=[q])
            E("vector", lambda e, q=q: e.tensor_tensor(q[:, 3:4], q[:, 2:3], q[:, 2:3], ALU.mult),
              reads=[q], writes=[q])
            E("vector", lambda e, q=q: e.tensor_single_scalar(q[:, 4:5], q[:, 1:2], 1.0 / D, ALU.mult),
              reads=[q], writes=[q])
            E("vector", lambda e, q=q: e.tensor_tensor(q[:, 4:5], q[:, 4:5], q[:, 3:4], ALU.subtract),
              reads=[q], writes=[q])
            E("scalar", lambda e, q=q: e.activation(q[:, 5:6], q[:, 4:5], AF.Ln, bias=kq[:, 0:1]),
              reads=[q, kq], writes=[q])
            E("scalar", lambda e, q=q: e.activation(q[:, 5:6], q[:, 5:6], AF.Exp, scale=-0.5),
              reads=[q], writes=[q])
            E("vector", lambda e, q=q: e.tensor_single_scalar(q[:, 6:7], q[:, 2:3], -1.0, ALU.mult),
              reads=[q], writes=[q])
            E("vector", lambda e, yo=yo, q=q: e.tensor_scalar(yo[:, :], yo[:, :], q[:, 6:7], q[:, 5:6], ALU.add, ALU.mult),
              reads=[yo, q], writes=[yo])
            DMA("scalar", lambda e, s=s, t0=t0, yo=yo: e.dma_start(dout[s, t0:t0 + 128, :], yo[:, :]),
                reads=[yo])

    p.final_wait("sync")
    p.build(cx)
    cx.close()
    return nc


def kernel(**inputs):
    x = np.ascontiguousarray(inputs["L0_attn_x"])

    def colmaj(v):
        return np.ascontiguousarray(np.asarray(v).reshape(8, 128).T.astype(np.float32))

    cols = np.concatenate([
        colmaj(inputs["L0_attn_time_decay"].reshape(D)),
        colmaj(inputs["L0_attn_time_first"].reshape(D)),
        colmaj(inputs["L0_attn_time_mix_r"].reshape(D)),
        colmaj(inputs["L0_attn_time_mix_k"].reshape(D)),
        colmaj(inputs["L0_attn_time_mix_v"].reshape(D)),
        colmaj(inputs["L0_attn_ln_gamma"].reshape(D)),
        colmaj(inputs["L0_attn_ln_beta"].reshape(D)),
        np.zeros((128, 8), np.float32)], axis=1)
    hmask = np.zeros((128, 2), np.float32)
    hmask[:64, 0] = 1.0
    hmask[64:, 1] = 1.0
    common = {
        "W_r": np.ascontiguousarray(inputs["L0_attn_W_r"], np.float32),
        "W_k": np.ascontiguousarray(inputs["L0_attn_W_k"], np.float32),
        "W_v": np.ascontiguousarray(inputs["L0_attn_W_v"], np.float32),
        "W_o": np.ascontiguousarray(inputs["L0_attn_W_o"], np.float32),
        "w1": np.ascontiguousarray(inputs["L0_ffn_w1"], np.float32),
        "w2": np.ascontiguousarray(inputs["L0_ffn_w2"], np.float32),
        "cols": cols,
        "n1row": np.ascontiguousarray(inputs["L0_ffn_norm1_w"].reshape(1, D), np.float32),
        "n2row": np.ascontiguousarray(inputs["L0_ffn_norm2_w"].reshape(1, HID), np.float32),
        "asb": np.ascontiguousarray(
            np.stack([np.asarray(inputs["L0_ffn_act_scale"]).reshape(()),
                      np.asarray(inputs["L0_ffn_act_bias"]).reshape(())]).reshape(2, 1), np.float32),
        "idf": np.eye(128, dtype=np.float32),
        "idb": np.eye(128, dtype=bfdt),
        "hmask": hmask,
        "hexp": np.ascontiguousarray(hmask.T),
        "ones1": np.ones((1, 128), np.float32),
        "onesc": np.ones((128, 1), np.float32),
    }
    xs = x.reshape(NCORE, BS, SEQ, D)
    in_maps = [dict(common, x=np.ascontiguousarray(xs[c])) for c in range(NCORE)]
    nc = build_nc()
    res = run_bass_kernel_spmd(nc, in_maps, core_ids=list(range(NCORE)))
    out = np.concatenate([np.asarray(r["out"]).reshape(1, BS, SEQ, D) for r in res.results], axis=0)
    return np.ascontiguousarray(out.reshape(16, SEQ, D)).astype(np.float32)


# version 23 (final)
# speedup vs baseline: 1.0757x; 1.0311x over previous
import numpy as np
import ml_dtypes
import contextlib
import concourse.bass as bass
import concourse.mybir as mybir
from concourse.bass_utils import run_bass_kernel_spmd

F32 = mybir.dt.float32
BF16 = mybir.dt.bfloat16
I32 = mybir.dt.int32
AF = mybir.ActivationFunctionType
ALU = mybir.AluOpType
AX = mybir.AxisListType.X
bfdt = ml_dtypes.bfloat16

D = 1024
SEQ = 4096
BS = 2
NCORE = 8
HID = 4096
F = 512
NCH = SEQ // F
CT = 8
DT = 8
JT = HID // 128
EPS = 1e-8
LN_EPS = 1e-5
EPS_NORM = 1e-8
EPS_Q = 1e-5
GN_N = float(SEQ * 64)


class Prog:
    ENGS = ("vector", "scalar", "gpsimd", "tensor", "sync")

    def __init__(self, nc, dma_pool=8):
        self.nc = nc
        self.scope = "prep"
        self.q = {e: [] for e in self.ENGS}
        self.cnt = {e: 0 for e in self.ENGS}
        self.dma_pool = dma_pool
        self.dma_idx = {e: 0 for e in self.ENGS}
        self.dma_uses = {}
        self.buf = {}
        self.last_wait = {}
        self.sem_keys = set(self.ENGS)

    def _bufstate(self, h):
        return self.buf.setdefault(id(h), {"w": None, "r": []})

    def _collect(self, eng, reads, writes):
        waits = {}

        def need(dep):
            if dep is None:
                return
            key, val = dep
            if key == eng == "tensor":
                return
            if val <= self.last_wait.get((eng, key), 0):
                return
            if waits.get(key, 0) < val:
                waits[key] = val
        for h in reads:
            need(self._bufstate(h)["w"])
        for h in writes:
            st = self._bufstate(h)
            need(st["w"])
            for r in st["r"]:
                need(r)
        for key, val in waits.items():
            self.last_wait[(eng, key)] = val
        return waits

    def emit(self, eng, fn, reads=(), writes=()):
        waits = self._collect(eng, reads, writes)
        self.cnt[eng] += 1
        me = (eng, self.cnt[eng])
        for h in reads:
            self._bufstate(h)["r"].append(me)
        for h in writes:
            st = self._bufstate(h)
            st["w"] = me
            st["r"] = []
        self.q[eng].append((waits, fn, eng, 1, self.scope))
        return me

    def dma(self, eng, fn, reads=(), writes=()):
        waits = self._collect(eng, reads, writes)
        j = self.dma_idx[eng]
        self.dma_idx[eng] = (j + 1) % self.dma_pool
        key = (eng, "dma", j)
        self.sem_keys.add(key)
        uses = self.dma_uses.get(key, 0)
        if uses > 0:
            v = 16 * uses
            if v > self.last_wait.get((eng, key), 0):
                if waits.get(key, 0) < v:
                    waits[key] = v
                self.last_wait[(eng, key)] = v
        self.dma_uses[key] = uses + 1
        me = (key, 16 * (uses + 1))
        for h in reads:
            self._bufstate(h)["r"].append(me)
        for h in writes:
            st = self._bufstate(h)
            st["w"] = me
            st["r"] = []
        self.q[eng].append((waits, fn, key, 16, self.scope))
        return me

    def barrier_all(self):
        snap = [(e, self.cnt[e]) for e in self.ENGS if self.cnt[e] > 0]
        snap += [(k, 16 * u) for k, u in self.dma_uses.items()]
        for eng in self.ENGS:
            waits = {}
            for key, val in snap:
                if key == eng:
                    continue
                if val > self.last_wait.get((eng, key), 0):
                    waits[key] = val
                    self.last_wait[(eng, key)] = val
            if waits:
                self.q[eng].append((waits, None, None, 0, self.scope))

    def final_wait(self, eng):
        waits = {}
        for key, val in [(e, self.cnt[e]) for e in self.ENGS] + \
                        [(k, 16 * u) for k, u in self.dma_uses.items()]:
            if key == eng or val == 0:
                continue
            if val > self.last_wait.get((eng, key), 0):
                waits[key] = val
                self.last_wait[(eng, key)] = val
        self.q[eng].append((waits, None, None, 0, self.scope))

    def build(self, cx):
        sems = {}
        for key in sorted(self.sem_keys, key=str):
            name = key if isinstance(key, str) else f"{key[0]}_d{key[2]}"
            sems[key] = cx.enter_context(self.nc.semaphore(f"s_{name}"))

        def mk(eng):
            def body(e):
                cur = None
                import contextlib as _cl
                stack = _cl.ExitStack()
                for waits, fn, sem_key, inc, scope in self.q[eng]:
                    if scope != cur:
                        stack.close()
                        stack = _cl.ExitStack()
                        stack.enter_context(self.nc.named_scope(f"{scope}_{eng}"))
                        cur = scope
                    for key, val in sorted(waits.items(), key=str):
                        e.wait_ge(sems[key], val)
                    if fn is not None:
                        fn(e).then_inc(sems[sem_key], inc)
                stack.close()
            return body
        with self.nc.Block() as blk:
            for eng in self.ENGS:
                if self.q[eng]:
                    getattr(blk, eng)(mk(eng))


class Arena:
    TOP = 229344

    def __init__(self, nc, base=0):
        self.nc = nc
        self.off = base
        self.n = 0

    def alloc(self, name, shape, dtype):
        per_part = int(np.prod(shape[1:])) * mybir.dt.size(dtype)
        size32 = (per_part + 31) & ~31
        self.off = ((self.off + 31) & ~31) + size32
        self.n += 1
        assert self.off <= 212500, f"SBUF overflow at {name}: {self.off}"
        addr = self.TOP - self.off
        return self.nc.alloc_sbuf_tensor_at(f"{name}_{self.n}", shape, dtype, offset=addr)

    def mark(self):
        return self.off

    def reset(self, off):
        self.off = off


def build_nc():
    nc = bass.Bass("TRN2", target_bir_lowering=False)
    dx = nc.dram_tensor("x", [BS, SEQ, D], F32, kind="ExternalInput")
    dout = nc.dram_tensor("out", [BS, SEQ, D], F32, kind="ExternalOutput")
    dWr = nc.dram_tensor("W_r", [D, D], F32, kind="ExternalInput")
    dWk = nc.dram_tensor("W_k", [D, D], F32, kind="ExternalInput")
    dWv = nc.dram_tensor("W_v", [D, D], F32, kind="ExternalInput")
    dWo = nc.dram_tensor("W_o", [D, D], F32, kind="ExternalInput")
    dw1 = nc.dram_tensor("w1", [HID, D], F32, kind="ExternalInput")
    dw2 = nc.dram_tensor("w2", [D, HID], F32, kind="ExternalInput")
    dcols = nc.dram_tensor("cols", [128, 64], F32, kind="ExternalInput")
    dn1row = nc.dram_tensor("n1row", [1, D], F32, kind="ExternalInput")
    dn2row = nc.dram_tensor("n2row", [1, HID], F32, kind="ExternalInput")
    dasb = nc.dram_tensor("asb", [2, 1], F32, kind="ExternalInput")
    didf = nc.dram_tensor("idf", [128, 128], F32, kind="ExternalInput")
    didb = nc.dram_tensor("idb", [128, 128], BF16, kind="ExternalInput")
    dhmask = nc.dram_tensor("hmask", [128, 2], F32, kind="ExternalInput")
    dhexp = nc.dram_tensor("hexp", [2, 128], F32, kind="ExternalInput")
    dones = nc.dram_tensor("ones1", [1, 128], F32, kind="ExternalInput")
    donesc = nc.dram_tensor("onesc", [128, 1], F32, kind="ExternalInput")
    drwkv = nc.dram_tensor("rwkv_s", [BS, CT, 128, SEQ], BF16)
    dx1 = nc.dram_tensor("x1_s", [BS, SEQ, D], BF16)
    dhq = nc.dram_tensor("hq_s", [BS, SEQ // 128, 128, HID], BF16)
    dsc = nc.dram_tensor("scal_s", [1, 8], F32)

    p = Prog(nc)
    E, DMA = p.emit, p.dma
    cx = contextlib.ExitStack()
    ar = Arena(nc)

    cols = ar.alloc("cols", [128, 64], F32)
    ccon = ar.alloc("ccon", [128, 24], F32)
    gnsc = ar.alloc("gnsc", [128, 32], F32)
    gnsh = ar.alloc("gnsh", [128, 32], F32)
    n1rep = ar.alloc("n1rep", [128, D], BF16)
    asbc = ar.alloc("asbc", [128, 2], F32)
    idf = ar.alloc("idf", [128, 128], F32)
    idb = ar.alloc("idb", [128, 128], BF16)
    hmask = ar.alloc("hmask", [128, 2], F32)
    hexp = ar.alloc("hexp", [2, 128], F32)
    ones1 = ar.alloc("ones1", [1, 128], F32)
    onesc = ar.alloc("onesc", [128, 1], F32)
    stS = ar.alloc("stS", [128, CT * BS * NCH], F32)
    stQ = ar.alloc("stQ", [128, CT * BS * NCH], F32)
    carry = ar.alloc("carry", [128, CT * BS * 4], F32)
    wsc = ar.alloc("wsc", [128, 4], F32)
    tiny = ar.alloc("tiny", [128, 32], F32)
    ffc = ar.alloc("ffc", [128, 4], F32)
    kq = ar.alloc("kq", [128, 1], F32)
    kn = ar.alloc("kn", [128, 1], F32)
    c05 = ar.alloc("c05", [128, 1], F32)
    cm05 = ar.alloc("cm05", [128, 1], F32)
    RES_END = ar.mark()
    wTr = ar.alloc("wTr", [128, DT * D], BF16)
    wTk = ar.alloc("wTk", [128, DT * D], BF16)
    wTv = ar.alloc("wTv", [128, DT * D], BF16)
    woT = ar.alloc("woT", [128, CT * D], BF16)
    n1tmp = ar.alloc("n1tmp", [1, D], F32)
    REGA_END = ar.mark()

    pA = cx.enter_context(nc.psum_tensor("pA", [128, 1024], F32))
    pB = cx.enter_context(nc.psum_tensor("pB", [128, 1024], F32))
    pC = cx.enter_context(nc.psum_tensor("pC", [128, 512], F32))
    pT = cx.enter_context(nc.psum_tensor("pT", [128, 512], F32))
    pTB = cx.enter_context(nc.psum_tensor("pTB", [128, 512], BF16))
    pTB2 = cx.enter_context(nc.psum_tensor("pTB2", [128, 512], BF16))
    KpA = [("pA", 0), ("pA", 1)]
    KpB = [("pB", 0), ("pB", 1)]

    def stS_col(ct, s, ci):
        c = ct * BS * NCH + s * NCH + ci
        return stS[:, c:c + 1]

    def stQ_col(ct, s, ci):
        c = ct * BS * NCH + s * NCH + ci
        return stQ[:, c:c + 1]

    DMA("sync", lambda e: e.dma_start(cols[:, :], dcols[:, :]), writes=[cols])
    DMA("sync", lambda e: e.dma_start(idf[:, :], didf[:, :]), writes=[idf])
    DMA("sync", lambda e: e.dma_start(idb[:, :], didb[:, :]), writes=[idb])
    DMA("sync", lambda e: e.dma_start(hmask[:, :], dhmask[:, :]), writes=[hmask])
    DMA("sync", lambda e: e.dma_start(hexp[:, :], dhexp[:, :]), writes=[hexp])
    DMA("sync", lambda e: e.dma_start(ones1[:, :], dones[:, :]), writes=[ones1])
    DMA("sync", lambda e: e.dma_start(onesc[:, :], donesc[:, :]), writes=[onesc])
    DMA("sync", lambda e: e.dma_start(asbc[:, 0:1], dasb[0:1, 0:1].broadcast_to([128, 1])), writes=[asbc])
    DMA("sync", lambda e: e.dma_start(asbc[:, 1:2], dasb[1:2, 0:1].broadcast_to([128, 1])), writes=[asbc])
    DMA("sync", lambda e: e.dma_start(n1tmp[:, :], dn1row[:, :]), writes=[n1tmp])

    E("vector", lambda e: e.memset(kq[:, :], EPS_Q), writes=[kq])
    E("vector", lambda e: e.memset(kn[:, :], EPS_NORM), writes=[kn])
    E("vector", lambda e: e.memset(c05[:, :], 0.5), writes=[c05])
    E("vector", lambda e: e.memset(cm05[:, :], -0.5), writes=[cm05])
    E("vector", lambda e: e.tensor_tensor(ffc[:, 0:1], asbc[:, 0:1], asbc[:, 0:1], ALU.mult),
      reads=[asbc], writes=[ffc])
    E("vector", lambda e: e.tensor_single_scalar(ffc[:, 0:1], ffc[:, 0:1], 1.0 / HID, ALU.mult),
      reads=[ffc], writes=[ffc])
    E("vector", lambda e: e.tensor_tensor(ffc[:, 1:2], asbc[:, 0:1], asbc[:, 1:2], ALU.mult),
      reads=[asbc], writes=[ffc])
    E("vector", lambda e: e.tensor_single_scalar(ffc[:, 1:2], ffc[:, 1:2], 2.0 / HID, ALU.mult),
      reads=[ffc], writes=[ffc])
    E("vector", lambda e: e.tensor_tensor(ffc[:, 2:3], asbc[:, 1:2], asbc[:, 1:2], ALU.mult),
      reads=[asbc], writes=[ffc])

    E("scalar", lambda e: e.activation(ccon[:, 16:24], cols[:, 0:8], AF.Exp),
      reads=[cols], writes=[ccon])
    E("scalar", lambda e: e.activation(ccon[:, 0:8], ccon[:, 16:24], AF.Exp, scale=-1.0),
      reads=[ccon], writes=[ccon])
    E("scalar", lambda e: e.activation(ccon[:, 8:16], cols[:, 8:16], AF.Exp),
      reads=[cols], writes=[ccon])

    def Ew_col(ct):
        return ccon[:, ct:ct + 1]

    def Eu_col(ct):
        return ccon[:, 8 + ct:8 + ct + 1]

    def tm_col(m, ct):
        return cols[:, (2 + m) * 8 + ct:(2 + m) * 8 + ct + 1]

    def gam_col(ct):
        return cols[:, 40 + ct:40 + ct + 1]

    def bet_col(ct):
        return cols[:, 48 + ct:48 + ct + 1]

    for h in range(2):
        E("tensor", lambda e, h=h: e.matmul(pT[:, :512], ones1[:, :],
                                            n1tmp[:, h * 512:(h + 1) * 512], start=True, stop=True),
          reads=[ones1, n1tmp], writes=[pT])
        E("vector", lambda e, h=h: e.tensor_copy(n1rep[:, h * 512:(h + 1) * 512], pT[:, :512]),
          reads=[pT], writes=[n1rep])

    ps_mark = ar.mark()
    ldAb = [ar.alloc(f"ldA{i}", [128, D], F32) for i in range(2)]
    u1pb = [ar.alloc(f"u1p{i}", [128, HID], F32) for i in range(2)]
    wabs = ar.alloc("wabs", [128, 4], F32)
    wabt = ar.alloc("wabt", [128, 2], F32)

    def evac(engname, fn, reads, writes):
        E(engname, fn, reads=reads, writes=writes)

    for wi, (dw, wT) in enumerate([(dWr, wTr), (dWk, wTk), (dWv, wTv), (dWo, woT)]):
        for s in range(8):
            ldA = ldAb[s % 2]
            DMA("sync", lambda e, dw=dw, s=s, ldA=ldA: e.dma_start(ldA[:, :D], dw[s * 128:(s + 1) * 128, :]),
                writes=[ldA])
            for dt in range(DT):
                sl = dt % 4
                E("tensor", lambda e, dt=dt, sl=sl, ldA=ldA: e.transpose(
                    pT[:, sl * 128:(sl + 1) * 128], ldA[:, dt * 128:(dt + 1) * 128], idf[:, :]),
                    reads=[ldA, idf], writes=[pT])
                eng = "vector" if dt % 2 == 0 else "scalar"
                dst = wT[:, dt * D + s * 128:dt * D + (s + 1) * 128]
                if eng == "vector":
                    E(eng, lambda e, dst=dst, sl=sl: e.tensor_copy(dst, pT[:, sl * 128:(sl + 1) * 128]),
                      reads=[pT], writes=[wT])
                else:
                    E(eng, lambda e, dst=dst, sl=sl: e.copy(dst, pT[:, sl * 128:(sl + 1) * 128]),
                      reads=[pT], writes=[wT])

    for wi, (dw, nslab, width) in enumerate([(dw1, 32, D), (dw2, 8, HID)]):
        for s in range(nslab):
            ldA = u1pb[s % 2]
            tcol = wabt[:, s % 2:s % 2 + 1]
            DMA("sync", lambda e, dw=dw, s=s, width=width, ldA=ldA: e.dma_start(ldA[:, :width], dw[s * 128:(s + 1) * 128, :]),
                writes=[ldA])
            E("vector", lambda e, width=width, ldA=ldA, tcol=tcol: e.tensor_reduce(tcol, ldA[:, :width], AX, ALU.add,
                                                               apply_absolute_value=True),
              reads=[ldA], writes=[wabt])
            if s == 0:
                E("vector", lambda e, wi=wi, tcol=tcol: e.tensor_copy(wabs[:, wi:wi + 1], tcol),
                  reads=[wabt], writes=[wabs])
            else:
                E("vector", lambda e, wi=wi, tcol=tcol: e.tensor_add(wabs[:, wi:wi + 1], wabs[:, wi:wi + 1], tcol),
                  reads=[wabt, wabs], writes=[wabs])
    for wi in range(2):
        E("tensor", lambda e, wi=wi: e.matmul(pT[0:1, 0:1], wabs[:, wi:wi + 1], onesc[:, :],
                                              start=True, stop=True),
          reads=[wabs, onesc], writes=[pT])
        E("vector", lambda e, wi=wi: e.tensor_scalar(tiny[0:1, 16 + wi:17 + wi], pT[0:1, 0:1],
                                                     1.0 / (HID * D), kq[0:1, 0:1], ALU.mult, ALU.max),
          reads=[pT], writes=[tiny])
        E("vector", lambda e, wi=wi: e.reciprocal(tiny[0:1, 18 + wi:19 + wi], tiny[0:1, 16 + wi:17 + wi]),
          reads=[tiny], writes=[tiny])
    DMA("scalar", lambda e: e.dma_start(dsc[0:1, 0:4], tiny[0:1, 16:20]), reads=[tiny], writes=[dsc])
    for j in range(4):
        DMA("sync", lambda e, j=j: e.dma_start(wsc[:, j:j + 1], dsc[0:1, j:j + 1].broadcast_to([128, 1])),
            reads=[dsc], writes=[wsc])

    p.barrier_all()
    ar.reset(ps_mark)

    p.scope = "s1"
    s1_mark = ar.mark()
    xT = [[ar.alloc(f"xT{g}_{dt}", [128, 1 + F], BF16) for dt in range(DT)] for g in range(2)]
    xmT = [[ar.alloc(f"xmT{m}_{dt}", [128, F], BF16) for dt in range(DT)] for m in range(3)]
    xin = [ar.alloc(f"xin{i}", [128, D], F32) for i in range(2)]
    difb = ar.alloc("difb", [128, F], BF16)
    NS1 = 3
    Ge = [ar.alloc(f"Ge{i}", [128, 1 + F], F32) for i in range(NS1)]
    Gte = [ar.alloc(f"Gte{i}", [128, 1 + F], F32) for i in range(NS1)]
    Ne = [ar.alloc(f"Ne{i}", [128, 1 + F], F32) for i in range(NS1)]
    SCR = [{n: ar.alloc(f"{n}{i}", [128, F], F32) for n in
            ["ek", "euk", "dD", "mM", "t1", "tq", "rf", "ts1"]} for i in range(NS1)]
    rwb = [ar.alloc(f"rwb{i}", [128, F], BF16) for i in range(NS1)]
    scrapg = ar.alloc("scrapg", [128, F], F32)
    xlast = ar.alloc("xlast", [128, BS * DT], BF16)
    S1_END = ar.mark()

    E("vector", lambda e: e.memset(carry[:, :], 0.0), writes=[carry])

    for ci in range(NCH):
        for s in range(BS):
            g = (ci * BS + s) % 2
            xTg = xT[g]
            for tt in range(F // 128):
                xi = xin[tt % 2]
                t0 = ci * F + tt * 128
                DMA("sync", lambda e, s=s, t0=t0, xi=xi: e.dma_start(xi[:, :], dx[s, t0:t0 + 128, :]),
                    writes=[xi])
                for dt in range(DT):
                    sl = dt % 4
                    E("tensor", lambda e, xi=xi, dt=dt, sl=sl: e.transpose(
                        pT[:, sl * 128:(sl + 1) * 128], xi[:, dt * 128:(dt + 1) * 128], idf[:, :]),
                        reads=[xi, idf], writes=[pT])
                    dst = xTg[dt][:, 1 + tt * 128:1 + (tt + 1) * 128]
                    E("scalar", lambda e, dst=dst, sl=sl: e.copy(dst, pT[:, sl * 128:(sl + 1) * 128]),
                      reads=[pT], writes=[xTg[dt]])
            for dt in range(DT):
                if ci == 0:
                    E("vector", lambda e, dt=dt, xTg=xTg: e.memset(xTg[dt][:, 0:1], 0.0), writes=[xTg[dt]])
                else:
                    E("gpsimd", lambda e, dt=dt, s=s, xTg=xTg: e.tensor_copy(
                        xTg[dt][:, 0:1], xlast[:, s * DT + dt:s * DT + dt + 1]),
                        reads=[xlast], writes=[xTg[dt]])
                E("gpsimd", lambda e, dt=dt, s=s, xTg=xTg: e.tensor_copy(
                    xlast[:, s * DT + dt:s * DT + dt + 1], xTg[dt][:, F:F + 1]),
                    reads=[xTg[dt]], writes=[xlast])
            for dt in range(DT):
                E("vector", lambda e, dt=dt, xTg=xTg: e.tensor_tensor(
                    difb[:, :], xTg[dt][:, 1:1 + F], xTg[dt][:, 0:F], ALU.subtract),
                    reads=[xTg[dt]], writes=[difb])
                for m in range(3):
                    E("vector", lambda e, dt=dt, m=m, xTg=xTg: e.scalar_tensor_tensor(
                        xmT[m][dt][:, :], difb[:, :], tm_col(m, dt), xTg[dt][:, 0:F],
                        ALU.mult, ALU.add),
                        reads=[difb, cols, xTg[dt]], writes=[xmT[m][dt]])
            for ct in range(CT):
                r2 = ct % NS1
                sc = SCR[r2]
                ge, gte, ne = Ge[r2], Gte[r2], Ne[r2]
                h2i = ct % 2
                pk = pA[:, h2i * 512:h2i * 512 + F]
                pv = pB[:, h2i * 512:h2i * 512 + F]
                pr = pC
                kk, kv, kr = KpA[h2i], KpB[h2i], pC
                for wT, pdst, pkey, mix in [(wTk, pk, kk, 1), (wTv, pv, kv, 2), (wTr, pr, kr, 0)]:
                    for dt in range(DT):
                        E("tensor", lambda e, wT=wT, pdst=pdst, mix=mix, dt=dt, ct=ct:
                          e.matmul(pdst[:, 0:F],
                                   wT[:, dt * D + ct * 128:dt * D + (ct + 1) * 128],
                                   xmT[mix][dt][:, :],
                                   start=(dt == 0), stop=(dt == DT - 1)),
                          reads=[wT, xmT[mix][dt]], writes=[pkey])
                co = (ct * BS + s) * 4
                E("gpsimd", lambda e, ge=ge, co=co: e.tensor_copy(ge[:, 0:1], carry[:, co:co + 1]),
                  reads=[carry], writes=[ge])
                E("gpsimd", lambda e, gte=gte, co=co: e.tensor_copy(gte[:, 0:1], carry[:, co + 1:co + 2]),
                  reads=[carry], writes=[gte])
                E("gpsimd", lambda e, ne=ne, co=co: e.tensor_copy(ne[:, 0:1], carry[:, co + 2:co + 3]),
                  reads=[carry], writes=[ne])
                E("scalar", lambda e, sc=sc, pk=pk: e.activation(sc["ek"][:, :], pk[:, 0:F], AF.Exp),
                  reads=[kk], writes=[sc["ek"]])
                E("scalar", lambda e, sc=sc, ct=ct: e.activation(sc["euk"][:, :], sc["ek"][:, :], AF.Copy, scale=Eu_col(ct)),
                  reads=[sc["ek"], ccon], writes=[sc["euk"]])
                E("scalar", lambda e, sc=sc, pr=pr: e.activation(sc["ts1"][:, :], pr[:, 0:F], AF.Exp, scale=-1.0),
                  reads=[pr], writes=[sc["ts1"]])
                E("scalar", lambda e, sc=sc: e.activation(sc["ts1"][:, :], sc["ts1"][:, :], AF.Ln, bias=1.0),
                  reads=[sc["ts1"]], writes=[sc["ts1"]])
                E("scalar", lambda e, sc=sc: e.activation(sc["ts1"][:, :], sc["ts1"][:, :], AF.Exp, scale=-1.0),
                  reads=[sc["ts1"]], writes=[sc["ts1"]])
                E("vector", lambda e, sc=sc, ge=ge, ct=ct: e.tensor_tensor_scan(
                    ge[:, 1:1 + F], Ew_col(ct).broadcast_to([128, F]), sc["ek"][:, :],
                    ge[:, 0:1], ALU.mult, ALU.add),
                    reads=[ccon, sc["ek"], ge], writes=[ge])
                E("vector", lambda e, sc=sc, ge=ge, ct=ct: e.tensor_single_scalar(
                    sc["dD"][:, :], ge[:, 0:F], Ew_col(ct), ALU.mult),
                    reads=[ge, ccon], writes=[sc["dD"]])
                E("vector", lambda e, sc=sc: e.tensor_tensor(sc["mM"][:, :], sc["dD"][:, :], sc["ek"][:, :], ALU.max),
                  reads=[sc["dD"], sc["ek"]], writes=[sc["mM"]])
                E("scalar", lambda e, sc=sc, ge=ge: e.activation(sc["t1"][:, :], ge[:, 1:1 + F], AF.Ln),
                  reads=[ge], writes=[sc["t1"]])
                E("scalar", lambda e, sc=sc: e.activation(sc["t1"][:, :], sc["t1"][:, :], AF.Exp, scale=-1.0),
                  reads=[sc["t1"]], writes=[sc["t1"]])
                E("vector", lambda e, sc=sc: e.tensor_tensor(sc["tq"][:, :], sc["mM"][:, :], sc["t1"][:, :], ALU.mult),
                  reads=[sc["mM"], sc["t1"]], writes=[sc["tq"]])
                E("vector", lambda e, sc=sc: e.tensor_scalar(sc["tq"][:, :], sc["tq"][:, :], EPS, 1.0, ALU.mult, ALU.add),
                  reads=[sc["tq"]], writes=[sc["tq"]])
                E("vector", lambda e, sc=sc, ct=ct: e.tensor_single_scalar(sc["dD"][:, :], sc["tq"][:, :], Ew_col(ct), ALU.mult),
                  reads=[sc["tq"], ccon], writes=[sc["dD"]])
                E("vector", lambda e, sc=sc: e.tensor_tensor(sc["mM"][:, :], sc["tq"][:, :], sc["ek"][:, :], ALU.mult),
                  reads=[sc["tq"], sc["ek"]], writes=[sc["mM"]])
                E("vector", lambda e, sc=sc, gte=gte: e.tensor_tensor_scan(
                    gte[:, 1:1 + F], sc["dD"][:, :], sc["mM"][:, :], gte[:, 0:1], ALU.mult, ALU.add),
                    reads=[sc["dD"], sc["mM"], gte], writes=[gte])
                E("vector", lambda e, sc=sc, gte=gte, ct=ct: e.tensor_single_scalar(
                    sc["dD"][:, :], gte[:, 0:F], Ew_col(ct), ALU.mult),
                    reads=[gte, ccon], writes=[sc["dD"]])
                E("vector", lambda e, sc=sc: e.tensor_tensor(sc["mM"][:, :], sc["dD"][:, :], sc["ek"][:, :], ALU.max),
                  reads=[sc["dD"], sc["ek"]], writes=[sc["mM"]])
                E("vector", lambda e, sc=sc: e.tensor_tensor(sc["tq"][:, :], sc["dD"][:, :], sc["ek"][:, :], ALU.add),
                  reads=[sc["dD"], sc["ek"]], writes=[sc["tq"]])
                E("scalar", lambda e, sc=sc: e.activation(sc["t1"][:, :], sc["mM"][:, :], AF.Ln),
                  reads=[sc["mM"]], writes=[sc["t1"]])
                E("scalar", lambda e, sc=sc: e.activation(sc["t1"][:, :], sc["t1"][:, :], AF.Exp, scale=-1.0),
                  reads=[sc["t1"]], writes=[sc["t1"]])
                E("vector", lambda e, sc=sc: e.tensor_tensor(sc["rf"][:, :], sc["tq"][:, :], sc["t1"][:, :], ALU.mult),
                  reads=[sc["tq"], sc["t1"]], writes=[sc["rf"]])
                E("vector", lambda e, sc=sc, ct=ct: e.tensor_scalar(sc["dD"][:, :], sc["rf"][:, :], EPS, Ew_col(ct), ALU.add, ALU.mult),
                  reads=[sc["rf"], ccon], writes=[sc["dD"]])
                E("vector", lambda e, sc=sc, pv=pv: e.tensor_tensor(sc["mM"][:, :], sc["ek"][:, :], pv[:, 0:F], ALU.mult),
                  reads=[sc["ek"], kv], writes=[sc["mM"]])
                E("vector", lambda e, sc=sc: e.scalar_tensor_tensor(sc["tq"][:, :], sc["rf"][:, :], EPS, sc["mM"][:, :], ALU.add, ALU.mult),
                  reads=[sc["rf"], sc["mM"]], writes=[sc["tq"]])
                E("vector", lambda e, sc=sc, ne=ne: e.tensor_tensor_scan(
                    ne[:, 1:1 + F], sc["dD"][:, :], sc["tq"][:, :], ne[:, 0:1], ALU.mult, ALU.add),
                    reads=[sc["dD"], sc["tq"], ne], writes=[ne])
                E("vector", lambda e, sc=sc, pv=pv: e.tensor_tensor(sc["mM"][:, :], sc["euk"][:, :], pv[:, 0:F], ALU.mult),
                  reads=[sc["euk"], kv], writes=[sc["mM"]])
                E("vector", lambda e, sc=sc, ne=ne: e.tensor_tensor(sc["tq"][:, :], ne[:, 0:F], sc["mM"][:, :], ALU.add),
                  reads=[ne, sc["mM"]], writes=[sc["tq"]])
                E("vector", lambda e, sc=sc, gte=gte: e.tensor_tensor(sc["dD"][:, :], gte[:, 0:F], sc["euk"][:, :], ALU.add),
                  reads=[gte, sc["euk"]], writes=[sc["dD"]])
                E("scalar", lambda e, sc=sc: e.activation(sc["t1"][:, :], sc["dD"][:, :], AF.Ln),
                  reads=[sc["dD"]], writes=[sc["t1"]])
                E("scalar", lambda e, sc=sc: e.activation(sc["t1"][:, :], sc["t1"][:, :], AF.Exp, scale=-1.0),
                  reads=[sc["t1"]], writes=[sc["t1"]])
                E("vector", lambda e, sc=sc: e.tensor_tensor(sc["rf"][:, :], sc["tq"][:, :], sc["t1"][:, :], ALU.mult),
                  reads=[sc["tq"], sc["t1"]], writes=[sc["rf"]])
                rw = rwb[r2]
                E("vector", lambda e, sc=sc, rw=rw: e.tensor_tensor(rw[:, :], sc["rf"][:, :], sc["ts1"][:, :], ALU.mult),
                  reads=[sc["rf"], sc["ts1"]], writes=[rw])
                E("gpsimd", lambda e, ge=ge, co=co: e.tensor_copy(carry[:, co:co + 1], ge[:, F:F + 1]),
                  reads=[ge], writes=[carry])
                E("gpsimd", lambda e, gte=gte, co=co: e.tensor_copy(carry[:, co + 1:co + 2], gte[:, F:F + 1]),
                  reads=[gte], writes=[carry])
                E("gpsimd", lambda e, ne=ne, co=co: e.tensor_copy(carry[:, co + 2:co + 3], ne[:, F:F + 1]),
                  reads=[ne], writes=[carry])
                E("vector", lambda e, rw=rw, ct=ct, s=s, ci=ci: e.tensor_reduce(
                    stS_col(ct, s, ci), rw[:, :], AX, ALU.add),
                    reads=[rw], writes=[stS])
                E("scalar", lambda e, rw=rw, ct=ct, s=s, ci=ci: e.activation(
                    scrapg[:, :], rw[:, :], AF.Square, accum_out=stQ_col(ct, s, ci)),
                    reads=[rw], writes=[scrapg, stQ])
                DMA("scalar", lambda e, rw=rw, s=s, ct=ct, ci=ci: e.dma_start(
                    drwkv[s, ct, :, ci * F:(ci + 1) * F], rw[:, :]),
                    reads=[rw], writes=[drwkv])

    p.barrier_all()
    ar.reset(s1_mark)

    p.scope = "s2a"
    s2a_mark = ar.mark()
    rwin = [ar.alloc(f"rwin{i}", [128, CT * 512], BF16) for i in range(2)]
    nrm = [ar.alloc(f"nrm{i}", [128, CT * 512], BF16) for i in range(2)]
    x2in = [ar.alloc(f"x2in{i}", [128, D], F32) for i in range(2)]
    h1t = [ar.alloc(f"h1t{i}", [128, D], F32) for i in range(2)]
    x1bt = [ar.alloc(f"x1b{i}", [128, D], BF16) for i in range(2)]
    lnstb = [ar.alloc(f"lnst{i}", [128, 8], F32) for i in range(2)]
    scr2b = [ar.alloc(f"scr2{i}", [128, D], F32) for i in range(2)]
    hstat = ar.alloc("hstat", [2, 8], F32)
    S2A_END = ar.mark()

    for ct in range(CT):
        for s in range(BS):
            c0 = ct * BS * NCH + s * NCH
            E("vector", lambda e, c0=c0: e.tensor_reduce(tiny[:, 0:1], stS[:, c0:c0 + NCH], AX, ALU.add),
              reads=[stS], writes=[tiny])
            E("vector", lambda e, c0=c0: e.tensor_reduce(tiny[:, 1:2], stQ[:, c0:c0 + NCH], AX, ALU.add),
              reads=[stQ], writes=[tiny])
            E("tensor", lambda e: e.matmul(pT[0:2, 0:2], hmask[:, :], tiny[:, 0:2],
                                           start=True, stop=True),
              reads=[hmask, tiny], writes=[pT])
            E("vector", lambda e: e.tensor_single_scalar(hstat[0:2, 0:2], pT[0:2, 0:2], 1.0 / GN_N, ALU.mult),
              reads=[pT], writes=[hstat])
            E("vector", lambda e: e.tensor_tensor(hstat[0:2, 2:3], hstat[0:2, 0:1], hstat[0:2, 0:1], ALU.mult),
              reads=[hstat], writes=[hstat])
            E("vector", lambda e: e.tensor_tensor(hstat[0:2, 2:3], hstat[0:2, 1:2], hstat[0:2, 2:3], ALU.subtract),
              reads=[hstat], writes=[hstat])
            E("scalar", lambda e: e.activation(hstat[0:2, 3:4], hstat[0:2, 2:3], AF.Ln, bias=kq[0:2, 0:1]),
              reads=[hstat, kq], writes=[hstat])
            E("scalar", lambda e: e.activation(hstat[0:2, 3:4], hstat[0:2, 3:4], AF.Exp, scale=-0.5),
              reads=[hstat], writes=[hstat])
            E("tensor", lambda e: e.matmul(pT[:, 4:5], hexp[:, :], hstat[0:2, 3:4], start=True, stop=True),
              reads=[hexp, hstat], writes=[pT])
            E("tensor", lambda e: e.matmul(pT[:, 5:6], hexp[:, :], hstat[0:2, 0:1], start=True, stop=True),
              reads=[hexp, hstat], writes=[pT])
            col = ct * BS + s
            E("vector", lambda e, ct=ct, col=col: e.tensor_single_scalar(
                gnsc[:, col:col + 1], pT[:, 4:5], gam_col(ct), ALU.mult),
                reads=[pT, cols], writes=[gnsc])
            E("vector", lambda e, col=col: e.tensor_tensor(
                tiny[:, 2:3], pT[:, 5:6], gnsc[:, col:col + 1], ALU.mult),
                reads=[pT, gnsc], writes=[tiny])
            E("vector", lambda e, ct=ct, col=col: e.tensor_scalar(
                gnsh[:, col:col + 1], tiny[:, 2:3], -1.0, bet_col(ct), ALU.mult, ALU.add),
                reads=[tiny, cols], writes=[gnsh])

    for s in range(BS):
        for gq in range(SEQ // 512):
            g = (s * (SEQ // 512) + gq) % 2
            rw, nm = rwin[g], nrm[g]
            for ct in range(CT):
                DMA("scalar", lambda e, rw=rw, ct=ct, s=s, gq=gq: e.dma_start(
                    rw[:, ct * 512:(ct + 1) * 512], drwkv[s, ct, :, gq * 512:(gq + 1) * 512]),
                    reads=[drwkv], writes=[rw])
                col = ct * BS + s
                E("vector", lambda e, rw=rw, nm=nm, ct=ct, col=col: e.tensor_scalar(
                    nm[:, ct * 512:(ct + 1) * 512], rw[:, ct * 512:(ct + 1) * 512],
                    gnsc[:, col:col + 1], gnsh[:, col:col + 1], ALU.mult, ALU.add),
                    reads=[rw, gnsc, gnsh], writes=[nm])
            def s2a_p1(tt):
                xi = x2in[tt % 2]
                h1 = h1t[tt % 2]
                lnst = lnstb[tt % 2]
                scr2 = scr2b[tt % 2]
                t0 = gq * 512 + tt * 128
                DMA("sync", lambda e, xi=xi, s=s, t0=t0: e.dma_start(xi[:, :], dx[s, t0:t0 + 128, :]),
                    writes=[xi])
                pat = pA if tt % 2 == 0 else pB
                for ct in range(CT):
                    for hh in range(2):
                        E("tensor", lambda e, nm=nm, ct=ct, tt=tt, hh=hh, pat=pat: e.matmul(
                            pat[:, hh * 512:(hh + 1) * 512],
                            nm[:, ct * 512 + tt * 128:ct * 512 + (tt + 1) * 128],
                            woT[:, ct * D + hh * 512:ct * D + (hh + 1) * 512],
                            start=(ct == 0), stop=(ct == CT - 1)),
                            reads=[nm, woT], writes=[pat])
                E("vector", lambda e, xi=xi, h1=h1, pat=pat: e.tensor_tensor(h1[:, :], xi[:, :], pat[:, :], ALU.add),
                  reads=[xi, pat], writes=[h1])
                E("vector", lambda e, h1=h1, lnst=lnst: e.tensor_reduce(lnst[:, 0:1], h1[:, :], AX, ALU.add),
                  reads=[h1], writes=[lnst])
                E("scalar", lambda e, h1=h1, scr2=scr2, lnst=lnst: e.activation(scr2[:, :], h1[:, :], AF.Square, accum_out=lnst[:, 1:2]),
                  reads=[h1], writes=[scr2, lnst])
                E("vector", lambda e, lnst=lnst: e.tensor_single_scalar(lnst[:, 2:3], lnst[:, 0:1], 1.0 / D, ALU.mult),
                  reads=[lnst], writes=[lnst])
                E("vector", lambda e, lnst=lnst: e.tensor_tensor(lnst[:, 3:4], lnst[:, 2:3], lnst[:, 2:3], ALU.mult),
                  reads=[lnst], writes=[lnst])
                E("vector", lambda e, lnst=lnst: e.tensor_single_scalar(lnst[:, 4:5], lnst[:, 1:2], 1.0 / D, ALU.mult),
                  reads=[lnst], writes=[lnst])
                E("vector", lambda e, lnst=lnst: e.tensor_tensor(lnst[:, 4:5], lnst[:, 4:5], lnst[:, 3:4], ALU.subtract),
                  reads=[lnst], writes=[lnst])
                E("scalar", lambda e, lnst=lnst: e.activation(lnst[:, 5:6], lnst[:, 4:5], AF.Ln, bias=kq[:, 0:1]),
                  reads=[lnst, kq], writes=[lnst])
                E("scalar", lambda e, lnst=lnst: e.activation(lnst[:, 5:6], lnst[:, 5:6], AF.Exp, scale=-0.5),
                  reads=[lnst], writes=[lnst])

            def s2a_p2(tt):
                h1 = h1t[tt % 2]
                xb = x1bt[tt % 2]
                lnst = lnstb[tt % 2]
                t0 = gq * 512 + tt * 128
                E("vector", lambda e, lnst=lnst: e.tensor_single_scalar(lnst[:, 6:7], lnst[:, 2:3], -1.0, ALU.mult),
                  reads=[lnst], writes=[lnst])
                E("vector", lambda e, h1=h1, xb=xb, lnst=lnst: e.tensor_scalar(
                    xb[:, :], h1[:, :], lnst[:, 6:7], lnst[:, 5:6], ALU.add, ALU.mult),
                    reads=[h1, lnst], writes=[xb])
                DMA("scalar", lambda e, xb=xb, s=s, t0=t0: e.dma_start(dx1[s, t0:t0 + 128, :], xb[:, :]),
                    reads=[xb], writes=[dx1])

            for tt in range(5):
                if tt < 4:
                    s2a_p1(tt)
                if tt >= 1:
                    s2a_p2(tt - 1)

    p.barrier_all()
    ar.reset(RES_END)

    p.scope = "s2b"
    s2b_mark = ar.mark()
    hscst = ar.alloc("hscst", [128, 64], F32)
    s2b2_mark = ar.mark()
    w1T = ar.alloc("w1T", [128, DT * HID], BF16)
    n2rep = ar.alloc("n2rep", [128, HID], BF16)
    n2tmp = ar.alloc("n2tmp", [1, 512], F32)
    HH = HID // 2
    Txb = [ar.alloc(f"p1x1_{i}", [128, D], BF16) for i in range(2)]
    Txnb = [ar.alloc(f"p1xn_{i}", [128, D], BF16) for i in range(2)]
    Tyf = [ar.alloc(f"p1yf_{i}", [128, D], BF16) for i in range(2)]
    TxqT = [ar.alloc(f"p1xq_{i}", [128, D], BF16) for i in range(3)]
    Tyi = [ar.alloc(f"p1yi_{i}", [128, D], I32) for i in range(4)]
    Thyq = [ar.alloc(f"p1hq_{i}", [128, HH], BF16) for i in range(2)]
    Thyf = [ar.alloc(f"p1hf_{i}", [128, HH], BF16) for i in range(3)]
    Thu1 = [ar.alloc(f"p1hu_{i}", [128, HH], BF16) for i in range(3)]
    Threl = [ar.alloc(f"p1hr_{i}", [128, HH], F32) for i in range(2)]
    ThrelB = [ar.alloc(f"p1hB_{i}", [128, HH], F32) for i in range(2)]
    ThqS = [[ar.alloc(f"p1hs_{i}_{k}", [128, 512], BF16) for k in range(2)] for i in range(2)]
    Tq = [ar.alloc(f"p1q_{i}", [128, 32], F32) for i in range(4)]
    Tdif = [ar.alloc(f"p1df_{i}", [128, HH], BF16) for i in range(2)]
    asn2 = ar.alloc("asn2", [128, HID], BF16)
    dumS = ar.alloc("dumS", [128, 2048], BF16)
    S2B1_END = ar.mark()

    wldb = [Threl[0][:, 0:D], Threl[0][:, D:2 * D]]
    wu1b = [Thyq[0][:, 0:D], Thyq[0][:, D:2 * D]]
    wu2b = [Thyf[0][:, 0:D], Thyf[0][:, D:2 * D]]
    wtnb = [Thu1[0][:, 0:D], Thu1[0][:, D:2 * D]]
    wkeys = [Threl[0], Thyq[0], Thyf[0], Thu1[0]]
    for s in range(32):
        wld, wu1, wu2, wtn = wldb[s % 2], wu1b[s % 2], wu2b[s % 2], wtnb[s % 2]
        DMA("sync", lambda e, s=s, wld=wld: e.dma_start(wld[:, :], dw1[s * 128:(s + 1) * 128, :]),
            writes=[wkeys[0]])
        E("scalar", lambda e, wld=wld: e.activation(wld[:, :], wld[:, :], AF.Copy, scale=wsc[:, 2:3]),
          reads=[wkeys[0], wsc], writes=[wkeys[0]])
        E("vector", lambda e, wld=wld, wu1=wu1: e.tensor_single_scalar(wu1[:, :], wld[:, :], 0.5, ALU.is_gt),
          reads=[wkeys[0]], writes=[wkeys[1]])
        E("vector", lambda e, wld=wld, wu2=wu2: e.tensor_single_scalar(wu2[:, :], wld[:, :], -0.5, ALU.is_lt),
          reads=[wkeys[0]], writes=[wkeys[2]])
        E("vector", lambda e, wu1=wu1, wu2=wu2, wtn=wtn: e.tensor_tensor(wtn[:, :], wu1[:, :], wu2[:, :], ALU.subtract),
          reads=[wkeys[1], wkeys[2]], writes=[wkeys[3]])
        for bb in range(8):
            sl = bb % 4
            ptb = pTB if (bb // 4) % 2 == 0 else pTB2
            E("tensor", lambda e, bb=bb, sl=sl, wtn=wtn, ptb=ptb: e.transpose(
                ptb[:, sl * 128:(sl + 1) * 128], wtn[:, bb * 128:(bb + 1) * 128], idb[:, :]),
                reads=[wkeys[3], idb], writes=[ptb])
            if bb % 4 == 3:
                eng = "vector" if (bb // 4) % 2 == 0 else "scalar"
                for k in range(4):
                    dstk = w1T[:, (bb - 3 + k) * HID + s * 128:(bb - 3 + k) * HID + (s + 1) * 128]
                    if eng == "vector":
                        E(eng, lambda e, dstk=dstk, k=k, ptb=ptb: e.tensor_copy(dstk, ptb[:, k * 128:(k + 1) * 128]),
                          reads=[ptb], writes=[w1T])
                    else:
                        E(eng, lambda e, dstk=dstk, k=k, ptb=ptb: e.copy(dstk, ptb[:, k * 128:(k + 1) * 128]),
                          reads=[ptb], writes=[w1T])
    for h in range(HID // 512):
        DMA("sync", lambda e, h=h: e.dma_start(n2tmp[:, :], dn2row[:, h * 512:(h + 1) * 512]),
            writes=[n2tmp])
        E("tensor", lambda e, h=h: e.matmul(pT[:, :512], ones1[:, :], n2tmp[:, :], start=True, stop=True),
          reads=[ones1, n2tmp], writes=[pT])
        E("vector", lambda e, h=h: e.tensor_copy(n2rep[:, h * 512:(h + 1) * 512], pT[:, :512]),
          reads=[pT], writes=[n2rep])
    E("vector", lambda e: e.tensor_single_scalar(asn2[:, :], n2rep[:, :], asbc[:, 0:1], ALU.mult),
      reads=[n2rep, asbc], writes=[asn2])
    E("vector", lambda e: e.reciprocal(ffc[:, 3:4], asbc[:, 0:1]), reads=[asbc], writes=[ffc])
    E("vector", lambda e: e.tensor_tensor(ffc[:, 3:4], ffc[:, 3:4], asbc[:, 1:2], ALU.mult),
      reads=[ffc, asbc], writes=[ffc])

    NTC = SEQ // 128
    NUN = BS * NTC
    pCb = pC[:, :].bitcast(BF16)

    def p1_Apre(u):
        s, t0 = u // NTC, (u % NTC) * 128
        xb = Txb[u % 2]
        DMA("sync", lambda e, xb=xb, s=s, t0=t0: e.dma_start(xb[:, :], dx1[s, t0:t0 + 128, :]),
            reads=[dx1], writes=[xb])

    def p1_A0(u):
        xb, q = Txb[u % 2], Tq[u % 4]
        E("scalar", lambda e, xb=xb, q=q: e.activation(dumS[:, 0:D], xb[:, :], AF.Square, accum_out=q[:, 0:1]),
          reads=[xb], writes=[dumS, q])
        E("scalar", lambda e, q=q: e.activation(q[:, 1:2], q[:, 0:1], AF.Ln, bias=kn[:, 0:1], scale=1.0 / D),
          reads=[q, kn], writes=[q])
        E("scalar", lambda e, q=q: e.activation(q[:, 1:2], q[:, 1:2], AF.Exp, scale=-0.5),
          reads=[q], writes=[q])

    def p1_Av(u):
        xb, xnb, yf, yi, q = Txb[u % 2], Txnb[u % 2], Tyf[u % 2], Tyi[u % 4], Tq[u % 4]
        E("vector", lambda e, xb=xb, xnb=xnb: e.tensor_tensor(xnb[:, :], xb[:, :], n1rep[:, :], ALU.mult),
          reads=[xb, n1rep], writes=[xnb])
        E("vector", lambda e, xnb=xnb, q=q: e.tensor_reduce(q[:, 2:3], xnb[:, :], AX, ALU.max, apply_absolute_value=True),
          reads=[xnb], writes=[q])
        E("vector", lambda e, q=q: e.tensor_scalar(q[:, 3:4], q[:, 2:3], q[:, 1:2], kq[:, 0:1], ALU.mult, ALU.max),
          reads=[q, kq], writes=[q])
        E("vector", lambda e, q=q: e.reciprocal(q[:, 4:5], q[:, 3:4]), reads=[q], writes=[q])
        E("vector", lambda e, q=q: e.tensor_scalar(q[:, 6:7], q[:, 4:5], 127.0, q[:, 1:2], ALU.mult, ALU.mult),
          reads=[q], writes=[q])
        E("vector", lambda e, q=q: e.tensor_scalar(q[:, 7:8], q[:, 3:4], 1.0 / 127.0, wsc[:, 0:1], ALU.mult, ALU.mult),
          reads=[q, wsc], writes=[q])
        E("vector", lambda e, xnb=xnb, q=q: e.tensor_single_scalar(xnb[:, :], xnb[:, :], q[:, 6:7], ALU.mult),
          reads=[xnb, q], writes=[xnb])
        E("vector", lambda e, xnb=xnb, yi=yi: e.tensor_copy(yi[:, :], xnb[:, :]), reads=[xnb], writes=[yi])
        E("vector", lambda e, yi=yi, yf=yf: e.tensor_copy(yf[:, :], yi[:, :]), reads=[yi], writes=[yf])
        E("vector", lambda e, xnb=xnb, yf=yf: e.tensor_tensor(xnb[:, :], xnb[:, :], yf[:, :], ALU.subtract),
          reads=[xnb, yf], writes=[xnb])
        E("vector", lambda e, xnb=xnb, yf=yf: e.scalar_tensor_tensor(
            yf[:, :], xnb[:, :], c05[:, 0:1], yf[:, :], ALU.is_gt, ALU.add),
          reads=[xnb, c05, yf], writes=[yf])
        E("vector", lambda e, xnb=xnb, yf=yf: e.scalar_tensor_tensor(
            yf[:, :], xnb[:, :], cm05[:, 0:1], yf[:, :], ALU.is_ge, ALU.add),
          reads=[xnb, cm05, yf], writes=[yf])
        E("vector", lambda e, yf=yf: e.tensor_single_scalar(yf[:, :], yf[:, :], -1.0, ALU.add),
          reads=[yf], writes=[yf])

    pTbc = pT[:, :].bitcast(BF16)

    def p1_Atr(u):
        yf, xqT = Tyf[u % 2], TxqT[u % 3]
        for bb in range(8):
            sl = bb % 4
            if bb < 4:
                E("tensor", lambda e, bb=bb, sl=sl, yf=yf: e.transpose(
                    pTB[:, sl * 128:(sl + 1) * 128], yf[:, bb * 128:(bb + 1) * 128], idb[:, :]),
                    reads=[yf, idb], writes=[pTB])
            else:
                E("tensor", lambda e, bb=bb, sl=sl, yf=yf: e.transpose(
                    pTbc[:, sl * 128:(sl + 1) * 128], yf[:, bb * 128:(bb + 1) * 128], idb[:, :]),
                    reads=[yf, idb], writes=[pT])
        E("vector", lambda e, xqT=xqT: e.tensor_copy(xqT[:, 0:512], pTB[:, 0:512]),
          reads=[pTB], writes=[xqT])
        E("vector", lambda e, xqT=xqT: e.tensor_copy(xqT[:, 512:1024], pTbc[:, 0:512]),
          reads=[pT], writes=[xqT])

    def p1_Bmm(u):
        xqT, q = TxqT[u % 3], Tq[u % 4]
        for jg in range(2):
            hdst = Threl[u % 2] if jg == 0 else ThrelB[u % 2]
            for dt in range(DT):
                for hh in range(4):
                    ph = pA if hh < 2 else pB
                    ho = (hh % 2) * 512
                    E("tensor", lambda e, dt=dt, jg=jg, hh=hh, ph=ph, ho=ho, xqT=xqT: e.matmul(
                        ph[:, ho:ho + 512],
                        xqT[:, dt * 128:(dt + 1) * 128],
                        w1T[:, dt * HID + jg * 2048 + hh * 512:dt * HID + jg * 2048 + (hh + 1) * 512],
                        start=(dt == 0), stop=(dt == DT - 1)),
                        reads=[xqT, w1T], writes=[ph])
            for piece, ph in [(0, pA), (1, pB)]:
                off = piece * 1024
                E("scalar", lambda e, ph=ph, off=off, hdst=hdst, q=q: e.activation(
                    hdst[:, off:off + 1024], ph[:, :], AF.Relu, scale=q[:, 7:8]),
                  reads=[ph, q], writes=[hdst])
            E("scalar", lambda e, jg=jg, hdst=hdst, q=q: e.activation(hdst[:, :], hdst[:, :], AF.Square,
                                                                      accum_out=q[:, 8 + jg:9 + jg]),
              reads=[hdst], writes=[hdst, q])
            E("scalar", lambda e, jg=jg, hdst=hdst, q=q: e.activation(dumS[:, 0:2048], hdst[:, :], AF.Square,
                                                                      accum_out=q[:, 10 + jg:11 + jg]),
              reads=[hdst], writes=[dumS, q])

    def p1_Bv(u):
        hrel, hrelB, hyf, hu1, q = Threl[u % 2], ThrelB[u % 2], Thyf[u % 3], Thu1[u % 3], Tq[u % 4]
        E("vector", lambda e, hrel=hrel, hyf=hyf: e.scalar_tensor_tensor(
            hyf[:, :], hrel[:, :], ffc[:, 3:4], asn2[:, 0:HH], ALU.add, ALU.mult),
          reads=[hrel, ffc, asn2], writes=[hyf])
        E("vector", lambda e, hrelB=hrelB, hu1=hu1: e.scalar_tensor_tensor(
            hu1[:, :], hrelB[:, :], ffc[:, 3:4], asn2[:, HH:HID], ALU.add, ALU.mult),
          reads=[hrelB, ffc, asn2], writes=[hu1])
        E("vector", lambda e, q=q: e.tensor_tensor(q[:, 16:17], q[:, 8:9], q[:, 9:10], ALU.add),
          reads=[q], writes=[q])
        E("vector", lambda e, q=q: e.tensor_tensor(q[:, 17:18], q[:, 10:11], q[:, 11:12], ALU.add),
          reads=[q], writes=[q])
        E("vector", lambda e, q=q: e.tensor_scalar(q[:, 20:21], q[:, 17:18], ffc[:, 0:1], None, ALU.mult),
          reads=[q, ffc], writes=[q])
        E("vector", lambda e, q=q: e.scalar_tensor_tensor(q[:, 20:21], q[:, 16:17], ffc[:, 1:2], q[:, 20:21], ALU.mult, ALU.add),
          reads=[q, ffc], writes=[q])
        E("vector", lambda e, q=q: e.tensor_single_scalar(q[:, 20:21], q[:, 20:21], ffc[:, 2:3], ALU.add),
          reads=[q, ffc], writes=[q])
        E("scalar", lambda e, q=q: e.activation(q[:, 21:22], q[:, 20:21], AF.Ln, bias=kn[:, 0:1]),
          reads=[q, kn], writes=[q])
        E("scalar", lambda e, q=q: e.activation(q[:, 21:22], q[:, 21:22], AF.Exp, scale=-0.5),
          reads=[q], writes=[q])

    def p1_Bq(u):
        hyf, hu1, q = Thyf[u % 3], Thu1[u % 3], Tq[u % 4]
        E("vector", lambda e, hyf=hyf, q=q: e.tensor_reduce(q[:, 14:15], hyf[:, :], AX, ALU.max, apply_absolute_value=True),
          reads=[hyf], writes=[q])
        E("vector", lambda e, hu1=hu1, q=q: e.tensor_reduce(q[:, 15:16], hu1[:, :], AX, ALU.max, apply_absolute_value=True),
          reads=[hu1], writes=[q])
        E("vector", lambda e, q=q: e.tensor_tensor(q[:, 14:15], q[:, 14:15], q[:, 15:16], ALU.max),
          reads=[q], writes=[q])
        E("vector", lambda e, q=q: e.tensor_scalar(q[:, 22:23], q[:, 14:15], q[:, 21:22], kq[:, 0:1], ALU.mult, ALU.max),
          reads=[q, kq], writes=[q])
        E("vector", lambda e, q=q: e.reciprocal(q[:, 23:24], q[:, 22:23]), reads=[q], writes=[q])
        E("vector", lambda e, q=q: e.tensor_scalar(q[:, 25:26], q[:, 23:24], 127.0, q[:, 21:22], ALU.mult, ALU.mult),
          reads=[q], writes=[q])
        E("vector", lambda e, q=q: e.tensor_scalar(q[:, 26:27], q[:, 22:23], 1.0 / 127.0, wsc[:, 1:2], ALU.mult, ALU.mult),
          reads=[q, wsc], writes=[q])
        E("vector", lambda e, q=q, u=u: e.tensor_copy(hscst[:, u:u + 1], q[:, 26:27]),
          reads=[q], writes=[hscst])

    def p1_Cra(u):
        q = Tq[u % 4]
        for jg in range(2):
            rnd = Thyf[u % 3] if jg == 0 else Thu1[u % 3]
            hyq = Thyq[(u + jg) % 2]
            E("vector", lambda e, rnd=rnd, hyq=hyq, q=q: e.tensor_single_scalar(
                hyq[:, :], rnd[:, :], q[:, 25:26], ALU.mult),
              reads=[rnd, q], writes=[hyq])
        for jg in range(2):
            rnd = Thyf[u % 3] if jg == 0 else Thu1[u % 3]
            hyq = Thyq[(u + jg) % 2]
            yi = Tyi[(u + 2 * jg) % 4]
            for qq in range(2):
                if jg == 0:
                    E("scalar", lambda e, qq=qq, hyq=hyq, yi=yi: e.copy(yi[:, :], hyq[:, qq * 1024:(qq + 1) * 1024]),
                      reads=[hyq], writes=[yi])
                    E("scalar", lambda e, qq=qq, rnd=rnd, yi=yi: e.copy(rnd[:, qq * 1024:(qq + 1) * 1024], yi[:, :]),
                      reads=[yi], writes=[rnd])
                else:
                    E("vector", lambda e, qq=qq, hyq=hyq, yi=yi: e.tensor_copy(yi[:, :], hyq[:, qq * 1024:(qq + 1) * 1024]),
                      reads=[hyq], writes=[yi])
                    E("vector", lambda e, qq=qq, rnd=rnd, yi=yi: e.tensor_copy(rnd[:, qq * 1024:(qq + 1) * 1024], yi[:, :]),
                      reads=[yi], writes=[rnd])

    def p1_Crb(u):
        for jg in range(2):
            rnd = Thyf[u % 3] if jg == 0 else Thu1[u % 3]
            hyq = Thyq[(u + jg) % 2]
            dhf = Tdif[(u + jg) % 2]
            E("vector", lambda e, hyq=hyq, rnd=rnd, dhf=dhf: e.tensor_tensor(dhf[:, :], hyq[:, :], rnd[:, :], ALU.subtract),
              reads=[hyq, rnd], writes=[dhf])
            E("vector", lambda e, dhf=dhf, rnd=rnd: e.scalar_tensor_tensor(
                rnd[:, :], dhf[:, :], c05[:, 0:1], rnd[:, :], ALU.is_gt, ALU.add),
              reads=[dhf, c05, rnd], writes=[rnd])
            E("vector", lambda e, dhf=dhf, rnd=rnd: e.scalar_tensor_tensor(
                rnd[:, :], dhf[:, :], cm05[:, 0:1], rnd[:, :], ALU.is_ge, ALU.add),
              reads=[dhf, cm05, rnd], writes=[rnd])
            E("vector", lambda e, rnd=rnd: e.tensor_single_scalar(rnd[:, :], rnd[:, :], -1.0, ALU.add),
              reads=[rnd], writes=[rnd])

    KpC = [("pClo",), ("pChi",), ("pThi",)]

    def p1_Ct(u, jg):
        s, tc = u // NTC, u % NTC
        hqS = ThqS[u % 2]
        slotsets = [(pTB2, 0, pTB2), (pCb, 0, KpC[0]), (pCb, 512, KpC[1]), (pTbc, 512, KpC[2])]
        hof = jg * HH
        rnd = Thyf[u % 3] if jg == 0 else Thu1[u % 3]
        for bb in range(HH // 128):
            gg = u * 8 + jg * 4 + bb // 4
            base, off, pkey = slotsets[gg % 4]
            sl = bb % 4
            E("tensor", lambda e, bb=bb, sl=sl, rnd=rnd, base=base, off=off: e.transpose(
                base[:, off + sl * 128:off + (sl + 1) * 128], rnd[:, bb * 128:(bb + 1) * 128], idb[:, :]),
                reads=[rnd, idb], writes=[pkey])
            if bb % 4 == 3:
                hq_s = hqS[gg % 2]
                E("scalar", lambda e, hq_s=hq_s, base=base, off=off: e.copy(hq_s[:, :], base[:, off:off + 512]),
                  reads=[pkey], writes=[hq_s])
                DMA("scalar", lambda e, hq_s=hq_s, s=s, tc=tc, hof=hof, bb=bb: e.dma_start(
                    dhq[s, tc, :, hof + (bb - 3) * 128:hof + (bb + 1) * 128], hq_s[:, :]),
                    reads=[hq_s], writes=[dhq])

    p1_Apre(0)
    for c in range(NUN + 4):
        if 0 <= c - 1 < NUN:
            p1_Atr(c - 1)
        if c < NUN:
            p1_A0(c)
        if 0 <= c - 3 < NUN:
            p1_Bq(c - 3)
            p1_Cra(c - 3)
        if 0 <= c - 4 < NUN:
            p1_Ct(c - 4, 0)
        if 0 <= c - 1 < NUN:
            p1_Bmm(c - 1)
        if c < NUN:
            p1_Av(c)
        if 0 <= c - 3 < NUN:
            p1_Crb(c - 3)
        if 0 <= c - 2 < NUN:
            p1_Bv(c - 2)
        if 0 <= c - 4 < NUN:
            p1_Ct(c - 4, 1)
        if c + 1 < NUN:
            p1_Apre(c + 1)

    p.barrier_all()
    ar.reset(s2b2_mark)

    p.scope = "s2c"
    s2c_mark = ar.mark()
    w2T = ar.alloc("w2T", [128, JT * D], BF16)
    P2 = []
    for i in range(3):
        P2.append({
            "hqT": ar.alloc(f"p2hq_{i}", [128, HID], BF16),
            "x1": ar.alloc(f"p2x1_{i}", [128, D], BF16),
            "xf": ar.alloc(f"p2xf_{i}", [128, D], F32),
            "yo": ar.alloc(f"p2yo_{i}", [128, D], F32),
            "sq": ar.alloc(f"p2sq_{i}", [128, D], F32),
            "q": ar.alloc(f"p2q_{i}", [128, 8], F32),
        })
    S2C_END = ar.mark()

    w2ldb = [ar.alloc(f"w2ld{i}", [128, HID], F32) for i in range(2)]
    w2u1b = [ar.alloc(f"w2u1{i}", [128, HID], BF16) for i in range(2)]
    w2tnb = [ar.alloc(f"w2tn{i}", [128, HID], BF16) for i in range(2)]
    for s in range(8):
        wld, wu1, wtn = w2ldb[s % 2], w2u1b[s % 2], w2tnb[s % 2]
        DMA("sync", lambda e, s=s, wld=wld: e.dma_start(wld[:, :], dw2[s * 128:(s + 1) * 128, :]),
            writes=[wld])
        E("scalar", lambda e, wld=wld: e.activation(wld[:, :], wld[:, :], AF.Copy, scale=wsc[:, 3:4]),
          reads=[wld, wsc], writes=[wld])
        E("vector", lambda e, wld=wld, wtn=wtn: e.tensor_single_scalar(wtn[:, :], wld[:, :], 0.5, ALU.is_gt),
          reads=[wld], writes=[wtn])
        E("vector", lambda e, wld=wld, wu1=wu1: e.tensor_single_scalar(wu1[:, :], wld[:, :], -0.5, ALU.is_lt),
          reads=[wld], writes=[wu1])
        E("vector", lambda e, wu1=wu1, wtn=wtn: e.tensor_tensor(wtn[:, :], wtn[:, :], wu1[:, :], ALU.subtract),
          reads=[wu1, wtn], writes=[wtn])
        for bb in range(JT):
            sl = bb % 4
            ptb = pTB if (bb // 4) % 2 == 0 else pTB2
            E("tensor", lambda e, bb=bb, sl=sl, wtn=wtn, ptb=ptb: e.transpose(
                ptb[:, sl * 128:(sl + 1) * 128], wtn[:, bb * 128:(bb + 1) * 128], idb[:, :]),
                reads=[wtn, idb], writes=[ptb])
            if bb % 4 == 3:
                eng = "vector" if (bb // 4) % 2 == 0 else "scalar"
                for k in range(4):
                    dstk = w2T[:, (bb - 3 + k) * D + s * 128:(bb - 3 + k) * D + (s + 1) * 128]
                    if eng == "vector":
                        E(eng, lambda e, dstk=dstk, k=k, ptb=ptb: e.tensor_copy(dstk, ptb[:, k * 128:(k + 1) * 128]),
                          reads=[ptb], writes=[w2T])
                    else:
                        E(eng, lambda e, dstk=dstk, k=k, ptb=ptb: e.copy(dstk, ptb[:, k * 128:(k + 1) * 128]),
                          reads=[ptb], writes=[w2T])
    for s in range(BS):
        for tc in range(NTC):
            t0 = tc * 128
            S = P2[(s * NTC + tc) % 3]
            hqT, xb, xf, yo, sqt, q = S["hqT"], S["x1"], S["xf"], S["yo"], S["sq"], S["q"]
            cidx = s * NTC + tc
            DMA("sync", lambda e, hqT=hqT, s=s, tc=tc: e.dma_start(hqT[:, :], dhq[s, tc, :, :]),
                reads=[dhq], writes=[hqT])
            DMA("sync", lambda e, xb=xb, s=s, t0=t0: e.dma_start(xb[:, :], dx1[s, t0:t0 + 128, :]),
                reads=[dx1], writes=[xb])
            pcy = pA if (s * NTC + tc) % 2 == 0 else pB
            for jt in range(JT):
                for hh in range(2):
                    E("tensor", lambda e, jt=jt, hh=hh, hqT=hqT, pcy=pcy: e.matmul(
                        pcy[:, hh * 512:(hh + 1) * 512],
                        hqT[:, jt * 128:(jt + 1) * 128],
                        w2T[:, jt * D + hh * 512:jt * D + (hh + 1) * 512],
                        start=(jt == 0), stop=(jt == JT - 1)),
                        reads=[hqT, w2T], writes=[pcy])
            E("scalar", lambda e, xb=xb, xf=xf: e.copy(xf[:, :], xb[:, :]), reads=[xb], writes=[xf])
            E("vector", lambda e, yo=yo, cidx=cidx, pcy=pcy: e.tensor_single_scalar(yo[:, :], pcy[:, :], hscst[:, cidx:cidx + 1], ALU.mult),
              reads=[pcy, hscst], writes=[yo])
            E("vector", lambda e, yo=yo, xf=xf: e.tensor_tensor(yo[:, :], yo[:, :], xf[:, :], ALU.add),
              reads=[yo, xf], writes=[yo])
            E("vector", lambda e, yo=yo, q=q: e.tensor_reduce(q[:, 0:1], yo[:, :], AX, ALU.add),
              reads=[yo], writes=[q])
            E("scalar", lambda e, yo=yo, sqt=sqt, q=q: e.activation(sqt[:, :], yo[:, :], AF.Square, accum_out=q[:, 1:2]),
              reads=[yo], writes=[sqt, q])
            E("vector", lambda e, q=q: e.tensor_single_scalar(q[:, 2:3], q[:, 0:1], 1.0 / D, ALU.mult),
              reads=[q], writes=[q])
            E("vector", lambda e, q=q: e.tensor_tensor(q[:, 3:4], q[:, 2:3], q[:, 2:3], ALU.mult),
              reads=[q], writes=[q])
            E("vector", lambda e, q=q: e.tensor_single_scalar(q[:, 4:5], q[:, 1:2], 1.0 / D, ALU.mult),
              reads=[q], writes=[q])
            E("vector", lambda e, q=q: e.tensor_tensor(q[:, 4:5], q[:, 4:5], q[:, 3:4], ALU.subtract),
              reads=[q], writes=[q])
            E("scalar", lambda e, q=q: e.activation(q[:, 5:6], q[:, 4:5], AF.Ln, bias=kq[:, 0:1]),
              reads=[q, kq], writes=[q])
            E("scalar", lambda e, q=q: e.activation(q[:, 5:6], q[:, 5:6], AF.Exp, scale=-0.5),
              reads=[q], writes=[q])
            E("vector", lambda e, q=q: e.tensor_single_scalar(q[:, 6:7], q[:, 2:3], -1.0, ALU.mult),
              reads=[q], writes=[q])
            E("vector", lambda e, yo=yo, q=q: e.tensor_scalar(yo[:, :], yo[:, :], q[:, 6:7], q[:, 5:6], ALU.add, ALU.mult),
              reads=[yo, q], writes=[yo])
            DMA("scalar", lambda e, s=s, t0=t0, yo=yo: e.dma_start(dout[s, t0:t0 + 128, :], yo[:, :]),
                reads=[yo])

    p.final_wait("sync")
    p.build(cx)
    cx.close()
    return nc


def kernel(**inputs):
    x = np.ascontiguousarray(inputs["L0_attn_x"])

    def colmaj(v):
        return np.ascontiguousarray(np.asarray(v).reshape(8, 128).T.astype(np.float32))

    cols = np.concatenate([
        colmaj(inputs["L0_attn_time_decay"].reshape(D)),
        colmaj(inputs["L0_attn_time_first"].reshape(D)),
        colmaj(inputs["L0_attn_time_mix_r"].reshape(D)),
        colmaj(inputs["L0_attn_time_mix_k"].reshape(D)),
        colmaj(inputs["L0_attn_time_mix_v"].reshape(D)),
        colmaj(inputs["L0_attn_ln_gamma"].reshape(D)),
        colmaj(inputs["L0_attn_ln_beta"].reshape(D)),
        np.zeros((128, 8), np.float32)], axis=1)
    hmask = np.zeros((128, 2), np.float32)
    hmask[:64, 0] = 1.0
    hmask[64:, 1] = 1.0
    common = {
        "W_r": np.ascontiguousarray(inputs["L0_attn_W_r"], np.float32),
        "W_k": np.ascontiguousarray(inputs["L0_attn_W_k"], np.float32),
        "W_v": np.ascontiguousarray(inputs["L0_attn_W_v"], np.float32),
        "W_o": np.ascontiguousarray(inputs["L0_attn_W_o"], np.float32),
        "w1": np.ascontiguousarray(inputs["L0_ffn_w1"], np.float32),
        "w2": np.ascontiguousarray(inputs["L0_ffn_w2"], np.float32),
        "cols": cols,
        "n1row": np.ascontiguousarray(inputs["L0_ffn_norm1_w"].reshape(1, D), np.float32),
        "n2row": np.ascontiguousarray(inputs["L0_ffn_norm2_w"].reshape(1, HID), np.float32),
        "asb": np.ascontiguousarray(
            np.stack([np.asarray(inputs["L0_ffn_act_scale"]).reshape(()),
                      np.asarray(inputs["L0_ffn_act_bias"]).reshape(())]).reshape(2, 1), np.float32),
        "idf": np.eye(128, dtype=np.float32),
        "idb": np.eye(128, dtype=bfdt),
        "hmask": hmask,
        "hexp": np.ascontiguousarray(hmask.T),
        "ones1": np.ones((1, 128), np.float32),
        "onesc": np.ones((128, 1), np.float32),
    }
    xs = x.reshape(NCORE, BS, SEQ, D)
    in_maps = [dict(common, x=np.ascontiguousarray(xs[c])) for c in range(NCORE)]
    nc = build_nc()
    res = run_bass_kernel_spmd(nc, in_maps, core_ids=list(range(NCORE)))
    out = np.concatenate([np.asarray(r["out"]).reshape(1, BS, SEQ, D) for r in res.results], axis=0)
    return np.ascontiguousarray(out.reshape(16, SEQ, D)).astype(np.float32)
